# version 07
# baseline (speedup 1.0000x reference)
import math
import numpy as np
import ml_dtypes
from contextlib import ExitStack

import concourse.bass as bass
import concourse.mybir as mybir
from concourse.bass_utils import run_bass_kernel_spmd

F32 = mybir.dt.float32
BF16 = mybir.dt.bfloat16
I32 = mybir.dt.int32
AF = mybir.ActivationFunctionType
ALU = mybir.AluOpType
AX = mybir.AxisListType

MAGIC = 12582912.0
LNEPS = 1e-5
BIG = 1.0e9


def make_cfg(small=False):
    if small == "med":
        D, DL, I, NT, B = 1024, 512, 2048, 1024, 1
    elif small:
        D, DL, I, NT, B = 256, 128, 512, 512, 1
    else:
        D, DL, I, NT, B = 2048, 1024, 8192, 2048, 4
    c = dict(D=D, DL=DL, I=I, NT=NT, T=2 * NT, B=B,
             NH=DL // 64, HD=64, NH1=D // 128, TOPK=64, TS=512)
    c["KD"] = D // 128
    c["KI"] = I // 128
    c["KL"] = DL // 128
    c["NTT"] = NT // 128
    c["NS"] = NT // c["TS"]
    c["TQ"] = c["T"] // c["TS"]
    c["NKT"] = c["T"] // 128
    c["NHm"] = c["NH"] // 2
    c["NH1m"] = c["NH1"] // 2
    c["NCORE"] = 2 * B
    return c


def _is_sw_inc(instr):
    try:
        ins = instr.ins
    except Exception:
        return False
    cn = type(ins).__name__
    return ins.engine == mybir.EngineType.Pool and (
        "DMA" in cn or "Collective" in cn or "TensorCopy" in cn)


class Sems:

    SW_NAMES = {'L0pw0', 'L0pw1', 'L1pw0', 'L1pw1', 'SWN', 'f1w', 'f1wb', 'fcm', 'm1w0', 'm1w1', 'm2qw0', 'm2qw1', 'm2wo', 'w2l'}

    def __init__(self, nc, ctx):
        self.nc, self.ctx, self.m = nc, ctx, {}
        self.free, self.dead = [], set()
        self.serial = 0

    def h(self, name):
        if name in self.dead:
            raise KeyError(f"semaphore name {name!r} was retired")
        if name not in self.m:
            if name in self.SW_NAMES or not self.free:
                self.serial += 1
                ph = [self.ctx.enter_context(
                    self.nc.semaphore(f"{name}_{self.serial}")), 0, 0]
            else:
                ph = self.free.pop()
            self.m[name] = [ph, 0]
        return self.m[name]

    def inc(self, name, instr, n=1):
        e = self.h(name)
        ph = e[0]
        if _is_sw_inc(instr):
            if ph[2] == 1 or (ph[2] == 0 and ph[1] - e[1] != 0):
                raise RuntimeError(f"sem {name!r} needs a fresh SWDGE phys "
                                   f"sem: add to SW_NAMES")
            ph[2] = 2
        else:
            if ph[2] == 2:
                raise RuntimeError(f"sem {name!r} mixes SWDGE and other "
                                   f"updaters")
            ph[2] = 1
        instr.then_inc(ph[0], n)
        ph[1] += n
        e[1] += n
        return e[1]

    def tok(self, name):
        return self.h(name)[1]

    def wait(self, eng, name, val=None):
        e = self.h(name)
        lv = e[1] if val is None else val
        eng.wait_ge(e[0][0], e[0][1] - e[1] + lv)

    def bump(self, eng, name, n=1):
        e = self.h(name)
        if e[0][2] == 2:
            raise RuntimeError(f"sem {name!r} mixes SWDGE and engine bumps")
        e[0][2] = 1
        eng.sem_inc(e[0][0], n)
        e[0][1] += n
        e[1] += n
        return e[1]

    def revive(self, *names):
        for nm in names:
            self.dead.discard(nm)

    def retire(self, *names):
        for nm in names:
            if nm in self.m:
                ph = self.m.pop(nm)[0]
                if ph[2] != 2:
                    self.free.append(ph)
                self.dead.add(nm)

    def save(self):
        return {n: (e[0], e[1], e[0][1]) for n, e in self.m.items()}

    def delta(self, snap):
        d = {}
        for n, e in self.m.items():
            base = snap[n][1] if n in snap else 0
            if e[1] != base:
                d[n] = e[1] - base
        return d

    def restore(self, snap):
        for n, e in list(self.m.items()):
            if n in snap:
                _, loc, phv = snap[n]
                e[0][1] = phv
                e[1] = loc
            else:
                e[0][1] -= e[1]
                e[1] = 0

    def phys_state(self):
        seen, out = set(), []
        for e in self.m.values():
            if id(e[0]) not in seen:
                seen.add(id(e[0]))
                out.append(e[0])
        for ph in self.free:
            if id(ph) not in seen:
                seen.add(id(ph))
                out.append(ph)
        return [(ph[0], ph[1]) for ph in out]


def _chain(S, eng, name="vch"):
    def step(instr):
        t = S.inc(name, instr, 1)
        S.wait(eng, name, t)
        return t
    return step


class Bld:
    def mark_gate(self):
        cc = self.S.m.get("cc")
        gd = [(h, v) for (h, v) in self.S.phys_state()
              if cc is None or h is not cc[0][0]]
        self.pending_gates.append(gd)

    def gate(self, eng):
        for gd in self.pending_gates[-1:]:
            for h, v in gd:
                eng.wait_ge(h, v)

    def sswitch(self, eng, ename):
        snap = d0 = None
        for side in self.nc.Switch(engines=[eng], index=[self.sreg[ename]], n=2):
            if side == 0:
                snap = self.S.save()
            else:
                d0 = self.S.delta(snap)
                self.S.restore(snap)
            yield side
        d1 = self.S.delta(snap)
        assert d0 == d1, ("switch bodies differ in sem deltas", d0, d1)


def build_core(cfg, debug=0, upto="all"):
    nc = bass.Bass("TRN2", target_bir_lowering=False, num_devices=cfg["NCORE"])
    b = Bld()
    b.cfg, b.debug, b.upto = cfg, debug, upto
    c = cfg
    D, DL, I, NT, T, TS = c["D"], c["DL"], c["I"], c["NT"], c["T"], c["TS"]
    NHm, NH1m, HD = c["NHm"], c["NH1m"], c["HD"]
    MYH0, MYH1 = NHm * HD, NH1m * 128
    TRIV = TS // 128

    inp = lambda n, sh, dt=F32: nc.dram_tensor(n, sh, dt, kind="ExternalInput").ap()
    DBG = {"xn32_sp", "h_sp", "x1_sp", "y1_sp", "q2_dr", "ctx1m_dr", "qw2T_dr",
           "nrm_dr", "idx_dr", "inv_dr", "dbg_dr"} if debug else set()
    dram = lambda n, sh, dt: nc.dram_tensor(
        n, sh, dt, kind="ExternalOutput" if n in DBG else "Internal").ap()

    b.x0 = inp("x0", [NT, D])
    for n_, sh in [("wk", [D, D]), ("wq", [D, D]), ("wv", [D, D]),
                   ("wkl", [D, DL]), ("wql", [D, DL]), ("wvl", [D, DL]),
                   ("wo", [DL, D]), ("w1", [I, D]), ("w2", [D, I]),
                   ("Wq1", [D, D]), ("Wk1", [D, D]), ("Wv1", [D, D]),
                   ("w2T", [I, D]), ("WbT", [D, D]), ("WrT", [D, 8]),
                   ("AT", [D, 64]), ("Bcat", [64, D])]:
        setattr(b, n_, inp(n_, sh))
    for n_ in ["bk", "bq", "bv", "bo", "b2", "bq1", "bk1", "bv1", "bbase"]:
        setattr(b, n_, inp(n_, [D]))
    for n_ in ["bkl", "bql", "bvl"]:
        setattr(b, n_, inp(n_, [DL]))
    b.b1 = inp("b1", [I])
    b.soff = inp("soff", [1, 8], I32)
    b.IOTA = inp("IOTA", [64, 512])
    b.ID128 = inp("ID128", [128, 128])
    b.TRI = inp("TRI", [TRIV * 128, TS], BF16)
    b.R8 = inp("R8", [8, 64], BF16)

    b.out = nc.dram_tensor("out", [NT, D], F32, kind="ExternalOutput").ap()
    b.dbg = {}

    b.xn32_sp = dram("xn32_sp", [NT, D], F32)
    b.h_sp = dram("h_sp", [I, NT], F32)
    b.qw2T_dr = dram("qw2T_dr", [I, D], BF16)
    b.q2_dr = dram("q2_dr", [I, NT], BF16)
    b.ctx1m_dr = dram("ctx1m_dr", [D, NT], BF16)
    b.x1_sp = dram("x1_sp", [NT, D], F32)
    b.y1_sp = dram("y1_sp", [NT, D], F32)
    b.row_dr = dram("row_dr", [8192], F32)
    b.dbg_dr = dram("dbg_dr", [4 * 128, NT], F32)
    b.nrm_dr = dram("nrm_dr", [max(NH1m, 1), T], F32)
    b.inv_dr = dram("inv_dr", [max(NH1m, 1), T], F32)
    b.idx_dr = dram("idx_dr", [max(NH1m, 1), 64], F32)
    b.ag0_in = dram("ag0_in", [3 * DL, NT], BF16)
    b.ag0_out = dram("ag0_out", [2 * 3 * DL, NT], BF16)
    b.ag0c_in = dram("ag0c_in", [MYH0, T], BF16)
    b.ag0c_out = dram("ag0c_out", [2 * MYH0, T], BF16)
    b.ag1_in = dram("ag1_in", [3 * D, NT], BF16)
    b.ag1_out = dram("ag1_out", [2 * 3 * D, NT], BF16)
    b.ag1c_in = dram("ag1c_in", [MYH1, T], BF16)
    b.ag1c_out = dram("ag1c_out", [2 * MYH1, T], BF16)
    b.ar_in = dram("ar_in", [1, 128], F32)
    b.ar_out = dram("ar_out", [1, 128], F32)
    b.PAIRS = [[2 * i, 2 * i + 1] for i in range(c["NCORE"] // 2)]
    b.ALLG = [list(range(c["NCORE"]))]

    ctx = ExitStack()
    b.ctx, b.nc = ctx, nc
    S = Sems(nc, ctx)
    b.S = S
    b.sb = lambda n, sh, dt: ctx.enter_context(nc.sbuf_tensor(n, sh, dt))
    b.ps = lambda n, sh, dt=F32: ctx.enter_context(nc.psum_tensor(n, sh, dt))
    PS = nc.SBUF_PARTITION_SIZE_BYTES - 64
    RSZ = {"RA": 67584, "RB": 65536, "RC": 57344, "RW": 12288}
    MB = PS - sum(RSZ.values())
    base = MB
    b.reg = {}
    for nm in ["RW", "RC", "RB", "RA"]:
        sz = RSZ[nm]
        b.reg[nm] = {
            "f": nc.alloc_sbuf_tensor_at(f"{nm}_f", [128, sz // 4], F32, offset=base),
            "b": nc.alloc_sbuf_tensor_at(f"{nm}_b", [128, sz // 2], BF16, offset=base),
            "i": nc.alloc_sbuf_tensor_at(f"{nm}_i", [128, sz // 4], I32, offset=base),
            "size": sz,
        }
        base += sz
    b.auto_limit = MB

    def RV(region, dt, off, p, cols, p0=0):
        r = b.reg[region]
        esz = 2 if dt == BF16 else 4
        t = r["b"] if dt == BF16 else (r["i"] if dt == I32 else r["f"])
        assert off % esz == 0 and off + cols * esz <= r["size"], (region, off, cols)
        return t[p0:p0 + p, off // esz: off // esz + cols]
    b.RV = RV

    def CUR(region, start=0):
        pos = [start]
        def take(dt, p, cols, p0=0, align=64):
            esz = 2 if dt == BF16 else 4
            off = (pos[0] + align - 1) // align * align
            pos[0] = off + cols * esz
            take.last_off = off
            return RV(region, dt, off, p, cols, p0)
        take.pos = pos
        return take
    b.CUR = CUR
    b.sreg = {}
    b.pending_gates = []

    KEEP = {'L0pe', 'L1pe', 'acc2', 'aee', 'am', 'anv', 'ax', 'cc', 'cst', 'f1ys0', 'f1ys1', 'ln1v', 'm1e', 'm1hx0', 'm1hx1', 'm1hx2', 'm1hx3', 'm2eB', 'm2ti', 'p0sp0', 'p0sp1', 's', 'sl_cb', 'sl_iw', 'sl_v2', 't', 'v', 'vch', 'w2bc', 'w2st0', 'w2st1'}
    phase_names = []

    def ph(fn, *a, **k):
        before = set(S.m) | S.dead
        fn(*a, **k)
        phase_names.append((set(S.m) - before) - KEEP)
        if len(phase_names) >= 2:
            S.retire(*phase_names[-2])
            phase_names[-2] = set()

    with ctx:
        _persistent(b)
        b.block = ctx.enter_context(nc.Block())
        ph(_preload, b)
        if upto == "pre":
            _finish(b)
            return nc
        ph(_p0_ln, b)
        if upto == "p0":
            _finish(b)
            return nc
        ph(_ar1, b)
        if upto == "ar":
            _finish(b)
            return nc
        ph(_proj, b, layer=0)
        if upto == "proj":
            _finish(b)
            return nc
        ph(_ag_qkv, b, layer=0)
        b.mark_gate()
        ph(_quant_x, b)
        if upto == "p1":
            _finish(b)
            return nc
        b.mark_gate()
        ph(_m1, b)
        if upto in ("m1", "m1a", "m1b", "m1c", "m1d", "m1e", "m1f"):
            _finish(b)
            return nc
        b.mark_gate()
        ph(_w2_quant, b)
        if upto == "w2":
            _finish(b)
            return nc
        ph(_attn0, b)
        ph(_agc, b, layer=0)
        if upto == "at0":
            _finish(b)
            return nc
        b.mark_gate()
        ph(_m2_x1, b)
        if upto in ("q2", "q2A", "q2B", "q2C", "q2ng", "q2v2", "l0"):
            _finish(b)
            return nc
        b.mark_gate()
        ph(_proj, b, layer=1)
        ph(_ag_qkv, b, layer=1)
        ph(_l1_select, b)
        b.mark_gate()
        ph(_ffn1, b)
        b.mark_gate()
        ph(_attn1, b)
        ph(_agc, b, layer=1)
        b.mark_gate()
        ph(_final, b)
        _finish(b)
        assert nc.sbuf_base <= MB, (nc.sbuf_base, MB)
    return nc


def _persistent(b):
    c, sb = b.cfg, b.sb
    D, T, TS = c["D"], c["T"], c["TS"]
    KD, KI, KL, NHm, NH1m = c["KD"], c["KI"], c["KL"], c["NHm"], c["NH1m"]
    TRIV = TS // 128
    b.ones_f = sb("ones_f", [128, 1], F32)
    b.onesr_f = sb("onesr_f", [1, 128], F32)
    b.ones_b = sb("ones_b", [128, 1], BF16)
    b.biasK = sb("biasK", [128, 3 * KD], F32)
    b.biasL = sb("biasL", [128, 3 * KL], F32)
    b.bias1 = sb("bias1", [128, KI], F32)
    b.biasQ1 = sb("biasQ1", [128, 3 * KD], F32)
    b.biasC1 = sb("biasC1", [128, max(NH1m, 1)], F32)
    b.s1c = sb("s1c", [128, KI], F32)
    b.absacc = sb("absacc", [128, 1], F32)
    b.habs = sb("habs", [128, 1], F32)
    b.sm = sb("sm", [128, 24], F32)
    b.rsx_b = sb("rsx_b", [128, 1], F32)
    b.rs2_b = sb("rs2_b", [128, 1], F32)
    b.idT = sb("idT", [128, 128], F32)
    b.tri = sb("tri", [128, TRIV * TS], BF16)
    b.r8 = sb("r8", [8, 64], BF16)
    b.iota = sb("iota", [64, 512], F32)
    b.soffs = sb("soffs", [1, 8], I32)
    b.psP = [b.ps(f"psP{i}", [128, 512]) for i in range(8)]


def _preload(b):
    S, c = b.S, b.cfg
    KD, KL, KI, TS = c["KD"], c["KL"], c["KI"], c["TS"]
    TRIV = TS // 128

    @b.block.sync
    def _(y):
        for bt, K, srcs in [(b.biasK, KD, (b.bq, b.bk, b.bv)),
                            (b.biasL, KL, (b.bql, b.bkl, b.bvl)),
                            (b.biasQ1, KD, (b.bq1, b.bk1, b.bv1))]:
            for j, src in enumerate(srcs):
                r = src.rearrange("(c p) -> c p", p=128)
                for cc in range(K):
                    ii = y.dma_start(out=bt[:, j * K + cc:j * K + cc + 1], in_=r[cc])
                    S.inc("cst", ii, 16)
        r = b.b1.rearrange("(c p) -> c p", p=128)
        for cc in range(KI):
            ii = y.dma_start(out=b.bias1[:, cc:cc + 1], in_=r[cc])
            S.inc("cst", ii, 16)
        for dst, src in [(b.idT, b.ID128), (b.r8, b.R8), (b.iota, b.IOTA),
                         (b.soffs, b.soff)]:
            ii = y.dma_start(out=dst[:], in_=src)
            S.inc("cst", ii, 16)
        for v in range(TRIV):
            ii = y.dma_start(out=b.tri[:, v * TS:(v + 1) * TS],
                             in_=b.TRI[v * 128:(v + 1) * 128, :])
            S.inc("cst", ii, 16)
    b.cst_tot = S.tok("cst")

    def loadoffs(eng, ename):
        S.wait(eng, "cst", b.cst_tot)
        b.sreg[ename] = eng.value_load(b.soffs[0:1, 0:1])

    @b.block.sync
    def _(y):
        loadoffs(y, "y")

    @b.block.gpsimd
    def _(g):
        loadoffs(g, "g")

    @b.block.vector
    def _(v):
        v.memset(b.ones_f[:], 1.0)
        v.memset(b.onesr_f[:], 1.0)
        v.memset(b.ones_b[:], 1.0)
        v.memset(b.absacc[:], 0.0)
        ii = v.memset(b.habs[:], 0.0)
        b.t_const = S.inc("v", ii, 1)


def _p0_ln(b):
    S, c = b.S, b.cfg
    D, NT, NTT, KD = c["D"], c["NT"], c["NTT"], c["KD"]
    RV = b.RV

    b.xn_fm = [RV("RA", BF16, k * 2 * NT, 128, NT) for k in range(KD)]
    x0t = [RV("RC", F32, j * 4 * D, 128, D) for j in range(2)]
    sqt = [RV("RC", F32, 8 * D + j * 4 * D, 128, D) for j in range(2)]
    xnt = [RV("RC", F32, 16 * D + j * 4 * D, 128, D) for j in range(2)]
    xnb = [RV("RC", BF16, 24 * D + j * 2 * D, 128, D) for j in range(2)]
    stat = b.sb("stat", [128, 8], F32)

    s_base = S.tok("s")
    vx_base = S.tok("vx")
    v0p = S.tok("v")
    vt = lambda tt: v0p + tt + 1

    def p0_spill(y, tt):
        j = tt % 2
        S.wait(y, "v", vt(tt))
        ii = y.dma_start(out=b.xn32_sp[tt * 128:(tt + 1) * 128, :], in_=xnt[j][:])
        S.inc(f"p0sp{j}", ii, 16)
        for k in range(KD):
            ii = y.dma_start_transpose(b.xn_fm[k][:, tt * 128:(tt + 1) * 128],
                                       xnb[j][:, k * 128:(k + 1) * 128])
            S.inc(f"p0sp{j}", ii, 16)

    @b.block.sync
    def _(y):
        for tt in range(NTT):
            j = tt % 2
            if tt >= 2:
                S.wait(y, "v", vt(tt - 2))
            ii = y.dma_start(out=x0t[j][:], in_=b.x0[tt * 128:(tt + 1) * 128, :])
            S.inc(f"p0ld{j}", ii, 16)
            if tt >= 1:
                p0_spill(y, tt - 1)
        p0_spill(y, NTT - 1)

    @b.block.vector
    def _(v):
        S.wait(v, "v", b.t_const)
        for tt in range(NTT):
            j = tt % 2
            st = _chain(S, v)
            S.wait(v, f"p0ld{j}", (tt // 2 + 1) * 16)
            S.wait(v, "s", s_base + 2 * tt + 1)
            if tt >= 2:
                S.wait(v, f"p0sp{j}", (tt // 2) * 16 * (1 + KD))
            v.reduce_sum(out=stat[:, 0:1], in_=x0t[j][:], axis=AX.X)
            st(v.reduce_sum(out=stat[:, 1:2], in_=sqt[j][:], axis=AX.X))
            st(v.tensor_scalar(out=stat[:, 2:3], in0=stat[:, 0:1], scalar1=-1.0 / D,
                               scalar2=None, op0=ALU.mult))
            st(v.tensor_tensor(out=stat[:, 4:5], in0=stat[:, 2:3], in1=stat[:, 2:3],
                               op=ALU.mult))
            st(v.tensor_scalar(out=stat[:, 3:4], in0=stat[:, 1:2], scalar1=1.0 / D,
                               scalar2=None, op0=ALU.mult))
            st(v.tensor_tensor(out=stat[:, 3:4], in0=stat[:, 3:4], in1=stat[:, 4:5],
                               op=ALU.subtract))
            st(v.tensor_scalar(out=stat[:, 3:4], in0=stat[:, 3:4], scalar1=LNEPS,
                               scalar2=None, op0=ALU.add))
            S.bump(v, "vx", 1)
            S.wait(v, "s", s_base + 2 * tt + 2)
            st(v.reciprocal(out=stat[:, 6:7], in_=stat[:, 5:6]))
            st(v.tensor_tensor(out=stat[:, 7:8], in0=stat[:, 6:7], in1=stat[:, 6:7],
                               op=ALU.mult))
            st(v.tensor_tensor(out=stat[:, 7:8], in0=stat[:, 7:8], in1=stat[:, 3:4],
                               op=ALU.mult))
            st(v.tensor_scalar(out=stat[:, 7:8], in0=stat[:, 7:8], scalar1=-0.5,
                               scalar2=1.5, op0=ALU.mult, op1=ALU.add))
            st(v.tensor_tensor(out=stat[:, 6:7], in0=stat[:, 6:7], in1=stat[:, 7:8],
                               op=ALU.mult))
            st(v.tensor_scalar(out=xnt[j][:], in0=x0t[j][:], scalar1=stat[:, 2:3],
                               scalar2=stat[:, 6:7], op0=ALU.add, op1=ALU.mult))
            st(v.reduce_max(out=stat[:, 4:5], in_=xnt[j][:], axis=AX.X,
                            apply_absolute_value=True))
            st(v.tensor_tensor(out=b.absacc[:], in0=b.absacc[:], in1=stat[:, 4:5],
                               op=ALU.max))
            st(v.tensor_copy(out=xnb[j][:], in_=xnt[j][:]))
            assert S.bump(v, "v", 1) == vt(tt)

    @b.block.scalar
    def _(sc):
        for tt in range(NTT):
            j = tt % 2
            S.wait(sc, f"p0ld{j}", (tt // 2 + 1) * 16)
            if tt >= 1:
                S.wait(sc, "v", vt(tt - 1))
            ii = sc.activation(out=sqt[j][:], in_=x0t[j][:], func=AF.Square)
            S.inc("s", ii, 1)
            S.wait(sc, "vx", vx_base + tt + 1)
            ii = sc.activation(out=stat[:, 5:6], in_=stat[:, 3:4], func=AF.Sqrt)
            S.inc("s", ii, 1)

    b.p0sp_tot = [S.tok("p0sp0"), S.tok("p0sp1")]
    b.v_p0_last = vt(NTT - 1)


def _ar1(b):
    S, sb = b.S, b.sb
    arrow = sb("arrow", [1, 128], F32)
    b.arrow = arrow

    @b.block.sync
    def _(y):
        S.wait(y, "v", b.v_p0_last)
        ii = y.dma_start(out=b.row_dr[0:128], in_=b.absacc[:, 0])
        S.inc("ario", ii, 16)
        S.wait(y, "ario", None)
        ii = y.dma_start(out=arrow[:], in_=b.row_dr.rearrange("(a x) -> a x", a=64)[0])
        S.inc("ario", ii, 16)

    @b.block.vector
    def _(v):
        S.wait(v, "ario", None)
        st = _chain(S, v)
        st(v.reduce_max(out=b.sm[0:1, 0:1], in_=arrow[:], axis=AX.X))
        st(v.tensor_scalar(out=arrow[:], in0=arrow[:], scalar1=0.0,
                           scalar2=b.sm[0:1, 0:1], op0=ALU.mult, op1=ALU.add))
        b.t_armax = S.bump(v, "v", 1)

    @b.block.gpsimd
    def _(g):
        S.wait(g, "v", b.t_armax)
        S.wait(g, "p0sp0", None)
        S.wait(g, "p0sp1", None)
        ii = g.dma_start(out=b.ar_in, in_=arrow[:])
        S.inc("arw", ii, 16)
        S.wait(g, "arw", None)
        ii = g.collective_compute("AllReduce", ALU.max, replica_groups=b.ALLG,
                                  ins=[b.ar_in.opt()], outs=[b.ar_out.opt()])
        S.inc("cc", ii, 1)
        S.wait(g, "cc", None)
        ii = g.dma_start(out=arrow[:], in_=b.ar_out)
        S.inc("arq", ii, 16)

    @b.block.vector
    def _(v):
        S.wait(v, "arq", None)
        st = _chain(S, v)
        st(v.tensor_scalar(out=b.sm[0:1, 1:2], in0=arrow[:, 0:1], scalar1=1.0 / 127.0,
                           scalar2=1e-8, op0=ALU.mult, op1=ALU.max))
        st(v.reciprocal(out=b.sm[0:1, 2:3], in_=b.sm[0:1, 1:2]))
        b.t_sx = S.bump(v, "v", 1)

    @b.block.tensor
    def _(t):
        S.wait(t, "v", b.t_sx)
        ii = t.matmul(b.psP[7][:, 0:2], b.onesr_f[:], b.sm[0:1, 1:3],
                      start=True, stop=True)
        b.t_sxmm = S.inc("t", ii, 1)

    @b.block.vector
    def _(v):
        S.wait(v, "t", b.t_sxmm)
        st = _chain(S, v)
        st(v.tensor_copy(out=b.sm[:, 3:5], in_=b.psP[7][:, 0:2]))
        st(v.tensor_copy(out=b.rsx_b[:], in_=b.sm[:, 4:5]))
        b.t_sxb = S.bump(v, "v", 1)


def _quant_x(b):
    S, c, RV = b.S, b.cfg, b.RV
    D, NT, NTT, KD = c["D"], c["NT"], c["NTT"], c["KD"]

    b.qx_fm = [RV("RA", BF16, k * 2 * NT, 128, NT) for k in range(KD)]
    qmg = [RV("RB", F32, j * 4 * D, 128, D) for j in range(2)]
    qint = [RV("RB", I32, 8 * D + j * 4 * D, 128, D) for j in range(2)]
    qbf = [RV("RB", BF16, 16 * D + j * 2 * D, 128, D) for j in range(2)]

    sq_base = S.tok("s")
    vq0 = S.tok("v")
    tqv = lambda tt: vq0 + tt + 1

    def q_transp(y, tt):
        j = tt % 2
        S.wait(y, "v", tqv(tt))
        for k in range(KD):
            ii = y.dma_start_transpose(b.qx_fm[k][:, tt * 128:(tt + 1) * 128],
                                       qbf[j][:, k * 128:(k + 1) * 128])
            S.inc(f"qtp{j}", ii, 16)

    @b.block.sync
    def _(y):
        b.gate(y)
        for tt in range(NTT):
            j = tt % 2
            S.wait(y, f"p0sp{j}", (tt // 2 + 1) * 16 * (1 + KD))
            if tt >= 2:
                S.wait(y, "v", tqv(tt - 2))
            ii = y.dma_start(out=qmg[j][:], in_=b.xn32_sp[tt * 128:(tt + 1) * 128, :])
            S.inc(f"qld{j}", ii, 16)

    @b.block.scalar
    def _(sc):
        b.gate(sc)
        S.wait(sc, "v", b.t_sxb)
        for tt in range(NTT):
            j = tt % 2
            S.wait(sc, f"qld{j}", (tt // 2 + 1) * 16)
            if tt >= 2:
                S.wait(sc, "v", tqv(tt - 2))
            ii = sc.activation(out=qmg[j][:], in_=qmg[j][:], func=AF.Copy,
                               bias=MAGIC, scale=b.rsx_b[:, 0:1])
            S.inc("s", ii, 1)
            if tt >= 1:
                q_transp(sc, tt - 1)
        q_transp(sc, NTT - 1)

    @b.block.vector
    def _(v):
        b.gate(v)
        for tt in range(NTT):
            j = tt % 2
            S.wait(v, "s", sq_base + tt + 1)
            if tt >= 2:
                S.wait(v, f"qtp{j}", (tt // 2) * 16 * KD)
            st = _chain(S, v)
            st(v.tensor_scalar(out=qint[j][:], in0=qmg[j][:], scalar1=-MAGIC,
                               scalar2=None, op0=ALU.add))
            st(v.tensor_scalar(out=qbf[j][:], in0=qint[j][:], scalar1=0.0,
                               scalar2=None, op0=ALU.add))
            assert S.bump(v, "v", 1) == tqv(tt)

    b.qtp_tot = [S.tok("qtp0"), S.tok("qtp1")]


def _proj(b, layer):
    S, c = b.S, b.cfg
    RV = b.RV
    D, DL, NT, TS = c["D"], c["DL"], c["NT"], c["TS"]
    KD, KL, NS = c["KD"], c["KL"], c["NS"]
    L = layer
    if L == 0:
        KO = KL
        ag_in, src_fm = b.ag0_in, b.xn_fm
        W1s, W2s = [b.wq, b.wk, b.wv], [b.wql, b.wkl, b.wvl]
        biasL, two_stage, OROW = b.biasL, True, DL
    else:
        KO = KD
        ag_in, src_fm = b.ag1_in, b.xn1_fm
        W1s, W2s = [b.Wq1, b.Wk1, b.Wv1], None
        biasL, two_stage, OROW = b.biasQ1, False, D
    pfx = f"L{L}p"
    SPP = 2 if NS >= 2 else 1
    NSP = max(NS // SPP, 1)

    if L == 0:
        b.a_fm = [RV("RB", BF16, k * 2 * NT, 128, NT) for k in range(KD)]
        b.wcol = [RV("RW", BF16, j * 2 * D, 128, D) for j in range(2)]
        b.pbounce = [RV("RW", BF16, 4 * D + jb * 2 * TS, 128, TS) for jb in range(4)]
    a_fm, wcol, pbounce = b.a_fm, b.wcol, b.pbounce
    psP = b.psP

    NBLK = KD + (KO if two_stage else 0)
    t0 = S.tok("t")
    e0 = S.tok(pfx + "e")

    @b.block.gpsimd
    def _(g):
        if L == 1:
            b.gate(g)
        n = 0
        for pj in range(3):
            for blk in range(NBLK):
                j = n % 2
                if n >= 2:
                    S.wait(g, "t", t0 + (n - 1) * NSP)
                for kq in range(KD):
                    if two_stage and blk >= KD:
                        W, dc = W2s[pj], blk - KD
                        ap = bass.AP(W.tensor, dc * 128 + kq * 128 * DL,
                                     [[DL, 128], [1, 128]])
                    else:
                        W = W1s[pj]
                        ap = bass.AP(W.tensor, blk * 128 + kq * 128 * D,
                                     [[D, 128], [1, 128]])
                    ii = g.dma_start(out=wcol[j][:, kq * 128:(kq + 1) * 128],
                                     in_=ap)
                    S.inc(f"{pfx}w{j}", ii, 16)
                n += 1

    @b.block.tensor
    def _(t):
        g = 0
        for pj in range(3):
            for blk in range(NBLK):
                n = pj * NBLK + blk
                j = n % 2
                S.wait(t, f"{pfx}w{j}", (n // 2 + 1) * 16 * KD)
                stage2 = two_stage and blk >= KD
                src = a_fm if stage2 else src_fm
                if L == 0 and n == 0:
                    S.wait(t, "p0sp0", b.p0sp_tot[0])
                    S.wait(t, "p0sp1", b.p0sp_tot[1])
                    S.wait(t, "v", b.t_sxb)
                if L == 1 and n == 0:
                    S.wait(t, "ln1tp0", b.ln1tp_tot[0])
                    S.wait(t, "ln1tp1", b.ln1tp_tot[1])
                bank0 = 4 if stage2 else 0
                for spp in range(NSP):
                    pp = bank0 + (g % 2) * SPP
                    if stage2 and blk == KD and spp == 0:
                        S.wait(t, pfx + "e", e0 + g)
                    elif g >= 2:
                        S.wait(t, pfx + "e", e0 + g - 1)
                    ii = None
                    for k in range(KD):
                        for s2 in range(SPP):
                            sp = spp * SPP + s2
                            ii = t.matmul(psP[pp + s2][:],
                                          wcol[j][:, k * 128:(k + 1) * 128],
                                          src[k][:, sp * TS:(sp + 1) * TS],
                                          start=(k == 0), stop=(k == KD - 1))
                    S.inc("t", ii, 1)
                    g += 1

    bnc_jobs = []

    @b.block.vector
    def _(v):
        S.wait(v, "cst", b.cst_tot)
        g = 0
        nb = 0
        for pj in range(3):
            for blk in range(NBLK):
                stage2 = two_stage and blk >= KD
                outp = stage2 or (not two_stage)
                bank0 = 4 if stage2 else 0
                for spp in range(NSP):
                    S.wait(v, "t", t0 + g + 1)
                    pp = bank0 + (g % 2) * SPP
                    lastii = None
                    for s2 in range(SPP):
                        sp = spp * SPP + s2
                        if not outp:
                            f = blk
                            lastii = v.tensor_scalar(
                                out=a_fm[f][:, sp * TS:(sp + 1) * TS],
                                in0=psP[pp + s2][:],
                                scalar1=b.biasK[:, pj * KD + f:pj * KD + f + 1],
                                scalar2=None, op0=ALU.add)
                        else:
                            dc = (blk - KD) if two_stage else blk
                            jb = nb % 4
                            if nb >= 4:
                                S.wait(v, f"{pfx}bn{jb}", (nb // 4) * 16)
                            lastii = v.tensor_scalar(
                                out=pbounce[jb][:], in0=psP[pp + s2][:],
                                scalar1=biasL[:, pj * KO + dc:pj * KO + dc + 1],
                                scalar2=None, op0=ALU.add)
                            tk = S.inc("v", lastii, 1)
                            bnc_jobs.append((pj, dc, sp, jb, tk))
                            nb += 1
                    if lastii is not None:
                        S.wait(v, "v" if outp else "t", None if outp else t0 + g + 1)
                        if not outp:
                            S.inc(pfx + "e", lastii, 1)
                        else:
                            S.bump(v, pfx + "e", 1)
                    g += 1

    @b.block.sync
    def _(y):
        for (pj, dc, sp, jb, tk) in bnc_jobs:
            S.wait(y, "v", tk)
            dst = ag_in[pj * OROW + dc * 128:pj * OROW + (dc + 1) * 128,
                        sp * TS:(sp + 1) * TS]
            ii = y.dma_start(out=dst, in_=pbounce[jb][:])
            S.inc(f"{pfx}bn{jb}", ii, 16)
    b.__dict__[f"{pfx}bn_tot"] = [S.tok(f"{pfx}bn{jj}") for jj in range(4)]
    b.__dict__[f"{pfx}tlast"] = S.tok("t")


def _ag_qkv(b, layer):
    S = b.S
    L = layer
    pfx = f"L{L}p"
    ag_in, ag_out = (b.ag0_in, b.ag0_out) if L == 0 else (b.ag1_in, b.ag1_out)

    @b.block.gpsimd
    def _(g):
        for jj in range(4):
            S.wait(g, f"{pfx}bn{jj}", b.__dict__[f"{pfx}bn_tot"][jj])
        ii = g.collective_compute("AllGather", ALU.bypass, replica_groups=b.PAIRS,
                                  ins=[ag_in.opt()], outs=[ag_out.opt()])
        b.__dict__[f"{pfx}cc"] = S.inc("cc", ii, 1)


def _m1(b):
    S, c = b.S, b.cfg
    D, I, NT, TS = c["D"], c["I"], c["NT"], c["TS"]
    KD, KI, NS = c["KD"], c["KI"], c["NS"]
    RV = b.RV

    w1f = [RV("RC", F32, j * 4 * D, 128, D) for j in range(2)]
    qw1r = [RV("RC", BF16, 8 * D + j * 2 * D, 128, D) for j in range(2)]
    qw1T = [RV("RC", BF16, 12 * D + j * 2 * D, 128, D) for j in range(2)]
    hpre = [RV("RC", F32, 16 * D + jb * 4 * TS, 128, TS) for jb in range(4)]
    hout = [RV("RC", F32, 16 * D + 16 * TS + jb * 4 * TS, 128, TS) for jb in range(4)]
    wst = b.sb("wst", [128, 8], F32)

    psP = b.psP
    t0 = S.tok("t")
    v0 = S.tok("m1e")
    s0 = S.tok("m1s")
    g0w = [S.tok("m1w0"), S.tok("m1w1")]



    @b.block.vector
    def _(v):
        S.wait(v, "cst", b.cst_tot)
        def absmax_step(n):
            jb = n % 4
            S.wait(v, "m1s", s0 + n + 1)
            st = _chain(S, v)
            st(v.reduce_max(out=wst[:, 3:4], in_=hout[jb][:], axis=AX.X,
                            apply_absolute_value=True))
            st(v.tensor_tensor(out=b.habs[:], in0=b.habs[:], in1=wst[:, 3:4],
                               op=ALU.max))
            S.bump(v, "m1hv", 1)

        for i in range(KI + 3):
            if i < KI:
                j = i % 2
                wb = 4 * j
                S.wait(v, f"m1w{j}", (i // 2 + 1) * 16)
                if i >= 2:
                    S.wait(v, f"m1tp{j}", (i // 2) * 16 * KD)
                    S.wait(v, "m1g", i - 1)
                st = _chain(S, v)
                st(v.reduce_max(out=wst[:, wb:wb + 1], in_=w1f[j][:], axis=AX.X,
                                apply_absolute_value=True))
                st(v.reciprocal(out=wst[:, wb + 1:wb + 2], in_=wst[:, wb:wb + 1]))
                st(v.tensor_scalar(out=wst[:, wb + 2:wb + 3],
                                   in0=wst[:, wb + 1:wb + 2], scalar1=127.0,
                                   scalar2=None, op0=ALU.mult))
                st(v.tensor_scalar(out=b.s1c[:, i:i + 1], in0=wst[:, wb:wb + 1],
                                   scalar1=1.0 / 127.0, scalar2=b.sm[:, 3:4],
                                   op0=ALU.mult, op1=ALU.mult))
                S.bump(v, "m1v", 1)
                S.wait(v, "m1g", i + 1)
                st(v.tensor_scalar(out=qw1r[j][:], in0=w1f[j][:], scalar1=-MAGIC,
                                   scalar2=None, op0=ALU.add))
                st(v.tensor_scalar(out=qw1r[j][:], in0=qw1r[j][:], scalar1=0.0,
                                   scalar2=None, op0=ALU.add))
                S.bump(v, "m1q", 1)
            if 3 <= i:
                ic = i - 3
                if ic < KI:
                    for sp in range(NS):
                        absmax_step(ic * NS + sp)
            if 2 <= i <= KI + 1:
                ic = i - 2
                for sp in range(NS):
                    n = ic * NS + sp
                    jb = n % 4
                    S.wait(v, "t", t0 + ic + 1)
                    if n >= 4:
                        S.wait(v, "m1s", s0 + n - 3)
                    ii = v.tensor_scalar(out=hpre[jb][:],
                                         in0=psP[(ic % 2) * 4 + sp][:],
                                         scalar1=b.s1c[:, ic:ic + 1],
                                         scalar2=b.bias1[:, ic:ic + 1],
                                         op0=ALU.mult, op1=ALU.add)
                    S.inc("m1e", ii, 1)

    @b.block.scalar
    def _(sc):
        for i in range(KI + 2):
            if i < KI:
                j = i % 2
                S.wait(sc, "m1v", i + 1)
                ii = sc.activation(out=w1f[j][:], in_=w1f[j][:], func=AF.Copy,
                                   bias=MAGIC, scale=wst[:, 4 * j + 2:4 * j + 3])
                S.inc("m1g", ii, 1)
            if i >= 2:
                ic = i - 2
                for sp in range(NS):
                    n = ic * NS + sp
                    jb = n % 4
                    S.wait(sc, "m1e", v0 + n + 1)
                    if n >= 4:
                        S.wait(sc, f"m1hx{jb}", (n // 4) * 16)
                    ii = sc.activation(out=hout[jb][:], in_=hpre[jb][:],
                                       func=AF.Gelu)
                    S.inc("m1s", ii, 1)


    def m1_spill(y, i):
        for sp in range(NS):
            n = i * NS + sp
            jb = n % 4
            S.wait(y, "m1hv", n + 1)
            ii = y.dma_start(out=b.h_sp[i * 128:(i + 1) * 128, sp * TS:(sp + 1) * TS],
                             in_=hout[jb][:])
            S.inc(f"m1hx{jb}", ii, 16)

    def w1_load(y, i):
        ii = y.dma_start(out=w1f[i % 2][:], in_=b.w1[i * 128:(i + 1) * 128, :])
        S.inc(f"m1w{i % 2}", ii, 16)

    @b.block.sync
    def _(y):
        b.gate(y)
        w1_load(y, 0)
        w1_load(y, 1)
        for i in range(KI):
            j = i % 2
            S.wait(y, "m1q", i + 1)
            if i + 2 < KI:
                w1_load(y, i + 2)
            if i >= 2:
                S.wait(y, "t", t0 + i - 1)
            for k in range(KD):
                ii = y.dma_start_transpose(qw1T[j][:, k * 128:(k + 1) * 128],
                                           qw1r[j][:, k * 128:(k + 1) * 128])
                S.inc(f"m1tp{j}", ii, 16)
            if i >= 1:
                m1_spill(y, i - 1)
        m1_spill(y, KI - 1)

    @b.block.tensor
    def _(t):
        for i in range(KI):
            j = i % 2
            S.wait(t, f"m1tp{j}", (i // 2 + 1) * 16 * KD)
            if i == 0:
                S.wait(t, "qtp0", b.qtp_tot[0])
                S.wait(t, "qtp1", b.qtp_tot[1])
                S.wait(t, "L0pe", None)
            if i >= 2:
                S.wait(t, "m1e", v0 + (i - 1) * NS)
            ii = None
            for k in range(KD):
                for sp in range(NS):
                    ii = t.matmul(psP[(i % 2) * 4 + sp][:],
                                  qw1T[j][:, k * 128:(k + 1) * 128],
                                  b.qx_fm[k][:, sp * TS:(sp + 1) * TS],
                                  start=(k == 0), stop=(k == KD - 1))
                if k == KD - 1:
                    S.inc("t", ii, 1)

    b.m1_end_t = S.tok("t")
    if b.upto == "m1a":
        return

    @b.block.sync
    def _(y):
        S.wait(y, "m1hv", KI * NS)
        ii = y.dma_start(out=b.row_dr[0:128], in_=b.habs[:, 0])
        S.inc("ar2io", ii, 16)
        S.wait(y, "ar2io", None)
        ii = y.dma_start(out=b.arrow[:], in_=b.row_dr.rearrange("(a x) -> a x", a=64)[0])
        S.inc("ar2io", ii, 16)

    @b.block.vector
    def _(v):
        S.wait(v, "ar2io", None)
        st = _chain(S, v)
        st(v.reduce_max(out=b.sm[0:1, 5:6], in_=b.arrow[:], axis=AX.X))
        st(v.tensor_scalar(out=b.arrow[:], in0=b.arrow[:], scalar1=0.0,
                           scalar2=b.sm[0:1, 5:6], op0=ALU.mult, op1=ALU.add))
        b.t_ar2 = S.bump(v, "v", 1)

    if b.upto == "m1c":
        return

    @b.block.gpsimd
    def _(g):
        S.wait(g, "v", b.t_ar2)
        for jb in range(4):
            S.wait(g, f"m1hx{jb}", None)
        ii = g.dma_start(out=b.ar_in, in_=b.arrow[:])
        S.inc("ar2w", ii, 16)
        S.wait(g, "ar2w", None)
        if b.upto == "m1b":
            ii = g.dma_start(out=b.ar_out, in_=b.ar_in)
            S.inc("cc", ii, 16)
            S.wait(g, "cc", None)
        else:
            ii = g.collective_compute("AllReduce", ALU.max, replica_groups=b.ALLG,
                                      ins=[b.ar_in.opt()], outs=[b.ar_out.opt()])
            S.inc("cc", ii, 1)
            S.wait(g, "cc", None)
        ii = g.dma_start(out=b.arrow[:], in_=b.ar_out)
        S.inc("ar2q", ii, 16)

    if b.upto == "m1d":
        return

    @b.block.vector
    def _(v):
        S.wait(v, "ar2q", None)
        st = _chain(S, v)
        st(v.tensor_scalar(out=b.sm[0:1, 6:7], in0=b.arrow[:, 0:1], scalar1=1.0 / 127.0,
                           scalar2=1e-8, op0=ALU.mult, op1=ALU.max))
        st(v.reciprocal(out=b.sm[0:1, 7:8], in_=b.sm[0:1, 6:7]))
        b.t_s2 = S.bump(v, "v", 1)

    if b.upto == "m1e":
        return

    @b.block.tensor
    def _(t):
        S.wait(t, "v", b.t_s2)
        ii = t.matmul(b.psP[7][:, 4:6], b.onesr_f[:], b.sm[0:1, 6:8],
                      start=True, stop=True)
        b.t_s2mm = S.inc("t", ii, 1)

    @b.block.vector
    def _(v):
        S.wait(v, "t", b.t_s2mm)
        st = _chain(S, v)
        st(v.tensor_copy(out=b.sm[:, 8:10], in_=b.psP[7][:, 4:6]))
        st(v.tensor_copy(out=b.rs2_b[:], in_=b.sm[:, 9:10]))
        b.t_s2b = S.bump(v, "v", 1)




def _w2_quant(b):
    S, c = b.S, b.cfg
    D, I, KD, KI = c["D"], c["I"], c["KD"], c["KI"]

    RV = b.RV
    w2f = RV("RB", F32, 0, 128, I)
    sw2 = b.sb("sw2", [128, KD], F32)
    rwbc = RV("RB", F32, 4 * I, 128, D)
    SWB0 = max(16 * D, 16 * c["NT"])
    b.sw2bc = RV("RC", F32, SWB0, 128, D)
    b.bias0bc = RV("RC", F32, SWB0 + 4 * D, 128, D)
    w2t = [RV("RB", F32, 4 * I + 4 * D + j * 4 * D, 128, D) for j in range(2)]
    w2i = [RV("RB", I32, j * 4 * D, 128, D) for j in range(2)]
    w2b = [RV("RC", BF16, j * 2 * D, 128, D) for j in range(2)]
    rwrow = RV("RC", F32, 4 * D, 1, D)
    swrow = RV("RC", F32, 8 * D, 1, D)
    b0row = RV("RC", F32, 12 * D, 1, D)
    amrow = RV("RB", F32, 4 * I + 4 * D, 1, D)
    borow = RV("RB", F32, 4 * I, 1, D)

    @b.block.gpsimd
    def _(g):
        b.gate(g)
        for oc in range(KD):
            if oc >= 1:
                S.wait(g, "w2a", oc)
            ii = g.dma_start(out=w2f[:], in_=b.w2[oc * 128:(oc + 1) * 128, :])
            S.inc("w2l", ii, 16)

    @b.block.vector
    def _(v):
        b.gate(v)
        for oc in range(KD):
            S.wait(v, "w2l", (oc + 1) * 16)
            st = _chain(S, v)
            st(v.reduce_max(out=sw2[:, oc:oc + 1], in_=w2f[:], axis=AX.X,
                            apply_absolute_value=True))
            S.bump(v, "w2a", 1)
    t_w2a = S.tok("w2a")

    @b.block.sync
    def _(y):
        b.gate(y)
        S.wait(y, "w2a", t_w2a)
        for oc in range(KD):
            ii = y.dma_start(out=b.row_dr[oc * 128:(oc + 1) * 128], in_=sw2[:, oc])
            S.inc("w2r", ii, 16)
        S.wait(y, "w2r", None)
        ii = y.dma_start(out=amrow[:], in_=bass.AP(b.row_dr.tensor, 0, [[D, 1], [1, D]]))
        S.inc("w2r2", ii, 16)
        ii = y.dma_start(out=b0row[:], in_=bass.AP(b.b2.tensor, 0, [[D, 1], [1, D]]))
        S.inc("w2r2", ii, 16)
        ii = y.dma_start(out=borow[:], in_=bass.AP(b.bo.tensor, 0, [[D, 1], [1, D]]))
        S.inc("w2r2", ii, 16)

    @b.block.vector
    def _(v):
        S.wait(v, "w2r2", None)
        st = _chain(S, v)
        st(v.reciprocal(out=rwrow[:], in_=amrow[:]))
        st(v.tensor_scalar(out=rwrow[:], in0=rwrow[:], scalar1=127.0, scalar2=None,
                           op0=ALU.mult))
        S.wait(v, "v", b.t_s2b)
        st(v.tensor_scalar(out=swrow[:], in0=amrow[:], scalar1=1.0 / 127.0,
                           scalar2=b.sm[0:1, 8:9], op0=ALU.mult, op1=ALU.mult))
        st(v.tensor_tensor(out=b0row[:], in0=b0row[:], in1=borow[:], op=ALU.add))
        b.t_w2rows = S.bump(v, "v", 1)

    OC = min(512, D)
    NSTR = D // OC

    @b.block.tensor
    def _(t):
        S.wait(t, "v", b.t_w2rows)
        S.wait(t, "m1e", None)
        for sseg in range(NSTR):
            sl = slice(sseg * OC, (sseg + 1) * OC)
            if sseg >= 1:
                S.wait(t, "w2v", 3 * sseg)
            ii = t.matmul(b.psP[6][:, 0:OC], b.onesr_f[:], rwrow[:, sl],
                          start=True, stop=True)
            S.inc("w2t", ii, 1)
            ii = t.matmul(b.psP[7][:, 0:OC], b.onesr_f[:], swrow[:, sl],
                          start=True, stop=True)
            S.inc("w2t", ii, 1)
            S.wait(t, "w2v", 3 * sseg + 2)
            ii = t.matmul(b.psP[6][:, 0:OC], b.onesr_f[:], b0row[:, sl],
                          start=True, stop=True)
            S.inc("w2t", ii, 1)
    t_w2t0 = S.tok("w2t") - 3 * NSTR

    @b.block.vector
    def _(v):
        for sseg in range(NSTR):
            sl = slice(sseg * OC, (sseg + 1) * OC)
            S.wait(v, "w2t", t_w2t0 + 3 * sseg + 2)
            st = _chain(S, v)
            st(v.tensor_copy(out=rwbc[:, sl], in_=b.psP[6][:, 0:OC]))
            st(v.tensor_copy(out=b.sw2bc[:, sl], in_=b.psP[7][:, 0:OC]))
            S.bump(v, "w2v", 2)
            S.wait(v, "w2t", t_w2t0 + 3 * sseg + 3)
            st(v.tensor_copy(out=b.bias0bc[:, sl], in_=b.psP[6][:, 0:OC]))
            S.bump(v, "w2v", 1)
        S.bump(v, "w2bc", 1)
    b.t_w2bc = S.tok("w2bc")

    @b.block.gpsimd
    def _(g):
        S.wait(g, "v", b.t_w2rows)
        for ic in range(KI):
            j = ic % 2
            if ic >= 2:
                S.wait(g, "w2q", ic - 1)
            ii = g.dma_start(out=w2t[j][:], in_=b.w2T[ic * 128:(ic + 1) * 128, :])
            S.inc(f"w2tl{j}", ii, 16)

    sqw_base = S.tok("s")

    @b.block.vector
    def _(v):
        S.wait(v, "w2bc", b.t_w2bc)
        for ic in range(KI):
            j = ic % 2
            S.wait(v, f"w2tl{j}", (ic // 2 + 1) * 16)
            if ic >= 2:
                S.wait(v, f"w2st{j}", (ic // 2) * 16)
            st = _chain(S, v)
            st(v.tensor_tensor(out=w2t[j][:], in0=w2t[j][:], in1=rwbc[:], op=ALU.mult))
            S.bump(v, "w2m", 1)
            S.wait(v, "s", sqw_base + ic + 1)
            st(v.tensor_scalar(out=w2i[j][:], in0=w2t[j][:], scalar1=-MAGIC,
                               scalar2=None, op0=ALU.add))
            st(v.tensor_scalar(out=w2b[j][:], in0=w2i[j][:], scalar1=0.0,
                               scalar2=None, op0=ALU.add))
            S.bump(v, "w2q", 1)

    @b.block.scalar
    def _(sc):
        b.gate(sc)
        for ic in range(KI):
            j = ic % 2
            S.wait(sc, "w2m", ic + 1)
            ii = sc.activation(out=w2t[j][:], in_=w2t[j][:], func=AF.Copy, bias=MAGIC)
            S.inc("s", ii, 1)

    @b.block.sync
    def _(y):
        for ic in range(KI):
            j = ic % 2
            S.wait(y, "w2q", ic + 1)
            ii = y.dma_start(out=b.qw2T_dr[ic * 128:(ic + 1) * 128, :], in_=w2b[j][:])
            S.inc(f"w2st{j}", ii, 16)
    b.w2st_tot = [S.tok("w2st0"), S.tok("w2st1")]


def _attn0(b):
    S, c = b.S, b.cfg
    D, DL, NT, T, TS = c["D"], c["DL"], c["NT"], c["T"], c["TS"]
    NHm, HD, TQ, NKT = c["NHm"], c["HD"], c["TQ"], c["NKT"]
    SCL = 1.0 / math.sqrt(HD)

    cu = b.CUR("RA")
    qh = [cu(BF16, 64, T) for j in range(2)]
    kh = [cu(BF16, 64, T) for j in range(2)]
    vtm = [cu(BF16, 128, 65 * NKT) for j in range(2)]
    sqs = cu(F32, 64, 512)
    Pb = [cu(BF16, 128, TS) for j in range(3)]
    ctxl = [cu(F32, 65, TS) for j in range(2)]
    rr_ = cu(F32, 4, 2 * TS)
    rro = cu.last_off
    rrow = [b.RV("RA", F32, rro + jj * 4 * TS, 1, TS) for jj in range(2)]
    cbnc = [cu(BF16, 64, TS) for j in range(2)]
    nmax = b.sb("nmax", [1, 4], F32)
    vb = [b.sb(f"vb{sl}", [128, 64], BF16) for sl in range(4)]

    @b.block.vector
    def _(v):
        b.gate(v)
        for j in range(2):
            ap = bass.AP(vtm[j].tensor, vtm[j].offset + 64,
                         [[vtm[j].tensor.shape[1], 128], [65, NKT], [1, 1]])
            ii = v.memset(ap, 1.0)
            S.inc("avo", ii, 1)

    @b.block.scalar
    def _(sc):
        b.gate(sc)
        sc.nop()

    NSQ = T // 512
    TQl = TQ
    NE0 = 2 * TQl * (TQl + 1)
    ent = 0
    grp = 0

    @b.block.sync
    def _(y):
        b.gate(y)
        S.wait(y, "cc", b.L0pcc)
        for side in b.sswitch(y, "y"):
            ro = side * NHm * HD * NT
            for h in range(NHm):
                j = h % 2
                if h >= 2:
                    S.wait(y, "ac", (h - 1) * NE0)
                    S.wait(y, "ax", h * NE0)
                for tb in range(2):
                    src_q = bass.AP(b.ag0_out.tensor,
                                    (tb * 3 * DL + h * 64) * NT + ro,
                                    [[NT, 64], [1, NT]])
                    ii = y.dma_start(out=qh[j][:, tb * NT:(tb + 1) * NT], in_=src_q)
                    S.inc(f"aq{j}", ii, 16)
                    src_k = bass.AP(b.ag0_out.tensor,
                                    (tb * 3 * DL + DL + h * 64) * NT + ro,
                                    [[NT, 64], [1, NT]])
                    ii = y.dma_start(out=kh[j][:, tb * NT:(tb + 1) * NT], in_=src_k)
                    S.inc(f"aq{j}", ii, 16)
                for kt in range(NKT):
                    tb = kt * 128 // NT
                    col = kt * 128 - tb * NT
                    n = h * NKT + kt
                    sl = n % 4
                    src_v = bass.AP(b.ag0_out.tensor,
                                    (tb * 3 * DL + 2 * DL + h * 64) * NT + ro + col,
                                    [[NT, 64], [1, 128]])
                    if n >= 4:
                        S.wait(y, f"acp{sl}", (n // 4) * 16)
                    ii = y.dma_start_transpose(vb[sl][:], src_v)
                    S.inc(f"avb{sl}", ii, 16)
                for qs in range(TQl):
                    g = h * TQl + qs
                    S.wait(y, "aee", g + 1)
                    ii = y.dma_start(out=b.ag0c_in[h * 64:(h + 1) * 64,
                                                   qs * TS:(qs + 1) * TS],
                                     in_=cbnc[g % 2][:])
                    S.inc(f"acw{g % 2}", ii, 16)

    for h in range(NHm):
        j = h % 2
        nsq0 = S.tok("ant")

        @b.block.vector
        def _(v, h=h, j=j):
            st = _chain(S, v)
            for kt in range(NKT):
                n = h * NKT + kt
                sl = n % 4
                S.wait(v, f"avb{sl}", (n // 4 + 1) * 16)
                st(v.tensor_copy(out=vtm[j][:, kt * 65:kt * 65 + 64],
                                 in_=vb[sl][:]))
                S.bump(v, f"acp{sl}", 16)

        @b.block.scalar
        def _(sc, h=h, j=j, nsq0=nsq0):
            S.wait(sc, f"aq{j}", (h // 2 + 1) * 64)
            for w_ in range(2):
                for st_ in range(NSQ):
                    n = w_ * NSQ + st_
                    if n >= 1:
                        S.wait(sc, "ant", nsq0 + n)
                    src = (qh if w_ == 0 else kh)[j][:, st_ * 512:(st_ + 1) * 512]
                    ii = sc.activation(out=sqs[:], in_=src, func=AF.Square)
                    S.inc("asq", ii, 1)

        asq0 = S.tok("asq") - 2 * NSQ

        @b.block.tensor
        def _(t, h=h, nsq0=nsq0, asq0=asq0):
            if h == 0:
                S.wait(t, "m1e", None)
                S.wait(t, "w2t", None)
            for n in range(2 * NSQ):
                S.wait(t, "asq", asq0 + n + 1)
                if n >= 1:
                    S.wait(t, "anv", nsq0 + n)
                ii = t.matmul(b.psP[4][0:1, 0:512], b.ones_f[0:64, :], sqs[:],
                              start=True, stop=True)
                S.inc("ant", ii, 1)

        @b.block.vector
        def _(v, h=h, nsq0=nsq0):
            st = _chain(S, v)
            st(v.memset(nmax[:], 0.0))
            for w_ in range(2):
                for st_ in range(NSQ):
                    n = w_ * NSQ + st_
                    S.wait(v, "ant", nsq0 + n + 1)
                    st(v.reduce_max(out=nmax[0:1, 2:3], in_=b.psP[4][0:1, 0:512],
                                    axis=AX.X))
                    st(v.tensor_tensor(out=nmax[0:1, w_:w_ + 1],
                                       in0=nmax[0:1, w_:w_ + 1],
                                       in1=nmax[0:1, 2:3], op=ALU.max))
                    S.bump(v, "anv", 1)
            st(v.tensor_tensor(out=nmax[0:1, 3:4], in0=nmax[0:1, 0:1],
                               in1=nmax[0:1, 1:2], op=ALU.add))
            st(v.tensor_scalar(out=nmax[0:1, 3:4], in0=nmax[0:1, 3:4],
                               scalar1=-0.5 * SCL, scalar2=None, op0=ALU.mult))
            S.bump(v, "acr", 1)

        @b.block.tensor
        def _(t, h=h):
            S.wait(t, "acr", h + 1)
            ii = t.matmul(b.psP[6][:, 0:1], b.onesr_f[:], nmax[0:1, 3:4],
                          start=True, stop=True)
            S.inc("acb", ii, 1)

        @b.block.vector
        def _(v, h=h, j=j):
            S.wait(v, "acb", h + 1)
            if h >= 2:
                S.wait(v, "ax", (h - 1) * NE0)
            st = _chain(S, v)
            st(v.tensor_copy(out=b.sm[:, 10 + j:11 + j], in_=b.psP[6][:, 0:1]))
            S.bump(v, "acc2", 1)

        ent0 = S.tok("at")
        sched_h = [(qs, kt) for qs in range(TQ) for kt in range(4 * (qs + 1))]
        NE = len(sched_h)
        eidx_h = {e: i for i, e in enumerate(sched_h)}
        ax0 = S.tok("ax")
        ac0 = S.tok("ac")
        am_map = {}
        nm = S.tok("am")
        for (qs, kt) in sched_h:
            if kt >= 4 * qs:
                am_map[(qs, kt)] = nm
                nm += 1
        grp0 = grp

        @b.block.tensor
        def _(t, h=h, j=j, ent0=ent0, ax0=ax0, ac0=ac0, am_map=am_map,
              sched_h=sched_h, eidx_h=eidx_h, grp0=grp0):
            S.wait(t, f"aq{j}", (h // 2 + 1) * 64)
            for sl_ in range(4):
                S.wait(t, f"acp{sl_}", (h + 1) * (NKT // 4) * 16)
            S.wait(t, "avo", 2)
            S.wait(t, "acc2", h + 1)
            for n, (qs, kt) in enumerate(sched_h):
                g = grp0 + qs
                if kt == 0 and g >= 2:
                    S.wait(t, "aev", g - 1)
                if ent0 + n >= 2:
                    S.wait(t, "ax", ent0 + n - 1)
                ii = t.matmul(b.psP[n % 2][:, 0:TS],
                              kh[j][:, kt * 128:(kt + 1) * 128],
                              qh[j][:, qs * TS:(qs + 1) * TS],
                              start=True, stop=True)
                S.inc("at", ii, 1)
                if kt >= 4 * qs:
                    S.wait(t, "am", am_map[(qs, kt)] + 1)
                else:
                    S.wait(t, "ax", ent0 + n + 1)
                ii = t.matmul(b.psP[2 + g % 2][0:65, 0:TS],
                              vtm[j][:, kt * 65:(kt + 1) * 65],
                              Pb[(ent0 + n) % 3][:],
                              start=(kt == 0), stop=(kt == 4 * (qs + 1) - 1))
                S.inc("ac", ii, 1)
                if kt == 4 * (qs + 1) - 1:
                    S.wait(t, "arx", g + 1)
                    ii = t.matmul(b.psP[5][0:64, 0:TS], b.onesr_f[:, 0:64],
                                  rrow[g % 2][:], start=True, stop=True)
                    S.inc("ab", ii, 1)

        @b.block.scalar
        def _(sc, h=h, j=j, ent0=ent0, ac0=ac0, sched_h=sched_h):
            for n, (qs, kt) in enumerate(sched_h):
                S.wait(sc, "at", ent0 + n + 1)
                if ent0 + n >= 3:
                    S.wait(sc, "ac", ac0 - ent0 + ent0 + n - 2)
                ii = sc.activation(out=Pb[(ent0 + n) % 3][:],
                                   in_=b.psP[n % 2][:, 0:TS],
                                   func=AF.Exp, bias=b.sm[:, 10 + j:11 + j],
                                   scale=SCL)
                S.inc("ax", ii, 1)

        @b.block.vector
        def _(v, h=h, ent0=ent0, ac0=ac0, sched_h=sched_h, eidx_h=eidx_h,
              grp0=grp0):
            for n, (qs, kt) in enumerate(sched_h):
                g = grp0 + qs
                if kt >= 4 * qs:
                    S.wait(v, "ax", ent0 + n + 1)
                    vv = kt - 4 * qs
                    st = _chain(S, v)
                    st(v.tensor_tensor(out=Pb[(ent0 + n) % 3][:],
                                       in0=Pb[(ent0 + n) % 3][:],
                                       in1=b.tri[:, vv * TS:(vv + 1) * TS],
                                       op=ALU.mult))
                    S.bump(v, "am", 1)
                if kt == 4 * (qs + 1) - 1:
                    S.wait(v, "ac", ac0 + n + 1)
                    if g >= 2:
                        S.wait(v, f"acw{g % 2}", (g // 2) * 16)
                    st = _chain(S, v)
                    st(v.tensor_copy(out=ctxl[g % 2][:],
                                     in_=b.psP[2 + g % 2][0:65, 0:TS]))
                    S.bump(v, "aev", 1)
                    st(v.reciprocal(out=rrow[g % 2][:], in_=ctxl[g % 2][64:65, :]))
                    S.bump(v, "arx", 1)
                    S.wait(v, "ab", g + 1)
                    st(v.tensor_tensor(out=cbnc[g % 2][:], in0=ctxl[g % 2][0:64, :],
                                       in1=b.psP[5][0:64, 0:TS], op=ALU.mult))
                    S.bump(v, "aee", 1)

        ent += NE
        grp += TQ
    b.acw_tot = [S.tok("acw0"), S.tok("acw1")]


def _agc(b, layer):
    S = b.S
    L = layer
    ag_in, ag_out = (b.ag0c_in, b.ag0c_out) if L == 0 else (b.ag1c_in, b.ag1c_out)

    @b.block.gpsimd
    def _(g):
        if L == 0:
            for jj in range(2):
                S.wait(g, f"acw{jj}", b.acw_tot[jj])
        else:
            for jj in range(2):
                S.wait(g, f"a1cw{jj}", b.c1w_tot[jj])
        ii = g.collective_compute("AllGather", ALU.bypass, replica_groups=b.PAIRS,
                                  ins=[ag_in.opt()], outs=[ag_out.opt()])
        b.__dict__[f"agc{L}cc"] = S.inc("cc", ii, 1)


def _m2_x1(b):
    S, c = b.S, b.cfg
    D, DL, I, NT, T = c["D"], c["DL"], c["I"], c["NT"], c["T"]
    KD, KI, NTT = c["KD"], c["KI"], c["NTT"]
    KLc = DL // 128
    NTB = NT // 512
    NOS = 2 if D >= 1024 else 1
    OSW = D // NOS
    W2 = min(512, OSW)
    NOS2 = OSW // W2
    psP = b.psP

    RV = b.RV
    SWB0 = max(16 * D, 16 * NT)
    hq = [RV("RC", F32, j * 4 * NT, 128, NT) for j in range(2)]
    hqi = [RV("RC", I32, 8 * NT + j * 4 * NT, 128, NT) for j in range(2)]
    hqb = [RV("RC", BF16, SWB0 + 8 * D + j * 2 * NT, 128, NT)
           for j in range(2)]

    def q2_store(y, i):
        j = i % 2
        S.wait(y, "q2v", i + 1)
        ii = y.dma_start(out=b.q2_dr[i * 128:(i + 1) * 128, :], in_=hqb[j][:])
        S.inc(f"q2w{j}", ii, 16)

    @b.block.sync
    def _(y):
        b.gate(y)
        for jb in range(4):
            S.wait(y, f"m1hx{jb}", None)
        for i in range(KI):
            j = i % 2
            if i >= 2:
                S.wait(y, "q2v", i - 1)
            ii = y.dma_start(out=hq[j][:], in_=b.h_sp[i * 128:(i + 1) * 128, :])
            S.inc(f"q2l{j}", ii, 16)
            if i >= 1:
                q2_store(y, i - 1)
        q2_store(y, KI - 1)

    if b.upto == "q2A":
        return

    @b.block.scalar
    def _(sc):
        S.wait(sc, "v", b.t_s2b)
        for i in range(KI):
            j = i % 2
            S.wait(sc, f"q2l{j}", (i // 2 + 1) * 16)
            if b.upto == "q2v2":
                ii = sc.activation(out=hq[j][:], in_=hq[j][:], func=AF.Copy,
                                   bias=MAGIC, scale=26.0)
            else:
                ii = sc.activation(out=hq[j][:], in_=hq[j][:], func=AF.Copy,
                                   bias=MAGIC, scale=b.rs2_b[:, 0:1])
            S.inc("q2s", ii, 1)
    if b.upto == "q2B":
        return

    @b.block.vector
    def _(v):
        if b.upto != "q2ng":
            b.gate(v)
        for i in range(KI):
            j = i % 2
            S.wait(v, "q2s", i + 1)
            if i >= 2:
                S.wait(v, f"q2w{j}", (i // 2) * 16)
            st = _chain(S, v)
            st(v.tensor_scalar(out=hqi[j][:], in0=hq[j][:], scalar1=-MAGIC,
                               scalar2=None, op0=ALU.add))
            st(v.tensor_scalar(out=hqb[j][:], in0=hqi[j][:], scalar1=0.0,
                               scalar2=None, op0=ALU.add))
            S.bump(v, "q2v", 1)

    if b.upto == "q2C":
        return

    b.q2w_tot = [S.tok("q2w0"), S.tok("q2w1")]
    if b.upto == "q2":
        return

    cb_ = b.CUR("RB")
    wop = [cb_(BF16, 128, OSW) for kc in range(KLc)]
    q2s_ = [cb_(BF16, 128, 512) for j in range(2)]
    qwS = [cb_(BF16, 128, OSW) for j in range(2)]
    ctxf = [cb_(BF16, 128, 512) for kc in range(KLc)]
    rbm_top = cb_.pos[0]
    ca_ = b.CUR("RA")
    x1b = [ca_(F32, 128, D) for tt in range(4)]
    x0m = [ca_(F32, 128, D) for j in range(2)]
    sqm = ca_(F32, 128, D)
    xob = [ca_(BF16, 128, D) for j in range(2)]
    n1 = min(KD, SWB0 // (2 * NT))
    n2 = min(KD - n1, (b.reg["RC"]["size"] - (SWB0 + 8 * D)) // (2 * NT))
    b.xn1_fm = []
    for k in range(KD):
        if k < n1:
            b.xn1_fm.append(RV("RC", BF16, k * 2 * NT, 128, NT))
        elif k < n1 + n2:
            b.xn1_fm.append(RV("RC", BF16, SWB0 + 8 * D + (k - n1) * 2 * NT,
                               128, NT))
        else:
            off = max(32768, ((rbm_top + 63) // 64) * 64)
            b.xn1_fm.append(RV("RB", BF16, off + (k - n1 - n2) * 2 * NT, 128, NT))
    statm = b.sb("statm", [128, 8], F32)

    @b.block.gpsimd
    def _(g):
        b.gate(g)
        for tb in range(NTB):
            for osp in range(NOS):
                nwo = tb * NOS + osp
                if nwo >= 1:
                    S.wait(g, "m2p", nwo)
                for kc in range(KLc):
                    ap = bass.AP(b.wo.tensor, kc * 128 * D + osp * OSW,
                                 [[D, 128], [1, OSW]])
                    ii = g.dma_start(out=wop[kc][:], in_=ap)
                    S.inc("m2wo", ii, 16)
                for i in range(KI):
                    j = i % 2
                    ni = nwo * KI + i
                    if ni >= 2:
                        S.wait(g, "m2ti", ni - 1)
                    ii = g.dma_start(out=qwS[j][:],
                                     in_=b.qw2T_dr[i * 128:(i + 1) * 128,
                                                   osp * OSW:(osp + 1) * OSW])
                    S.inc(f"m2qw{j}", ii, 16)

    def _m2_sync_body(y, co):
        for tb in range(NTB):
            for osp in range(NOS):
                nwo = tb * NOS + osp
                for i in range(KI):
                    j = i % 2
                    ni = nwo * KI + i
                    if ni >= 2:
                        S.wait(y, "m2ti", ni - 1)
                    ii = y.dma_start(out=q2s_[j][:],
                                     in_=b.q2_dr[i * 128:(i + 1) * 128,
                                                 tb * 512:(tb + 1) * 512])
                    S.inc(f"m2q2{j}", ii, 16)
                if osp == 0:
                    if tb >= 1:
                        S.wait(y, "m2p", tb * NOS)
                    for kc in range(KLc):
                        ap = bass.AP(b.ag0c_out.tensor,
                                     co + (kc * 128 * T + tb * 512),
                                     [[T, 128], [1, 512]])
                        ii = y.dma_start(out=ctxf[kc][:], in_=ap)
                        S.inc("m2cx", ii, 16)
                if osp == NOS - 1:
                    for tt4 in range(4):
                        n = tb * 4 + tt4
                        jx = n % 2
                        if n >= 2:
                            S.wait(y, "m2x1", n - 1)
                        ii = y.dma_start(out=x0m[jx][:],
                                         in_=b.x0[n * 128:(n + 1) * 128, :])
                        S.inc(f"m2x0{jx}", ii, 16)

    @b.block.sync
    def _(y):
        for jj in range(2):
            S.wait(y, f"w2st{jj}", b.w2st_tot[jj])
            S.wait(y, f"q2w{jj}", b.q2w_tot[jj])
        S.wait(y, "cc", b.agc0cc)
        for side in b.sswitch(y, "y"):
            _m2_sync_body(y, side * NT)

    @b.block.tensor
    def _(t):
        S.wait(t, "w2bc", b.t_w2bc)
        for nm_ in ("aee", "ax", "anv", "acc2", "am"):
            S.wait(t, nm_, None)
        for tb in range(NTB):
            for osp in range(NOS):
                nwo = tb * NOS + osp
                for i in range(KI):
                    j = i % 2
                    ni = nwo * KI + i
                    S.wait(t, f"m2qw{j}", (ni // 2 + 1) * 16)
                    S.wait(t, f"m2q2{j}", (ni // 2 + 1) * 16)
                    if i == 0 and nwo >= 1:
                        S.wait(t, "m2eB", nwo)
                    ii = None
                    for tt4 in range(4):
                        for os2 in range(NOS2):
                            ii = t.matmul(psP[tt4 * 2 + os2][:, 0:W2],
                                          q2s_[j][:, tt4 * 128:(tt4 + 1) * 128],
                                          qwS[j][:, os2 * W2:(os2 + 1) * W2],
                                          start=(i == 0), stop=(i == KI - 1))
                    S.inc("m2ti", ii, 1)
                S.wait(t, "m2ev1", nwo + 1)
                S.wait(t, "m2wo", (nwo + 1) * 16 * KLc)
                S.wait(t, "m2cx", (tb + 1) * 16 * KLc)
                ii = None
                for kc in range(KLc):
                    for tt4 in range(4):
                        for os2 in range(NOS2):
                            ii = t.matmul(psP[tt4 * 2 + os2][:, 0:W2],
                                          ctxf[kc][:, tt4 * 128:(tt4 + 1) * 128],
                                          wop[kc][:, os2 * W2:(os2 + 1) * W2],
                                          start=(kc == 0), stop=(kc == KLc - 1))
                S.inc("m2p", ii, 1)

    m2ti0 = S.tok("m2ti") - NTB * NOS * KI
    m2p0 = S.tok("m2p") - NTB * NOS
    u_sp = 16 * (KD + 1)
    svm0 = S.tok("ln1s")
    vln0 = S.tok("ln1v")

    @b.block.vector
    def _(v):
        for tb in range(NTB):
            for osp in range(NOS):
                nwo = tb * NOS + osp
                S.wait(v, "m2ti", m2ti0 + (nwo + 1) * KI)
                if tb >= 1 and osp == 0:
                    S.wait(v, "ln1tp0", 2 * tb * u_sp)
                    S.wait(v, "ln1tp1", 2 * tb * u_sp)
                st = _chain(S, v)
                for tt4 in range(4):
                    for os2 in range(NOS2):
                        csl = slice(osp * OSW + os2 * W2,
                                    osp * OSW + (os2 + 1) * W2)
                        st(v.tensor_tensor(
                            out=x1b[tt4][:, csl],
                            in0=psP[tt4 * 2 + os2][:, 0:W2],
                            in1=b.sw2bc[:, csl], op=ALU.mult))
                S.bump(v, "m2ev1", 1)
                S.wait(v, "m2p", m2p0 + nwo + 1)
                st = _chain(S, v)
                for tt4 in range(4):
                    for os2 in range(NOS2):
                        csl = slice(osp * OSW + os2 * W2,
                                    osp * OSW + (os2 + 1) * W2)
                        st(v.tensor_tensor(
                            out=x1b[tt4][:, csl],
                            in0=x1b[tt4][:, csl],
                            in1=psP[tt4 * 2 + os2][:, 0:W2], op=ALU.add))
                        st(v.tensor_tensor(out=x1b[tt4][:, csl],
                                           in0=x1b[tt4][:, csl],
                                           in1=b.bias0bc[:, csl], op=ALU.add))
                S.bump(v, "m2eB", 1)
            for tt4 in range(4):
                n = tb * 4 + tt4
                jx = n % 2
                jo = n % 2
                S.wait(v, f"m2x0{jx}", ((n // 2) + 1) * 16)
                st = _chain(S, v)
                st(v.tensor_tensor(out=x1b[tt4][:], in0=x1b[tt4][:], in1=x0m[jx][:],
                                   op=ALU.add))
                S.bump(v, "m2x1", 1)
                S.wait(v, "ln1s", svm0 + 2 * n + 1)
                v.reduce_sum(out=statm[:, 0:1], in_=x1b[tt4][:], axis=AX.X)
                st(v.reduce_sum(out=statm[:, 1:2], in_=sqm[:], axis=AX.X))
                st(v.tensor_scalar(out=statm[:, 2:3], in0=statm[:, 0:1],
                                   scalar1=-1.0 / D, scalar2=None, op0=ALU.mult))
                st(v.tensor_tensor(out=statm[:, 4:5], in0=statm[:, 2:3],
                                   in1=statm[:, 2:3], op=ALU.mult))
                st(v.tensor_scalar(out=statm[:, 3:4], in0=statm[:, 1:2],
                                   scalar1=1.0 / D, scalar2=None, op0=ALU.mult))
                st(v.tensor_tensor(out=statm[:, 3:4], in0=statm[:, 3:4],
                                   in1=statm[:, 4:5], op=ALU.subtract))
                st(v.tensor_scalar(out=statm[:, 3:4], in0=statm[:, 3:4],
                                   scalar1=LNEPS, scalar2=None, op0=ALU.add))
                S.bump(v, "ln1x", 1)
                S.wait(v, "ln1s", svm0 + 2 * n + 2)
                st(v.reciprocal(out=statm[:, 6:7], in_=statm[:, 5:6]))
                st(v.tensor_tensor(out=statm[:, 7:8], in0=statm[:, 6:7],
                                   in1=statm[:, 6:7], op=ALU.mult))
                st(v.tensor_tensor(out=statm[:, 7:8], in0=statm[:, 7:8],
                                   in1=statm[:, 3:4], op=ALU.mult))
                st(v.tensor_scalar(out=statm[:, 7:8], in0=statm[:, 7:8],
                                   scalar1=-0.5, scalar2=1.5, op0=ALU.mult,
                                   op1=ALU.add))
                st(v.tensor_tensor(out=statm[:, 6:7], in0=statm[:, 6:7],
                                   in1=statm[:, 7:8], op=ALU.mult))
                if n >= 2:
                    S.wait(v, f"ln1tp{jo}", ((n - 2) // 2 + 1) * u_sp)
                st(v.tensor_scalar(out=xob[jo][:], in0=x1b[tt4][:],
                                   scalar1=statm[:, 2:3], scalar2=statm[:, 6:7],
                                   op0=ALU.add, op1=ALU.mult))
                S.bump(v, "ln1v", 1)

    def ln1_spill(sc, n):
        tt4 = n % 4
        jo = n % 2
        S.wait(sc, "ln1v", vln0 + n + 1)
        ii = sc.dma_start(out=b.x1_sp[n * 128:(n + 1) * 128, :], in_=x1b[tt4][:])
        S.inc(f"ln1tp{jo}", ii, 16)
        for k in range(KD):
            ii = sc.dma_start_transpose(b.xn1_fm[k][:, n * 128:(n + 1) * 128],
                                        xob[jo][:, k * 128:(k + 1) * 128])
            S.inc(f"ln1tp{jo}", ii, 16)

    @b.block.scalar
    def _(sc):
        for n in range(NTT):
            S.wait(sc, "m2x1", n + 1)
            if n >= 1:
                S.wait(sc, "ln1v", vln0 + n)
            ii = sc.activation(out=sqm[:], in_=x1b[n % 4][:], func=AF.Square)
            S.inc("ln1s", ii, 1)
            S.wait(sc, "ln1x", n + 1)
            ii = sc.activation(out=statm[:, 5:6], in_=statm[:, 3:4], func=AF.Sqrt)
            S.inc("ln1s", ii, 1)
            ln1_spill(sc, n)
    b.ln1tp_tot = [S.tok("ln1tp0"), S.tok("ln1tp1")]


def _ffn1(b):
    S, c = b.S, b.cfg
    D, NT, T = c["D"], c["NT"], c["T"]
    KD, NTT = c["KD"], c["NTT"]
    NS = NT // 512
    NTB = NT // 512
    OC1 = min(512, D)
    NOS1 = D // OC1
    psP = b.psP

    cu = b.CUR("RA")
    wrt = cu(BF16, 128, 8 * KD)
    ats = cu(BF16, 128, 64 * KD)
    bcats = cu(BF16, 64, D)
    wbp = [b.RV("RB", BF16, kc * 2 * OC1, 128, OC1) for kc in range(KD)]
    logits_sb = b.RV("RB", F32, 2 * OC1 * KD, 8, NT)
    h1sb = b.RV("RB", F32, 2 * OC1 * KD + 4 * NT, 64, NT)
    assert 2 * OC1 * KD + 8 * NT <= 32768
    lt = cu(F32, 128, 8)
    gstat = cu(F32, 128, 6)
    eqs = cu(F32, 128, 16)
    wt_ = cu(F32, 128, 8)
    wfm = cu(F32, 8, NT)
    wfb = cu(BF16, 8, 512)
    hpb = cu(BF16, 64, NT)
    br_ = cu(F32, 2, D)
    bro = cu.last_off
    bbrow = b.RV("RA", F32, bro, 1, D, p0=0)
    bbasebc = cu(F32, 128, D)
    ybnc = [cu(F32, 128, 512) for j in range(2)]
    assert cu.pos[0] <= b.reg["RA"]["size"], cu.pos[0]

    @b.block.gpsimd
    def _(g):
        b.gate(g)
        for kq in range(KD):
            ap = bass.AP(b.WrT.tensor, kq * 128 * 8, [[8, 128], [1, 8]])
            ii = g.dma_start(out=wrt[:, kq * 8:(kq + 1) * 8], in_=ap)
            S.inc("f1w", ii, 16)
            ap = bass.AP(b.AT.tensor, kq * 128 * 64, [[64, 128], [1, 64]])
            ii = g.dma_start(out=ats[:, kq * 64:(kq + 1) * 64], in_=ap)
            S.inc("f1w", ii, 16)
        ii = g.dma_start(out=bcats[:], in_=b.Bcat)
        S.inc("f1w", ii, 16)
    f1w_tot = S.tok("f1w")

    @b.block.sync
    def _(y):
        b.gate(y)
        ii = y.dma_start(out=bbrow[:], in_=bass.AP(b.bbase.tensor, 0, [[D, 1], [1, D]]))
        S.inc("f1r", ii, 16)

    NSTR = NOS1

    @b.block.tensor
    def _(t):
        S.wait(t, "L1pe", None)
        S.wait(t, "m2eB", None)
        S.wait(t, "m2ti", None)
        S.wait(t, "ln1v", None)
        S.wait(t, "f1r", None)
        for sseg in range(NSTR):
            if sseg >= 1:
                S.wait(t, "f1bv", sseg)
            ii = t.matmul(psP[6][:, 0:OC1], b.onesr_f[:],
                          bbrow[:, sseg * OC1:(sseg + 1) * OC1], start=True, stop=True)
            S.inc("f1bt", ii, 1)
    f1bt0 = S.tok("f1bt") - NSTR

    @b.block.vector
    def _(v):
        b.gate(v)
        for sseg in range(NSTR):
            S.wait(v, "f1bt", f1bt0 + sseg + 1)
            st = _chain(S, v)
            st(v.tensor_copy(out=bbasebc[:, sseg * OC1:(sseg + 1) * OC1],
                             in_=psP[6][:, 0:OC1]))
            S.bump(v, "f1bv", 1)
    t_f1bc = S.tok("f1bv")

    @b.block.tensor
    def _(t):
        S.wait(t, "f1w", f1w_tot)
        for ss in range(NS):
            if ss >= 1:
                S.wait(t, "f1le", 2 * ss)
            ii = None
            for k in range(KD):
                ii = t.matmul(psP[4][0:8, 0:512], wrt[:, k * 8:(k + 1) * 8],
                              b.xn1_fm[k][:, ss * 512:(ss + 1) * 512],
                              start=(k == 0), stop=(k == KD - 1))
            S.inc("f1l", ii, 1)
            for k in range(KD):
                ii = t.matmul(psP[5][0:64, 0:512], ats[:, k * 64:(k + 1) * 64],
                              b.xn1_fm[k][:, ss * 512:(ss + 1) * 512],
                              start=(k == 0), stop=(k == KD - 1))
            S.inc("f1l", ii, 1)
    f1l0 = S.tok("f1l") - 2 * NS

    @b.block.vector
    def _(v):
        for ss in range(NS):
            S.wait(v, "f1l", f1l0 + 2 * ss + 1)
            st = _chain(S, v)
            st(v.tensor_copy(out=logits_sb[:, ss * 512:(ss + 1) * 512],
                             in_=psP[4][0:8, 0:512]))
            S.bump(v, "f1le", 1)
            S.wait(v, "f1l", f1l0 + 2 * ss + 2)
            st(v.tensor_copy(out=h1sb[:, ss * 512:(ss + 1) * 512],
                             in_=psP[5][0:64, 0:512]))
            S.bump(v, "f1le", 1)
    f1le_tot = S.tok("f1le")

    sg0 = S.tok("f1gs")

    @b.block.tensor
    def _(t):
        for tt in range(NTT):
            S.wait(t, "f1le", f1le_tot)
            if tt == 0:
                S.wait(t, "f1bv", t_f1bc)
            if tt >= 1:
                S.wait(t, "f1gv", 2 * tt)
            ii = t.transpose(psP[6][0:128, 0:8],
                             logits_sb[:, tt * 128:(tt + 1) * 128],
                             b.idT[0:8, 0:8])
            S.inc("f1gt", ii, 1)
            S.wait(t, "f1gv", 2 * tt + 1)
            ii = t.transpose(psP[6][0:8, 8:136],
                             wt_[:], b.idT[:])
            S.inc("f1gt", ii, 1)
    f1gt0 = S.tok("f1gt") - 2 * NTT

    @b.block.vector
    def _(v):
        for tt in range(NTT):
            S.wait(v, "f1gt", f1gt0 + 2 * tt + 1)
            st = _chain(S, v)
            st(v.tensor_copy(out=lt[:], in_=psP[6][0:128, 0:8]))
            st(v.reduce_max(out=gstat[:, 0:1], in_=lt[:], axis=AX.X))
            st(v.tensor_scalar(out=eqs[:, 0:8], in0=lt[:], scalar1=gstat[:, 0:1],
                               scalar2=None, op0=ALU.is_equal))
            st(v.scalar_tensor_tensor(out=eqs[:, 8:16], in0=eqs[:, 0:8],
                                      scalar=-BIG, in1=lt[:],
                                      op0=ALU.mult, op1=ALU.add))
            st(v.reduce_max(out=gstat[:, 1:2], in_=eqs[:, 8:16], axis=AX.X))
            st(v.tensor_scalar(out=eqs[:, 8:16], in0=eqs[:, 8:16],
                               scalar1=gstat[:, 1:2], scalar2=None, op0=ALU.is_equal))
            st(v.tensor_tensor(out=gstat[:, 2:3], in0=gstat[:, 0:1],
                               in1=gstat[:, 1:2], op=ALU.subtract))
            S.bump(v, "f1gd", 1)
            S.wait(v, "f1gg", tt + 1)
            st(v.tensor_scalar(out=gstat[:, 4:5], in0=gstat[:, 3:4], scalar1=-1.0,
                               scalar2=1.0, op0=ALU.mult, op1=ALU.add))
            st(v.tensor_scalar(out=wt_[:], in0=eqs[:, 0:8], scalar1=gstat[:, 3:4],
                               scalar2=2.0, op0=ALU.mult, op1=ALU.mult))
            st(v.scalar_tensor_tensor(out=eqs[:, 8:16], in0=eqs[:, 8:16],
                                      scalar=gstat[:, 4:5], in1=wt_[:],
                                      op0=ALU.mult, op1=ALU.bypass))
            st(v.tensor_scalar(out=eqs[:, 8:16], in0=eqs[:, 8:16], scalar1=2.0,
                               scalar2=None, op0=ALU.mult))
            st(v.tensor_tensor(out=wt_[:], in0=wt_[:], in1=eqs[:, 8:16], op=ALU.add))
            S.bump(v, "f1gv", 1)
            S.wait(v, "f1gt", f1gt0 + 2 * tt + 2)
            st(v.tensor_copy(out=wfm[:, tt * 128:(tt + 1) * 128],
                             in_=psP[6][0:8, 8:136]))
            S.bump(v, "f1gv", 1)
    f1gv_tot = S.tok("f1gv")

    @b.block.scalar
    def _(sc):
        for tt in range(NTT):
            S.wait(sc, "f1gd", tt + 1)
            ii = sc.activation(out=gstat[:, 3:4], in_=gstat[:, 2:3], func=AF.Sigmoid)
            S.inc("f1gg", ii, 1)

    @b.block.tensor
    def _(t):
        S.wait(t, "f1gv", f1gv_tot)
        for ss in range(NS):
            S.wait(t, "f1hv", 2 * ss + 1)
            ii = t.matmul(psP[7][0:64, 0:512], b.r8[:],
                          wfb[:], start=True, stop=True)
            S.inc("f1ht", ii, 1)
    f1ht0 = S.tok("f1ht") - NS

    @b.block.vector
    def _(v):
        for ss in range(NS):
            st = _chain(S, v)
            if ss >= 1:
                S.wait(v, "f1ht", f1ht0 + ss)
            st(v.tensor_copy(out=wfb[:], in_=wfm[:, ss * 512:(ss + 1) * 512]))
            S.bump(v, "f1hv", 1)
            S.wait(v, "f1ht", f1ht0 + ss + 1)
            st(v.tensor_tensor(out=hpb[:, ss * 512:(ss + 1) * 512],
                               in0=h1sb[:, ss * 512:(ss + 1) * 512],
                               in1=psP[7][0:64, 0:512], op=ALU.mult))
            S.bump(v, "f1hv", 1)
    f1hp_tot = S.tok("f1hv")

    @b.block.gpsimd
    def _(g):
        for os in range(NOS1):
            if os >= 1:
                S.wait(g, "f1p", os * NTB * 4)
            for kc in range(KD):
                ap = bass.AP(b.WbT.tensor, kc * 128 * D + os * OC1,
                             [[D, 128], [1, OC1]])
                ii = g.dma_start(out=wbp[kc][:], in_=ap)
                S.inc("f1wb", ii, 16)

    @b.block.tensor
    def _(t):
        S.wait(t, "f1hv", f1hp_tot)
        np_ = 0
        for os in range(NOS1):
            S.wait(t, "f1wb", (os + 1) * 16 * KD)
            for tb in range(NTB):
                for tt4 in range(4):
                    n = tb * 4 + tt4
                    if np_ >= 4:
                        S.wait(t, "f1ye", np_ - 3)
                    ii = None
                    for k in range(KD):
                        ii = t.matmul(psP[n % 4][:, 0:OC1],
                                      b.xn1_fm[k][:, n * 128:(n + 1) * 128],
                                      wbp[k][:], start=(k == 0), stop=False)
                    ii = t.matmul(psP[n % 4][:, 0:OC1],
                                  hpb[:, n * 128:(n + 1) * 128],
                                  bcats[:, os * OC1:(os + 1) * OC1],
                                  start=False, stop=True)
                    S.inc("f1p", ii, 1)
                    np_ += 1
    f1p0 = S.tok("f1p") - NOS1 * NTB * 4

    @b.block.vector
    def _(v):
        S.wait(v, "f1bv", t_f1bc)
        np_ = 0
        for os in range(NOS1):
            for tb in range(NTB):
                for tt4 in range(4):
                    n = tb * 4 + tt4
                    jb = np_ % 2
                    S.wait(v, "f1p", f1p0 + np_ + 1)
                    if np_ >= 2:
                        S.wait(v, f"f1ys{jb}", (np_ // 2) * 16)
                    st = _chain(S, v)
                    st(v.tensor_tensor(out=ybnc[jb][:, 0:OC1],
                                       in0=psP[n % 4][:, 0:OC1],
                                       in1=bbasebc[:, os * OC1:(os + 1) * OC1],
                                       op=ALU.add))
                    S.bump(v, "f1ye", 1)
                    S.bump(v, "f1yv", 1)
                    np_ += 1

    @b.block.sync
    def _(y):
        np_ = 0
        for os in range(NOS1):
            for tb in range(NTB):
                for tt4 in range(4):
                    n = tb * 4 + tt4
                    jb = np_ % 2
                    S.wait(y, "f1yv", np_ + 1)
                    ii = y.dma_start(out=b.y1_sp[n * 128:(n + 1) * 128,
                                                 os * OC1:(os + 1) * OC1],
                                     in_=ybnc[jb][:, 0:OC1])
                    S.inc(f"f1ys{jb}", ii, 16)
                    np_ += 1
    b.f1ys_tot = [S.tok("f1ys0"), S.tok("f1ys1")]


def _l1_select(b):
    S, c = b.S, b.cfg
    D, NT, T = c["D"], c["NT"], c["T"]
    NH1m = c["NH1m"]
    SCL1 = 1.0 / math.sqrt(128.0)
    NSQ = T // 512

    cu = b.CUR("RA")
    q1h = cu(BF16, 128, T)
    sq1 = cu(F32, 128, 512)
    nrow = cu(F32, 1, 512)
    knm = cu(F32, 1, 4)
    kn8 = cu(F32, NH1m, 2)
    nq = cu(F32, NH1m, T)
    nqa = cu(F32, NH1m, T)
    nqb = cu(F32, NH1m, T)
    m8 = cu(F32, NH1m, 8)
    m8w = cu(F32, NH1m, 8)
    m8e = cu(F32, NH1m, 8)
    th = cu(F32, NH1m, 2)
    mask = nqa
    invm = nqb
    negidx = cu(F32, NH1m, 64)
    crow = cu(F32, 1, NH1m)
    assert cu.pos[0] <= b.reg["RA"]["size"], cu.pos[0]


    @b.block.sync
    def _(y):
        S.wait(y, "cc", b.L1pcc)
        for side in b.sswitch(y, "y"):
            ro = side * NH1m * 128 * NT
            for h in range(NH1m):
                for w_ in range(2):
                    S.wait(y, "sl_s", (2 * h + w_) * NSQ)
                    for tb in range(2):
                        src = bass.AP(b.ag1_out.tensor,
                                      (tb * 3 * D + w_ * D + h * 128) * NT + ro,
                                      [[NT, 128], [1, NT]])
                        ii = y.dma_start(out=q1h[:, tb * NT:(tb + 1) * NT],
                                         in_=src)
                        S.inc("sl_l", ii, 16)


    for h in range(NH1m):
        sqn0 = S.tok("sl_t")

        @b.block.scalar
        def _(sc, h=h):
            def nstore(st_):
                S.wait(sc, "sl_nr", h * NSQ + st_ + 1)
                ii = sc.dma_start(out=b.nrm_dr[h, st_ * 512:(st_ + 1) * 512],
                                  in_=nrow[0:1, :])
                S.inc("sl_nw", ii, 16)
            for w_ in range(2):
                S.wait(sc, "sl_l", (2 * h + w_ + 1) * 32)
                for st_ in range(NSQ):
                    n = w_ * NSQ + st_
                    if n >= 1:
                        S.wait(sc, "sl_t", sqn0 + n)
                    ii = sc.activation(out=sq1[:], in_=q1h[:, st_ * 512:(st_ + 1) * 512],
                                       func=AF.Square)
                    S.inc("sl_s", ii, 1)
                    if w_ == 0 and st_ >= 1:
                        nstore(st_ - 1)
                    if w_ == 1 and st_ == 0:
                        nstore(NSQ - 1)
            S.wait(sc, "sl_v", h + 1)
            ii = sc.dma_start(out=b.row_dr[4224 + h:4225 + h],
                              in_=knm[0:1, 0:1])
            S.inc("sl_kw", ii, 16)

        sls0 = S.tok("sl_s") - 2 * NSQ

        @b.block.tensor
        def _(t, h=h, sls0=sls0, sqn0=sqn0):
            if h == 0:
                S.wait(t, "f1p", None)
                S.wait(t, "f1gv", None)
            for n in range(2 * NSQ):
                S.wait(t, "sl_s", sls0 + n + 1)
                if n >= 1:
                    S.wait(t, "sl_v2", sqn0 + n)
                ii = t.matmul(b.psP[4][0:1, 0:512], b.ones_f[:], sq1[:],
                              start=True, stop=True)
                S.inc("sl_t", ii, 1)

        @b.block.vector
        def _(v, h=h, sqn0=sqn0):
            st = _chain(S, v)
            if h >= 1:
                S.wait(v, "sl_kw", h * 16)
            st(v.memset(knm[:], 0.0))
            for w_ in range(2):
                for st_ in range(NSQ):
                    n = w_ * NSQ + st_
                    S.wait(v, "sl_t", sqn0 + n + 1)
                    if w_ == 0:
                        S.wait(v, "sl_nw", (h * NSQ + st_) * 16)
                        st(v.tensor_copy(out=nrow[:], in_=b.psP[4][0:1, 0:512]))
                        S.bump(v, "sl_nr", 1)
                    else:
                        st(v.reduce_max(out=knm[0:1, 1:2], in_=b.psP[4][0:1, 0:512],
                                        axis=AX.X))
                        st(v.tensor_tensor(out=knm[0:1, 0:1], in0=knm[0:1, 0:1],
                                           in1=knm[0:1, 1:2], op=ALU.max))
                    S.bump(v, "sl_v2", 1)
            S.bump(v, "sl_v", 1)


    @b.block.sync
    def _(y):
        S.wait(y, "sl_nw", None)
        ii = y.dma_start(out=nq[:], in_=b.nrm_dr)
        S.inc("sl_q", ii, 16)
        S.wait(y, "sl_kw", None)
        ii = y.dma_start(out=kn8[:, 0:1],
                         in_=bass.AP(b.row_dr.tensor, 4224, [[1, NH1m], [1, 1]]))
        S.inc("sl_q", ii, 16)

    @b.block.vector
    def _(v):
        S.wait(v, "sl_q", None)
        st = _chain(S, v)
        st(v.tensor_copy(out=nqa[:], in_=nq[:]))
        cur, oth = nqa, nqb
        for r in range(7):
            st(v.max(out=m8[:], in_=cur[:]))
            st(v.match_replace(out=oth[:], in_to_replace=m8[:], in_values=cur[:],
                               imm_value=-BIG))
            cur, oth = oth, cur
        st(v.max(out=m8[:], in_=cur[:]))
        st(v.tensor_copy(out=m8w[:], in_=m8[:]))
        for r in range(6):
            st(v.reduce_max(out=th[:, 0:1], in_=m8w[:], axis=AX.X))
            if r < 5:
                st(v.tensor_scalar(out=m8e[:], in0=m8w[:], scalar1=th[:, 0:1],
                                   scalar2=None, op0=ALU.is_equal))
                st(v.scalar_tensor_tensor(out=m8w[:], in0=m8e[:], scalar=-BIG,
                                          in1=m8w[:], op0=ALU.mult, op1=ALU.add))
        st(v.tensor_scalar(out=mask[:], in0=nq[:], scalar1=th[:, 0:1],
                           scalar2=None, op0=ALU.is_ge))
        st(v.memset(mask[:, 0:1], 1.0))
        st(v.memset(mask[:, T - 1:T], 1.0))
        st(v.tensor_scalar(out=invm[:], in0=mask[:], scalar1=-1.0, scalar2=1.0,
                           op0=ALU.mult, op1=ALU.add))
        S.bump(v, "sl_inv", 1)
        S.wait(v, "sl_iw", 16)
        for ss in range(T // 512):
            sl = slice(ss * 512, (ss + 1) * 512)
            st(v.scalar_tensor_tensor(out=nqa[:, sl], in0=b.iota[0:NH1m, :],
                                      scalar=float(ss * 512), in1=mask[:, sl],
                                      op0=ALU.add, op1=ALU.mult))
            st(v.scalar_tensor_tensor(out=nqa[:, sl], in0=invm[:, sl], scalar=BIG,
                                      in1=nqa[:, sl], op0=ALU.mult, op1=ALU.add))
        st(v.tensor_scalar(out=nqa[:], in0=nqa[:], scalar1=-1.0, scalar2=None,
                           op0=ALU.mult))
        cur, oth = nqa, nqb
        for r in range(8):
            st(v.max(out=m8[:], in_=cur[:]))
            st(v.tensor_copy(out=negidx[:, r * 8:(r + 1) * 8], in_=m8[:]))
            if r < 7:
                st(v.match_replace(out=oth[:], in_to_replace=m8[:], in_values=cur[:],
                                   imm_value=-BIG))
                cur, oth = oth, cur
        st(v.tensor_scalar(out=negidx[:], in0=negidx[:], scalar1=-1.0, scalar2=None,
                           op0=ALU.mult))
        S.bump(v, "sl_idx", 1)
        st(v.reduce_max(out=kn8[:, 1:2], in_=nq[:], axis=AX.X))
        st(v.tensor_tensor(out=kn8[:, 0:1], in0=kn8[:, 0:1], in1=kn8[:, 1:2],
                           op=ALU.add))
        st(v.tensor_scalar(out=kn8[:, 0:1], in0=kn8[:, 0:1], scalar1=-0.5 * SCL1,
                           scalar2=None, op0=ALU.mult))
        S.bump(v, "sl_c", 1)

    @b.block.sync
    def _(y):
        S.wait(y, "sl_inv", None)
        ii = y.dma_start(out=b.inv_dr, in_=invm[:])
        S.inc("sl_iw", ii, 16)
        S.wait(y, "sl_idx", None)
        ii = y.dma_start(out=b.idx_dr, in_=negidx[:])
        S.inc("sl_iw", ii, 16)
        S.wait(y, "sl_c", None)
        ii = y.dma_start(out=b.row_dr[4096:4096 + NH1m], in_=kn8[:, 0:1])
        S.inc("sl_iw", ii, 16)
        S.wait(y, "sl_iw", None)
        ii = y.dma_start(out=crow[:], in_=bass.AP(b.row_dr.tensor, 4096,
                                                  [[NH1m, 1], [1, NH1m]]))
        S.inc("sl_cr", ii, 16)

    @b.block.tensor
    def _(t):
        S.wait(t, "sl_cr", None)
        ii = t.matmul(b.psP[6][:, 0:NH1m], b.onesr_f[:], crow[:],
                      start=True, stop=True)
        S.inc("sl_ct", ii, 1)
    t_slct = S.tok("sl_ct")

    @b.block.vector
    def _(v):
        S.wait(v, "sl_ct", t_slct)
        st = _chain(S, v)
        st(v.tensor_copy(out=b.biasC1[:, 0:NH1m], in_=b.psP[6][:, 0:NH1m]))
        S.bump(v, "sl_cb", 1)
    b.t_slcb = S.tok("sl_cb")
    b.sel_idx_done = S.tok("sl_iw")


def _attn1(b):
    S, c = b.S, b.cfg
    D, NT, T = c["D"], c["NT"], c["T"]
    NH1m, NKT = c["NH1m"], c["NKT"]
    SCL1 = 1.0 / math.sqrt(128.0)
    psP = b.psP

    cu = b.CUR("RA")
    q1h = [cu(BF16, 128, T) for j in range(2)]
    k1h = [cu(BF16, 128, T) for j in range(2)]
    vt1 = [cu(BF16, 128, 129 * NKT) for j in range(2)]
    idxc = [cu(F32, 64, 1) for j in range(2)]
    qtm = [cu(BF16, 128, 128) for j in range(2)]
    stile = [cu(BF16, 128, 64) for j in range(2)]
    qsel = cu(BF16, 128, 64)
    Pg = [cu(BF16, 128, 64) for j in range(2)]
    ctxN = cu(BF16, 64, 129)
    cgl = cu(F32, 64, 129)
    Pl = [cu(BF16, 128, 128) for j in range(3)]
    cll = [cu(F32, 128, 128) for j in range(2)]
    lr_ = cu(F32, 2, 256)
    lro = cu.last_off
    lrow = [b.RV("RA", F32, lro + jj * 512, 1, 128) for jj in range(2)]
    cb1 = [cu(BF16, 128, 128) for j in range(2)]
    assert cu.pos[0] <= b.reg["RA"]["size"], cu.pos[0]
    ssel = [b.RV("RB", BF16, j * 2 * T, 64, T) for j in range(2)]
    invr = [b.RV("RB", F32, b.reg["RB"]["size"] - (2 - jj) * 4 * T, 1, T)
            for jj in range(2)]
    vb2 = [b.RV("RW", BF16, sl * 256, 128, 128) for sl in range(4)]
    S.revive(*[f"avb{sl}" for sl in range(4)], *[f"acp{sl}" for sl in range(4)])

    @b.block.vector
    def _(v):
        b.gate(v)
        for j in range(2):
            ap = bass.AP(vt1[j].tensor, vt1[j].offset + 128,
                         [[vt1[j].tensor.shape[1], 128], [129, NKT], [1, 1]])
            ii = v.memset(ap, 1.0)
            S.inc("a1vo", ii, 1)
        for jj in range(3):
            ii = v.memset(Pl[jj][:], 0.0)
            S.inc("a1vo", ii, 1)

    head_end = []

    @b.block.sync
    def _(y):
        b.gate(y)
        for side in b.sswitch(y, "y"):
            ro = side * NH1m * 128 * NT
            for h in range(NH1m):
                j = h % 2
                if h == 0:
                    S.wait(y, "sl_iw", b.sel_idx_done)
                if h >= 2:
                    S.wait(y, "a1ce", h - 1)
                for tb in range(2):
                    src = bass.AP(b.ag1_out.tensor,
                                  (tb * 3 * D + h * 128) * NT + ro,
                                  [[NT, 128], [1, NT]])
                    ii = y.dma_start(out=q1h[j][:, tb * NT:(tb + 1) * NT],
                                     in_=src)
                    S.inc(f"a1l{j}", ii, 16)
                    src = bass.AP(b.ag1_out.tensor,
                                  (tb * 3 * D + D + h * 128) * NT + ro,
                                  [[NT, 128], [1, NT]])
                    ii = y.dma_start(out=k1h[j][:, tb * NT:(tb + 1) * NT],
                                     in_=src)
                    S.inc(f"a1l{j}", ii, 16)
                for kt in range(NKT):
                    tb = kt * 128 // NT
                    col = kt * 128 - tb * NT
                    n = h * NKT + kt
                    sl = n % 4
                    src = bass.AP(b.ag1_out.tensor,
                                  (tb * 3 * D + 2 * D + h * 128) * NT + ro + col,
                                  [[NT, 128], [1, 128]])
                    if n >= 4:
                        S.wait(y, f"acp{sl}", (n // 4) * 16)
                    ii = y.dma_start_transpose(vb2[sl][:], src)
                    S.inc(f"avb{sl}", ii, 16)
                ii = y.dma_start(out=idxc[j][:],
                                 in_=bass.AP(b.idx_dr.tensor, h * 64,
                                             [[1, 64], [1, 1]]))
                S.inc(f"a1l{j}", ii, 16)
                ii = y.dma_start(out=invr[j][:], in_=b.inv_dr[h:h + 1, :])
                S.inc(f"a1l{j}", ii, 16)
                S.wait(y, "a1sb", h + 1)
                for kt in range(NKT):
                    jj = kt % 2
                    if kt >= 2:
                        S.wait(y, "a1gt", h * NKT + kt - 1)
                    ii = y.dma_start_transpose(qtm[jj][:],
                                               q1h[j][:, kt * 128:(kt + 1) * 128])
                    S.inc(f"a1tq{jj}", ii, 16)
                    ii = y.dma_start_transpose(stile[jj][:],
                                               ssel[j][:, kt * 128:(kt + 1) * 128])
                    S.inc(f"a1tq{jj}", ii, 16)
                for qt in range(NKT):
                    ng = h * NKT + qt
                    S.wait(y, "a1ee", ng + 1)
                    ii = y.dma_start(out=b.ag1c_in[h * 128:(h + 1) * 128,
                                                   qt * 128:(qt + 1) * 128],
                                     in_=cb1[qt % 2][:])
                    S.inc(f"a1cw{qt % 2}", ii, 16)

    for h in range(NH1m):
        j = h % 2

        @b.block.vector
        def _(v, h=h, j=j):
            st = _chain(S, v)
            for kt in range(NKT):
                n = h * NKT + kt
                sl = n % 4
                S.wait(v, f"avb{sl}", (n // 4 + 1) * 16)
                st(v.tensor_copy(out=vt1[j][:, kt * 129:kt * 129 + 128],
                                 in_=vb2[sl][:]))
                S.bump(v, f"acp{sl}", 16)

        @b.block.vector
        def _(v, h=h, j=j):
            S.wait(v, f"a1l{j}", (h // 2 + 1) * (4 + 2) * 16)
            if h >= 2:
                S.wait(v, "a1ge", h - 1)
                S.wait(v, "a1sc", (h - 1) * NKT)
            st = _chain(S, v)
            for ss in range(T // 512):
                st(v.tensor_scalar(out=ssel[j][:, ss * 512:(ss + 1) * 512],
                                   in0=b.iota[0:64, :],
                                   scalar1=float(ss * 512),
                                   scalar2=idxc[j][:, 0:1],
                                   op0=ALU.add, op1=ALU.is_equal))
            S.bump(v, "a1sb", 1)

        a1g0 = S.tok("a1gt")

        @b.block.tensor
        def _(t, h=h, j=j, a1g0=a1g0):
            if h == 0:
                S.wait(t, "sl_v2", None)
                S.wait(t, "a1vo", 5)
                S.wait(t, "sl_cb", b.t_slcb)
            for kt in range(NKT):
                jj = kt % 2
                S.wait(t, f"a1tq{jj}", (h * NKT // 2 + kt // 2 + 1) * 32)
                ii = t.matmul(psP[4][:, 0:64], qtm[jj][:], stile[jj][:],
                              start=(kt == 0), stop=(kt == NKT - 1))
                S.inc("a1gt", ii, 1)

        @b.block.vector
        def _(v, h=h):
            S.wait(v, "a1gt", None)
            st = _chain(S, v)
            st(v.tensor_copy(out=qsel[:], in_=psP[4][:, 0:64]))
            S.bump(v, "a1ge", 1)

        a1x0 = S.tok("a1gx")
        a1c0 = S.tok("a1gc")

        @b.block.tensor
        def _(t, h=h, j=j, a1x0=a1x0):
            S.wait(t, "a1ge", h + 1)
            for sl_ in range(4):
                S.wait(t, f"acp{sl_}", (h + 1) * (NKT // 4) * 16)
            for kt in range(NKT):
                if kt >= 2:
                    S.wait(t, "a1gx", a1x0 + kt - 1)
                ii = t.matmul(psP[kt % 2][:, 0:64],
                              k1h[j][:, kt * 128:(kt + 1) * 128],
                              qsel[:], start=True, stop=True)
                S.inc("a1gs", ii, 1)
                S.wait(t, "a1gx", a1x0 + kt + 1)
                ii = t.matmul(psP[2][0:64, 0:129],
                              Pg[kt % 2][:], vt1[j][:, kt * 129:(kt + 1) * 129],
                              start=(kt == 0), stop=(kt == NKT - 1))
                S.inc("a1gc", ii, 1)

        a1s0 = S.tok("a1gs") - NKT

        @b.block.scalar
        def _(sc, h=h, a1s0=a1s0, a1c0=a1c0):
            for kt in range(NKT):
                S.wait(sc, "a1gs", a1s0 + kt + 1)
                if kt >= 2:
                    S.wait(sc, "a1gc", a1c0 + kt - 1)
                ii = sc.activation(out=Pg[kt % 2][:], in_=psP[kt % 2][:, 0:64],
                                   func=AF.Exp, bias=b.biasC1[:, h:h + 1],
                                   scale=SCL1)
                S.inc("a1gx", ii, 1)

        @b.block.vector
        def _(v, h=h):
            S.wait(v, "a1gc", None)
            st = _chain(S, v)
            st(v.tensor_copy(out=cgl[:], in_=psP[2][0:64, 0:129]))
            st(v.reciprocal(out=cgl[:, 128:129], in_=cgl[:, 128:129]))
            st(v.tensor_scalar(out=ctxN[:, 0:128], in0=cgl[:, 0:128],
                               scalar1=cgl[:, 128:129], scalar2=None, op0=ALU.mult))
            S.bump(v, "a1cn", 1)

        loc = []
        for qt in range(NKT):
            ktts = [k_ for k_ in (qt - 1, qt, qt + 1) if 0 <= k_ < NKT]
            loc.append((qt, ktts))
        a1lx0 = S.tok("a1lx")
        a1ls0 = S.tok("a1ls")
        a1lc0 = S.tok("a1lc")
        nen = 0
        entl = []
        for (qt, ktts) in loc:
            for ktt in ktts:
                entl.append((qt, ktt))
                nen += 1

        @b.block.tensor
        def _(t, h=h, j=j, loc=loc, a1lx0=a1lx0, entl=entl):
            S.wait(t, "a1cn", h + 1)
            ne = 0
            ng = 0
            for (qt, ktts) in loc:
                for ktt in ktts:
                    if ne >= 2:
                        S.wait(t, "a1lx", a1lx0 + ne - 1)
                    ii = t.matmul(psP[ne % 2][:, 0:128],
                                  k1h[j][:, ktt * 128:(ktt + 1) * 128],
                                  q1h[j][:, qt * 128:(qt + 1) * 128],
                                  start=True, stop=True)
                    S.inc("a1ls", ii, 1)
                    S.wait(t, "a1lx", a1lx0 + ne + 1)
                    ii = t.matmul(psP[2][:, 0:128],
                                  vt1[j][:, ktt * 129:ktt * 129 + 128],
                                  Pl[ne % 3][:],
                                  start=(ktt == ktts[0]), stop=(ktt == ktts[-1]))
                    S.inc("a1lc", ii, 1)
                    ii = t.matmul(psP[3][0:1, 0:128],
                                  vt1[j][:, ktt * 129 + 128:ktt * 129 + 129],
                                  Pl[ne % 3][:],
                                  start=(ktt == ktts[0]), stop=(ktt == ktts[-1]))
                    S.inc("a1lc", ii, 1)
                    ne += 1
                S.wait(t, "a1sb", h + 1)
                if h * NKT + ng >= 1:
                    S.wait(t, "a1ee", h * NKT + ng)
                ii = t.matmul(psP[5][:, 0:128], ctxN[:, 0:128],
                              ssel[j][:, qt * 128:(qt + 1) * 128],
                              start=True, stop=True)
                S.inc("a1sc", ii, 1)
                S.wait(t, "a1rx", h * NKT + ng + 1)
                ii = t.matmul(psP[6][:, 0:128], b.onesr_f[:], lrow[ng % 2][:],
                              start=True, stop=True)
                S.inc("a1rb", ii, 1)
                ng += 1

        @b.block.scalar
        def _(sc, h=h, entl=entl, a1ls0=a1ls0, a1lc0=a1lc0):
            for ne, (qt, ktt) in enumerate(entl):
                S.wait(sc, "a1ls", a1ls0 + ne + 1)
                if ne >= 3:
                    S.wait(sc, "a1lc", a1lc0 + 2 * (ne - 2))
                if ktt < qt:
                    dst = Pl[ne % 3][64:128, :]
                    srcp = psP[ne % 2][64:128, 0:128]
                    bia = b.biasC1[64:128, h:h + 1]
                elif ktt > qt:
                    dst = Pl[ne % 3][0:64, :]
                    srcp = psP[ne % 2][0:64, 0:128]
                    bia = b.biasC1[0:64, h:h + 1]
                else:
                    dst = Pl[ne % 3][:]
                    srcp = psP[ne % 2][:, 0:128]
                    bia = b.biasC1[:, h:h + 1]
                ii = sc.activation(out=dst, in_=srcp, func=AF.Exp,
                                   bias=bia, scale=SCL1)
                S.inc("a1lx", ii, 1)

        @b.block.vector
        def _(v, h=h, loc=loc, a1lc0=a1lc0):
            ng = 0
            for (qt, ktts) in loc:
                last = a1lc0 + 2 * sum(len(k_) for (q_, k_) in loc[:loc.index((qt, ktts)) + 1])
                S.wait(v, "a1lc", last)
                if h * NKT + ng >= 2:
                    S.wait(v, f"a1cw{ng % 2}", ((h * NKT + ng) // 2) * 16)
                st = _chain(S, v)
                st(v.tensor_copy(out=cll[ng % 2][:], in_=psP[2][:, 0:128]))
                st(v.tensor_copy(out=lrow[ng % 2][:], in_=psP[3][0:1, 0:128]))
                S.bump(v, "a1le", 1)
                st(v.reciprocal(out=lrow[ng % 2][:], in_=lrow[ng % 2][:]))
                st(v.tensor_tensor(out=lrow[ng % 2][:], in0=lrow[ng % 2][:],
                                   in1=invr[h % 2][:, qt * 128:(qt + 1) * 128],
                                   op=ALU.mult))
                S.bump(v, "a1rx", 1)
                S.wait(v, "a1rb", h * NKT + ng + 1)
                S.wait(v, "a1sc", h * NKT + ng + 1)
                st(v.tensor_tensor(out=cll[ng % 2][:], in0=cll[ng % 2][:],
                                   in1=psP[6][:, 0:128], op=ALU.mult))
                st(v.tensor_tensor(out=cb1[ng % 2][:], in0=cll[ng % 2][:],
                                   in1=psP[5][:, 0:128], op=ALU.add))
                S.bump(v, "a1ee", 1)
                ng += 1

        @b.block.vector
        def _(v, h=h):
            S.wait(v, "a1ee", (h + 1) * NKT)
            S.bump(v, "a1ce", 1)
        head_end.append(S.tok("a1ce"))
    b.c1w_tot = [S.tok("a1cw0"), S.tok("a1cw1")]


def _final(b):
    S, c = b.S, b.cfg
    D, NT, T = c["D"], c["NT"], c["T"]
    KD, NTT = c["KD"], c["NTT"]

    cu = b.CUR("RA")
    x1t = [cu(F32, 128, D) for j in range(2)]
    y1t = [cu(F32, 128, D) for j in range(2)]
    ctm = [cu(BF16, 128, D) for j in range(2)]
    ctf = cu(F32, 128, D)
    ot = [cu(F32, 128, D) for j in range(2)]
    assert cu.pos[0] <= b.reg["RA"]["size"], cu.pos[0]

    @b.block.gpsimd
    def _(g):
        S.wait(g, "cc", b.agc1cc)
        for side in b.sswitch(g, "g"):
            co = side * NT
            for kc in range(KD):
                ap = bass.AP(b.ag1c_out.tensor, co + kc * 128 * T,
                             [[T, 128], [1, NT]])
                ii = g.dma_start(out=b.ctx1m_dr[kc * 128:(kc + 1) * 128, :],
                                 in_=ap)
                S.inc("fcm", ii, 16)
    fcm_tot = S.tok("fcm")

    @b.block.sync
    def _(y):
        for jj in range(2):
            S.wait(y, f"f1ys{jj}", b.f1ys_tot[jj])
        S.wait(y, "fcm", fcm_tot)
        for tt in range(NTT):
            j = tt % 2
            if tt >= 2:
                S.wait(y, "fv", 2 * (tt - 1))
            ii = y.dma_start(out=x1t[j][:], in_=b.x1_sp[tt * 128:(tt + 1) * 128, :])
            S.inc(f"fl{j}", ii, 16)
            ii = y.dma_start(out=y1t[j][:], in_=b.y1_sp[tt * 128:(tt + 1) * 128, :])
            S.inc(f"fl{j}", ii, 16)
            for k in range(KD):
                ii = y.dma_start_transpose(ctm[j][:, k * 128:(k + 1) * 128],
                                           b.ctx1m_dr[k * 128:(k + 1) * 128,
                                                      tt * 128:(tt + 1) * 128])
                S.inc(f"fl{j}", ii, 16)

    fv0 = S.tok("fv")

    @b.block.vector
    def _(v):
        for tt in range(NTT):
            j = tt % 2
            S.wait(v, f"fl{j}", (tt // 2 + 1) * 16 * (2 + KD))
            if tt >= 2:
                S.wait(v, f"fo{j}", (tt // 2) * 16)
            st = _chain(S, v)
            st(v.tensor_copy(out=ctf[:], in_=ctm[j][:]))
            st(v.tensor_tensor(out=ot[j][:], in0=x1t[j][:], in1=y1t[j][:],
                               op=ALU.add))
            st(v.tensor_tensor(out=ot[j][:], in0=ot[j][:], in1=ctf[:], op=ALU.add))
            S.bump(v, "fv", 2)

    @b.block.scalar
    def _(sc):
        for tt in range(NTT):
            j = tt % 2
            S.wait(sc, "fv", fv0 + 2 * (tt + 1))
            ii = sc.dma_start(out=b.out[tt * 128:(tt + 1) * 128, :], in_=ot[j][:])
            S.inc(f"fo{j}", ii, 16)
        S.wait(sc, "fo0", None)
        S.wait(sc, "fo1", None)



def _finish(b):
    S, c = b.S, b.cfg

    outw = S.m.get("outw")

    @b.block.sync
    def _(y):
        for h, v in S.phys_state():
            if outw is not None and h is outw[0][0]:
                continue
            y.wait_ge(h, v)

    if b.upto != "all":
        z_ = b.sb("zot", [128, c["D"]], F32)

        @b.block.vector
        def _(v):
            ii = v.memset(z_[:], 0.0)
            b.t_z = S.inc("v", ii, 1)

        @b.block.sync
        def _(y):
            S.wait(y, "v", b.t_z)
            for tt in range(c["NTT"]):
                ii = y.dma_start(out=b.out[tt * 128:(tt + 1) * 128, :], in_=z_[:])
                S.inc("outw", ii, 16)
            S.wait(y, "outw", None)


def make_consts(cfg):
    T, TS = cfg["T"], cfg["TS"]
    TRIV = TS // 128
    IOTA = np.tile(np.arange(512, dtype=np.float32), (64, 1))
    ID128 = np.eye(128, dtype=np.float32)

    tri = np.zeros((TRIV * 128, TS), np.float32)
    for v in range(TRIV):
        for kj in range(128):
            tri[v * 128 + kj, :] = (np.arange(TS) >= (128 * v + kj)).astype(np.float32)
    TRI = tri.astype(ml_dtypes.bfloat16)
    R8 = np.zeros((8, 64), np.float32)
    for e in range(8):
        R8[e, e * 8:(e + 1) * 8] = 1.0
    return dict(IOTA=IOTA, ID128=ID128, TRI=TRI,
                R8=R8.astype(ml_dtypes.bfloat16))


def make_inmaps(cfg, A):
    c = cfg
    NT, D = c["NT"], c["D"]
    NHm, NH1m, HD = c["NHm"], c["NH1m"], c["HD"]
    consts = make_consts(c)
    base = dict(
        wk=A["L0_attn_wk"], bk=A["L0_attn_bk"], wq=A["L0_attn_wq"], bq=A["L0_attn_bq"],
        wv=A["L0_attn_wv"], bv=A["L0_attn_bv"], wkl=A["L0_attn_wkl"], bkl=A["L0_attn_bkl"],
        wql=A["L0_attn_wql"], bql=A["L0_attn_bql"], wvl=A["L0_attn_wvl"], bvl=A["L0_attn_bvl"],
        wo=A["L0_attn_wo"], bo=A["L0_attn_bo"],
        w1=A["L0_ffn_w1"], b1=A["L0_ffn_b1"], w2=A["L0_ffn_w2"], b2=A["L0_ffn_b2"],
        w2T=np.ascontiguousarray(np.transpose(A["L0_ffn_w2"])),
        Wq1=A["L1_attn_Wq"], bq1=A["L1_attn_bq"], Wk1=A["L1_attn_Wk"], bk1=A["L1_attn_bk"],
        Wv1=A["L1_attn_Wv"], bv1=A["L1_attn_bv"],
        WbT=np.ascontiguousarray(np.transpose(A["L1_ffn_W_base"])),
        bbase=A["L1_ffn_b_base"],
        WrT=np.ascontiguousarray(np.transpose(A["L1_ffn_W_router"])),
        AT=np.ascontiguousarray(np.asarray(A["L1_ffn_A_w"]).reshape(64, D).T),
        Bcat=np.ascontiguousarray(np.asarray(A["L1_ffn_B_w"]).transpose(0, 2, 1).reshape(64, D)),
        **consts,
    )
    base = {k: np.ascontiguousarray(v) for k, v in base.items()}
    x = A["L0_attn_x"]
    maps = []
    for core in range(c["NCORE"]):
        bidx, s = core // 2, core % 2
        m = dict(base)
        m["x0"] = np.ascontiguousarray(x[bidx, s * NT:(s + 1) * NT, :])
        m["soff"] = np.array([[s, s * NHm * HD * NT, s * NH1m * 128 * NT,
                               s * NT, 0, 0, 0, 0]], np.int32)
        maps.append(m)
    return maps


def kernel(**inputs):
    cfg = make_cfg(small=False)
    nc = build_core(cfg, debug=0, upto="all")
    in_maps = make_inmaps(cfg, inputs)
    res = run_bass_kernel_spmd(nc, in_maps, core_ids=list(range(cfg["NCORE"])))
    NT, D, B = cfg["NT"], cfg["D"], cfg["B"]
    out = np.empty((B, 2 * NT, D), np.float32)
    for core in range(cfg["NCORE"]):
        bidx, s = core // 2, core % 2
        out[bidx, s * NT:(s + 1) * NT, :] = res.results[core]["out"]
    return out


# version 08
# speedup vs baseline: 1.0947x; 1.0947x over previous
import math
import numpy as np
import ml_dtypes
from contextlib import ExitStack

import concourse.bass as bass
import concourse.mybir as mybir
from concourse.bass_utils import run_bass_kernel_spmd

F32 = mybir.dt.float32
BF16 = mybir.dt.bfloat16
I32 = mybir.dt.int32
AF = mybir.ActivationFunctionType
ALU = mybir.AluOpType
AX = mybir.AxisListType

MAGIC = 12582912.0
LNEPS = 1e-5
BIG = 1.0e9


def make_cfg(small=False):
    if small == "med":
        D, DL, I, NT, B = 1024, 512, 2048, 1024, 1
    elif small:
        D, DL, I, NT, B = 256, 128, 512, 512, 1
    else:
        D, DL, I, NT, B = 2048, 1024, 8192, 2048, 4
    c = dict(D=D, DL=DL, I=I, NT=NT, T=2 * NT, B=B,
             NH=DL // 64, HD=64, NH1=D // 128, TOPK=64, TS=512)
    c["KD"] = D // 128
    c["KI"] = I // 128
    c["KL"] = DL // 128
    c["NTT"] = NT // 128
    c["NS"] = NT // c["TS"]
    c["TQ"] = c["T"] // c["TS"]
    c["NKT"] = c["T"] // 128
    c["NHm"] = c["NH"] // 2
    c["NH1m"] = c["NH1"] // 2
    c["NCORE"] = 2 * B
    return c


def _is_sw_inc(instr):
    try:
        ins = instr.ins
    except Exception:
        return False
    cn = type(ins).__name__
    return ins.engine == mybir.EngineType.Pool and (
        "DMA" in cn or "Collective" in cn or "TensorCopy" in cn)


class Sems:

    SW_NAMES = {'L0pw0', 'L0pw1', 'L1pw0', 'L1pw1', 'SWN', 'f1w', 'f1wb', 'fcm', 'm1w0', 'm1w1', 'm2qw0', 'm2qw1', 'm2wo', 'w2l'}

    def __init__(self, nc, ctx):
        self.nc, self.ctx, self.m = nc, ctx, {}
        self.free, self.dead = [], set()
        self.serial = 0

    def h(self, name):
        if name in self.dead:
            raise KeyError(f"semaphore name {name!r} was retired")
        if name not in self.m:
            if name in self.SW_NAMES or not self.free:
                self.serial += 1
                ph = [self.ctx.enter_context(
                    self.nc.semaphore(f"{name}_{self.serial}")), 0, 0]
            else:
                ph = self.free.pop()
            self.m[name] = [ph, 0]
        return self.m[name]

    def inc(self, name, instr, n=1):
        e = self.h(name)
        ph = e[0]
        if _is_sw_inc(instr):
            if ph[2] == 1 or (ph[2] == 0 and ph[1] - e[1] != 0):
                raise RuntimeError(f"sem {name!r} needs a fresh SWDGE phys "
                                   f"sem: add to SW_NAMES")
            ph[2] = 2
        else:
            if ph[2] == 2:
                raise RuntimeError(f"sem {name!r} mixes SWDGE and other "
                                   f"updaters")
            ph[2] = 1
        instr.then_inc(ph[0], n)
        ph[1] += n
        e[1] += n
        return e[1]

    def tok(self, name):
        return self.h(name)[1]

    def wait(self, eng, name, val=None):
        e = self.h(name)
        lv = e[1] if val is None else val
        eng.wait_ge(e[0][0], e[0][1] - e[1] + lv)

    def bump(self, eng, name, n=1):
        e = self.h(name)
        if e[0][2] == 2:
            raise RuntimeError(f"sem {name!r} mixes SWDGE and engine bumps")
        e[0][2] = 1
        eng.sem_inc(e[0][0], n)
        e[0][1] += n
        e[1] += n
        return e[1]

    def revive(self, *names):
        for nm in names:
            self.dead.discard(nm)

    def retire(self, *names):
        for nm in names:
            if nm in self.m:
                ph = self.m.pop(nm)[0]
                if ph[2] != 2:
                    self.free.append(ph)
                self.dead.add(nm)

    def save(self):
        return {n: (e[0], e[1], e[0][1]) for n, e in self.m.items()}

    def delta(self, snap):
        d = {}
        for n, e in self.m.items():
            base = snap[n][1] if n in snap else 0
            if e[1] != base:
                d[n] = e[1] - base
        return d

    def restore(self, snap):
        for n, e in list(self.m.items()):
            if n in snap:
                _, loc, phv = snap[n]
                e[0][1] = phv
                e[1] = loc
            else:
                e[0][1] -= e[1]
                e[1] = 0

    def phys_state(self):
        seen, out = set(), []
        for e in self.m.values():
            if id(e[0]) not in seen:
                seen.add(id(e[0]))
                out.append(e[0])
        for ph in self.free:
            if id(ph) not in seen:
                seen.add(id(ph))
                out.append(ph)
        return [(ph[0], ph[1]) for ph in out]


def _chain(S, eng, name="vch"):
    def step(instr):
        t = S.inc(name, instr, 1)
        S.wait(eng, name, t)
        return t
    return step


class Bld:
    def mark_gate(self):
        cc = self.S.m.get("cc")
        gd = [(h, v) for (h, v) in self.S.phys_state()
              if cc is None or h is not cc[0][0]]
        self.pending_gates.append(gd)

    def gate(self, eng):
        for gd in self.pending_gates[-1:]:
            for h, v in gd:
                eng.wait_ge(h, v)

    def sswitch(self, eng, ename):
        snap = d0 = None
        for side in self.nc.Switch(engines=[eng], index=[self.sreg[ename]], n=2):
            if side == 0:
                snap = self.S.save()
            else:
                d0 = self.S.delta(snap)
                self.S.restore(snap)
            yield side
        d1 = self.S.delta(snap)
        assert d0 == d1, ("switch bodies differ in sem deltas", d0, d1)


def build_core(cfg, debug=0, upto="all"):
    nc = bass.Bass("TRN2", target_bir_lowering=False, num_devices=cfg["NCORE"])
    b = Bld()
    b.cfg, b.debug, b.upto = cfg, debug, upto
    c = cfg
    D, DL, I, NT, T, TS = c["D"], c["DL"], c["I"], c["NT"], c["T"], c["TS"]
    NHm, NH1m, HD = c["NHm"], c["NH1m"], c["HD"]
    MYH0, MYH1 = NHm * HD, NH1m * 128
    TRIV = TS // 128

    inp = lambda n, sh, dt=F32: nc.dram_tensor(n, sh, dt, kind="ExternalInput").ap()
    DBG = {"xn32_sp", "h_sp", "x1_sp", "y1_sp", "q2_dr", "ctx1m_dr", "qw2T_dr",
           "nrm_dr", "idx_dr", "inv_dr", "dbg_dr"} if debug else set()
    dram = lambda n, sh, dt: nc.dram_tensor(
        n, sh, dt, kind="ExternalOutput" if n in DBG else "Internal").ap()

    b.x0 = inp("x0", [NT, D])
    for n_, sh in [("wk", [D, D]), ("wq", [D, D]), ("wv", [D, D]),
                   ("wkl", [D, DL]), ("wql", [D, DL]), ("wvl", [D, DL]),
                   ("wo", [DL, D]), ("w1", [I, D]), ("w2", [D, I]),
                   ("Wq1", [D, D]), ("Wk1", [D, D]), ("Wv1", [D, D]),
                   ("w2T", [I, D]), ("WbT", [D, D]), ("WrT", [D, 8]),
                   ("AT", [D, 64]), ("Bcat", [64, D])]:
        setattr(b, n_, inp(n_, sh))
    for n_ in ["bk", "bq", "bv", "bo", "b2", "bq1", "bk1", "bv1", "bbase"]:
        setattr(b, n_, inp(n_, [D]))
    for n_ in ["bkl", "bql", "bvl"]:
        setattr(b, n_, inp(n_, [DL]))
    b.b1 = inp("b1", [I])
    b.soff = inp("soff", [1, 8], I32)
    b.IOTA = inp("IOTA", [64, 512])
    b.ID128 = inp("ID128", [128, 128])
    b.TRI = inp("TRI", [TRIV * 128, TS], BF16)
    b.R8 = inp("R8", [8, 64], BF16)

    b.out = nc.dram_tensor("out", [NT, D], F32, kind="ExternalOutput").ap()
    b.dbg = {}

    b.xn32_sp = dram("xn32_sp", [NT, D], F32)
    b.h_sp = dram("h_sp", [I, NT], F32)
    b.qw2T_dr = dram("qw2T_dr", [I, D], BF16)
    b.q2_dr = dram("q2_dr", [I, NT], BF16)
    b.ctx1m_dr = dram("ctx1m_dr", [D, NT], BF16)
    b.x1_sp = dram("x1_sp", [NT, D], F32)
    b.y1_sp = dram("y1_sp", [NT, D], F32)
    b.row_dr = dram("row_dr", [8192], F32)
    b.dbg_dr = dram("dbg_dr", [4 * 128, NT], F32)
    b.nrm_dr = dram("nrm_dr", [max(NH1m, 1), T], F32)
    b.inv_dr = dram("inv_dr", [max(NH1m, 1), T], F32)
    b.idx_dr = dram("idx_dr", [max(NH1m, 1), 64], F32)
    b.ag0_in = dram("ag0_in", [3 * DL, NT], BF16)
    b.ag0_out = dram("ag0_out", [2 * 3 * DL, NT], BF16)
    b.ag0c_in = dram("ag0c_in", [MYH0, T], BF16)
    b.ag0c_out = dram("ag0c_out", [2 * MYH0, T], BF16)
    b.ag1_in = dram("ag1_in", [3 * D, NT], BF16)
    b.ag1_out = dram("ag1_out", [2 * 3 * D, NT], BF16)
    b.ag1c_in = dram("ag1c_in", [MYH1, T], BF16)
    b.ag1c_out = dram("ag1c_out", [2 * MYH1, T], BF16)
    b.ar_in = dram("ar_in", [1, 128], F32)
    b.ar_out = dram("ar_out", [1, 128], F32)
    b.PAIRS = [[2 * i, 2 * i + 1] for i in range(c["NCORE"] // 2)]
    b.ALLG = [list(range(c["NCORE"]))]

    ctx = ExitStack()
    b.ctx, b.nc = ctx, nc
    S = Sems(nc, ctx)
    b.S = S
    b.sb = lambda n, sh, dt: ctx.enter_context(nc.sbuf_tensor(n, sh, dt))
    b.ps = lambda n, sh, dt=F32: ctx.enter_context(nc.psum_tensor(n, sh, dt))
    PS = nc.SBUF_PARTITION_SIZE_BYTES - 64
    RSZ = {"RA": 67584, "RB": 65536, "RC": 57344, "RW": 12288}
    MB = PS - sum(RSZ.values())
    base = MB
    b.reg = {}
    for nm in ["RW", "RC", "RB", "RA"]:
        sz = RSZ[nm]
        b.reg[nm] = {
            "f": nc.alloc_sbuf_tensor_at(f"{nm}_f", [128, sz // 4], F32, offset=base),
            "b": nc.alloc_sbuf_tensor_at(f"{nm}_b", [128, sz // 2], BF16, offset=base),
            "i": nc.alloc_sbuf_tensor_at(f"{nm}_i", [128, sz // 4], I32, offset=base),
            "size": sz,
        }
        base += sz
    b.auto_limit = MB

    def RV(region, dt, off, p, cols, p0=0):
        r = b.reg[region]
        esz = 2 if dt == BF16 else 4
        t = r["b"] if dt == BF16 else (r["i"] if dt == I32 else r["f"])
        assert off % esz == 0 and off + cols * esz <= r["size"], (region, off, cols)
        return t[p0:p0 + p, off // esz: off // esz + cols]
    b.RV = RV

    def CUR(region, start=0):
        pos = [start]
        def take(dt, p, cols, p0=0, align=64):
            esz = 2 if dt == BF16 else 4
            off = (pos[0] + align - 1) // align * align
            pos[0] = off + cols * esz
            take.last_off = off
            return RV(region, dt, off, p, cols, p0)
        take.pos = pos
        return take
    b.CUR = CUR
    b.sreg = {}
    b.pending_gates = []

    KEEP = {'L0pe', 'L1pe', 'acc2', 'aee', 'am', 'anv', 'ax', 'cc', 'cst', 'f1ys0', 'f1ys1', 'ln1v', 'm1e', 'm1hx0', 'm1hx1', 'm1hx2', 'm1hx3', 'm2eB', 'm2ti', 'p0sp0', 'p0sp1', 's', 'sl_cb', 'sl_iw', 'sl_v2', 't', 'v', 'vch', 'w2bc', 'w2st0', 'w2st1'}
    phase_names = []

    def ph(fn, *a, **k):
        before = set(S.m) | S.dead
        fn(*a, **k)
        phase_names.append((set(S.m) - before) - KEEP)
        if len(phase_names) >= 2:
            S.retire(*phase_names[-2])
            phase_names[-2] = set()

    with ctx:
        _persistent(b)
        b.block = ctx.enter_context(nc.Block())
        ph(_preload, b)
        if upto == "pre":
            _finish(b)
            return nc
        ph(_p0_ln, b)
        if upto == "p0":
            _finish(b)
            return nc
        ph(_ar1, b)
        if upto == "ar":
            _finish(b)
            return nc
        ph(_proj, b, layer=0)
        if upto == "proj":
            _finish(b)
            return nc
        ph(_ag_qkv, b, layer=0)
        b.mark_gate()
        ph(_quant_x, b)
        if upto == "p1":
            _finish(b)
            return nc
        b.mark_gate()
        ph(_m1, b)
        if upto in ("m1", "m1a", "m1b", "m1c", "m1d", "m1e", "m1f"):
            _finish(b)
            return nc
        b.mark_gate()
        ph(_w2_quant, b)
        if upto == "w2":
            _finish(b)
            return nc
        ph(_attn0, b)
        ph(_agc, b, layer=0)
        if upto == "at0":
            _finish(b)
            return nc
        b.mark_gate()
        ph(_m2_x1, b)
        if upto in ("q2", "q2A", "q2B", "q2C", "q2ng", "q2v2", "l0"):
            _finish(b)
            return nc
        b.mark_gate()
        ph(_proj, b, layer=1)
        ph(_ag_qkv, b, layer=1)
        ph(_l1_select, b)
        b.mark_gate()
        ph(_ffn1, b)
        b.mark_gate()
        ph(_attn1, b)
        ph(_agc, b, layer=1)
        b.mark_gate()
        ph(_final, b)
        _finish(b)
        assert nc.sbuf_base <= MB, (nc.sbuf_base, MB)
    return nc


def _persistent(b):
    c, sb = b.cfg, b.sb
    D, T, TS = c["D"], c["T"], c["TS"]
    KD, KI, KL, NHm, NH1m = c["KD"], c["KI"], c["KL"], c["NHm"], c["NH1m"]
    TRIV = TS // 128
    b.ones_f = sb("ones_f", [128, 1], F32)
    b.onesr_f = sb("onesr_f", [1, 128], F32)
    b.ones_b = sb("ones_b", [128, 1], BF16)
    b.biasK = sb("biasK", [128, 3 * KD], F32)
    b.biasL = sb("biasL", [128, 3 * KL], F32)
    b.bias1 = sb("bias1", [128, KI], F32)
    b.biasQ1 = sb("biasQ1", [128, 3 * KD], F32)
    b.biasC1 = sb("biasC1", [128, max(NH1m, 1)], F32)
    b.s1c = sb("s1c", [128, KI], F32)
    b.absacc = sb("absacc", [128, 1], F32)
    b.habs = sb("habs", [128, 1], F32)
    b.sm = sb("sm", [128, 24], F32)
    b.rsx_b = sb("rsx_b", [128, 1], F32)
    b.rs2_b = sb("rs2_b", [128, 1], F32)
    b.idT = sb("idT", [128, 128], F32)
    b.tri = sb("tri", [128, TRIV * TS], BF16)
    b.r8 = sb("r8", [8, 64], BF16)
    b.iota = sb("iota", [64, 512], F32)
    b.soffs = sb("soffs", [1, 8], I32)
    b.psP = [b.ps(f"psP{i}", [128, 512]) for i in range(8)]


def _preload(b):
    S, c = b.S, b.cfg
    KD, KL, KI, TS = c["KD"], c["KL"], c["KI"], c["TS"]
    TRIV = TS // 128

    @b.block.sync
    def _(y):
        for bt, K, srcs in [(b.biasK, KD, (b.bq, b.bk, b.bv)),
                            (b.biasL, KL, (b.bql, b.bkl, b.bvl)),
                            (b.biasQ1, KD, (b.bq1, b.bk1, b.bv1))]:
            for j, src in enumerate(srcs):
                r = src.rearrange("(c p) -> c p", p=128)
                for cc in range(K):
                    ii = y.dma_start(out=bt[:, j * K + cc:j * K + cc + 1], in_=r[cc])
                    S.inc("cst", ii, 16)
        r = b.b1.rearrange("(c p) -> c p", p=128)
        for cc in range(KI):
            ii = y.dma_start(out=b.bias1[:, cc:cc + 1], in_=r[cc])
            S.inc("cst", ii, 16)
        for dst, src in [(b.idT, b.ID128), (b.r8, b.R8), (b.iota, b.IOTA),
                         (b.soffs, b.soff)]:
            ii = y.dma_start(out=dst[:], in_=src)
            S.inc("cst", ii, 16)
        for v in range(TRIV):
            ii = y.dma_start(out=b.tri[:, v * TS:(v + 1) * TS],
                             in_=b.TRI[v * 128:(v + 1) * 128, :])
            S.inc("cst", ii, 16)
    b.cst_tot = S.tok("cst")

    def loadoffs(eng, ename):
        S.wait(eng, "cst", b.cst_tot)
        b.sreg[ename] = eng.value_load(b.soffs[0:1, 0:1])

    @b.block.sync
    def _(y):
        loadoffs(y, "y")

    @b.block.gpsimd
    def _(g):
        loadoffs(g, "g")

    @b.block.vector
    def _(v):
        v.memset(b.ones_f[:], 1.0)
        v.memset(b.onesr_f[:], 1.0)
        v.memset(b.ones_b[:], 1.0)
        v.memset(b.absacc[:], 0.0)
        ii = v.memset(b.habs[:], 0.0)
        b.t_const = S.inc("v", ii, 1)


def _p0_ln(b):
    S, c = b.S, b.cfg
    D, NT, NTT, KD = c["D"], c["NT"], c["NTT"], c["KD"]
    RV = b.RV

    b.xn_fm = [RV("RA", BF16, k * 2 * NT, 128, NT) for k in range(KD)]
    x0t = [RV("RC", F32, j * 4 * D, 128, D) for j in range(2)]
    sqt = [RV("RC", F32, 8 * D + j * 4 * D, 128, D) for j in range(2)]
    xnt = [RV("RC", F32, 16 * D + j * 4 * D, 128, D) for j in range(2)]
    xnb = [RV("RC", BF16, 24 * D + j * 2 * D, 128, D) for j in range(2)]
    stat = b.sb("stat", [128, 8], F32)

    s_base = S.tok("s")
    vx_base = S.tok("vx")
    v0p = S.tok("v")
    vt = lambda tt: v0p + tt + 1

    def p0_spill(y, tt):
        j = tt % 2
        S.wait(y, "v", vt(tt))
        ii = y.dma_start(out=b.xn32_sp[tt * 128:(tt + 1) * 128, :], in_=xnt[j][:])
        S.inc(f"p0sp{j}", ii, 16)
        for k in range(KD):
            ii = y.dma_start_transpose(b.xn_fm[k][:, tt * 128:(tt + 1) * 128],
                                       xnb[j][:, k * 128:(k + 1) * 128])
            S.inc(f"p0sp{j}", ii, 16)

    @b.block.sync
    def _(y):
        for tt in range(NTT):
            j = tt % 2
            if tt >= 2:
                S.wait(y, "v", vt(tt - 2))
            ii = y.dma_start(out=x0t[j][:], in_=b.x0[tt * 128:(tt + 1) * 128, :])
            S.inc(f"p0ld{j}", ii, 16)
            if tt >= 1:
                p0_spill(y, tt - 1)
        p0_spill(y, NTT - 1)

    @b.block.vector
    def _(v):
        S.wait(v, "v", b.t_const)
        for tt in range(NTT):
            j = tt % 2
            st = _chain(S, v)
            S.wait(v, f"p0ld{j}", (tt // 2 + 1) * 16)
            S.wait(v, "s", s_base + 2 * tt + 1)
            if tt >= 2:
                S.wait(v, f"p0sp{j}", (tt // 2) * 16 * (1 + KD))
            v.reduce_sum(out=stat[:, 0:1], in_=x0t[j][:], axis=AX.X)
            st(v.reduce_sum(out=stat[:, 1:2], in_=sqt[j][:], axis=AX.X))
            st(v.tensor_scalar(out=stat[:, 2:3], in0=stat[:, 0:1], scalar1=-1.0 / D,
                               scalar2=None, op0=ALU.mult))
            st(v.tensor_tensor(out=stat[:, 4:5], in0=stat[:, 2:3], in1=stat[:, 2:3],
                               op=ALU.mult))
            st(v.tensor_scalar(out=stat[:, 3:4], in0=stat[:, 1:2], scalar1=1.0 / D,
                               scalar2=None, op0=ALU.mult))
            st(v.tensor_tensor(out=stat[:, 3:4], in0=stat[:, 3:4], in1=stat[:, 4:5],
                               op=ALU.subtract))
            st(v.tensor_scalar(out=stat[:, 3:4], in0=stat[:, 3:4], scalar1=LNEPS,
                               scalar2=None, op0=ALU.add))
            S.bump(v, "vx", 1)
            S.wait(v, "s", s_base + 2 * tt + 2)
            st(v.reciprocal(out=stat[:, 6:7], in_=stat[:, 5:6]))
            st(v.tensor_tensor(out=stat[:, 7:8], in0=stat[:, 6:7], in1=stat[:, 6:7],
                               op=ALU.mult))
            st(v.tensor_tensor(out=stat[:, 7:8], in0=stat[:, 7:8], in1=stat[:, 3:4],
                               op=ALU.mult))
            st(v.tensor_scalar(out=stat[:, 7:8], in0=stat[:, 7:8], scalar1=-0.5,
                               scalar2=1.5, op0=ALU.mult, op1=ALU.add))
            st(v.tensor_tensor(out=stat[:, 6:7], in0=stat[:, 6:7], in1=stat[:, 7:8],
                               op=ALU.mult))
            st(v.tensor_scalar(out=xnt[j][:], in0=x0t[j][:], scalar1=stat[:, 2:3],
                               scalar2=stat[:, 6:7], op0=ALU.add, op1=ALU.mult))
            st(v.reduce_max(out=stat[:, 4:5], in_=xnt[j][:], axis=AX.X,
                            apply_absolute_value=True))
            st(v.tensor_tensor(out=b.absacc[:], in0=b.absacc[:], in1=stat[:, 4:5],
                               op=ALU.max))
            st(v.tensor_copy(out=xnb[j][:], in_=xnt[j][:]))
            assert S.bump(v, "v", 1) == vt(tt)

    @b.block.scalar
    def _(sc):
        for tt in range(NTT):
            j = tt % 2
            S.wait(sc, f"p0ld{j}", (tt // 2 + 1) * 16)
            if tt >= 1:
                S.wait(sc, "v", vt(tt - 1))
            ii = sc.activation(out=sqt[j][:], in_=x0t[j][:], func=AF.Square)
            S.inc("s", ii, 1)
            S.wait(sc, "vx", vx_base + tt + 1)
            ii = sc.activation(out=stat[:, 5:6], in_=stat[:, 3:4], func=AF.Sqrt)
            S.inc("s", ii, 1)

    b.p0sp_tot = [S.tok("p0sp0"), S.tok("p0sp1")]
    b.v_p0_last = vt(NTT - 1)


def _ar1(b):
    S, sb = b.S, b.sb
    arrow = sb("arrow", [1, 128], F32)
    b.arrow = arrow

    @b.block.sync
    def _(y):
        S.wait(y, "v", b.v_p0_last)
        ii = y.dma_start(out=b.row_dr[0:128], in_=b.absacc[:, 0])
        S.inc("ario", ii, 16)
        S.wait(y, "ario", None)
        ii = y.dma_start(out=arrow[:], in_=b.row_dr.rearrange("(a x) -> a x", a=64)[0])
        S.inc("ario", ii, 16)

    @b.block.vector
    def _(v):
        S.wait(v, "ario", None)
        st = _chain(S, v)
        st(v.reduce_max(out=b.sm[0:1, 0:1], in_=arrow[:], axis=AX.X))
        st(v.tensor_scalar(out=arrow[:], in0=arrow[:], scalar1=0.0,
                           scalar2=b.sm[0:1, 0:1], op0=ALU.mult, op1=ALU.add))
        b.t_armax = S.bump(v, "v", 1)

    @b.block.gpsimd
    def _(g):
        S.wait(g, "v", b.t_armax)
        S.wait(g, "p0sp0", None)
        S.wait(g, "p0sp1", None)
        ii = g.dma_start(out=b.ar_in, in_=arrow[:])
        S.inc("arw", ii, 16)
        S.wait(g, "arw", None)
        ii = g.collective_compute("AllReduce", ALU.max, replica_groups=b.ALLG,
                                  ins=[b.ar_in.opt()], outs=[b.ar_out.opt()])
        S.inc("cc", ii, 1)
        S.wait(g, "cc", None)
        ii = g.dma_start(out=arrow[:], in_=b.ar_out)
        S.inc("arq", ii, 16)

    @b.block.vector
    def _(v):
        S.wait(v, "arq", None)
        st = _chain(S, v)
        st(v.tensor_scalar(out=b.sm[0:1, 1:2], in0=arrow[:, 0:1], scalar1=1.0 / 127.0,
                           scalar2=1e-8, op0=ALU.mult, op1=ALU.max))
        st(v.reciprocal(out=b.sm[0:1, 2:3], in_=b.sm[0:1, 1:2]))
        b.t_sx = S.bump(v, "v", 1)

    @b.block.tensor
    def _(t):
        S.wait(t, "v", b.t_sx)
        ii = t.matmul(b.psP[7][:, 0:2], b.onesr_f[:], b.sm[0:1, 1:3],
                      start=True, stop=True)
        b.t_sxmm = S.inc("t", ii, 1)

    @b.block.vector
    def _(v):
        S.wait(v, "t", b.t_sxmm)
        st = _chain(S, v)
        st(v.tensor_copy(out=b.sm[:, 3:5], in_=b.psP[7][:, 0:2]))
        st(v.tensor_copy(out=b.rsx_b[:], in_=b.sm[:, 4:5]))
        b.t_sxb = S.bump(v, "v", 1)


def _quant_x(b):
    S, c, RV = b.S, b.cfg, b.RV
    D, NT, NTT, KD = c["D"], c["NT"], c["NTT"], c["KD"]

    b.qx_fm = [RV("RA", BF16, k * 2 * NT, 128, NT) for k in range(KD)]
    qmg = [RV("RB", F32, j * 4 * D, 128, D) for j in range(2)]
    qint = [RV("RB", I32, 8 * D + j * 4 * D, 128, D) for j in range(2)]
    qbf = [RV("RB", BF16, 16 * D + j * 2 * D, 128, D) for j in range(2)]

    sq_base = S.tok("s")
    vq0 = S.tok("v")
    tqv = lambda tt: vq0 + tt + 1

    def q_transp(y, tt):
        j = tt % 2
        S.wait(y, "v", tqv(tt))
        for k in range(KD):
            ii = y.dma_start_transpose(b.qx_fm[k][:, tt * 128:(tt + 1) * 128],
                                       qbf[j][:, k * 128:(k + 1) * 128])
            S.inc(f"qtp{j}", ii, 16)

    @b.block.sync
    def _(y):
        b.gate(y)
        for tt in range(NTT):
            j = tt % 2
            S.wait(y, f"p0sp{j}", (tt // 2 + 1) * 16 * (1 + KD))
            if tt >= 2:
                S.wait(y, "v", tqv(tt - 2))
            ii = y.dma_start(out=qmg[j][:], in_=b.xn32_sp[tt * 128:(tt + 1) * 128, :])
            S.inc(f"qld{j}", ii, 16)
            if tt >= 1:
                q_transp(y, tt - 1)
        q_transp(y, NTT - 1)

    @b.block.scalar
    def _(sc):
        b.gate(sc)
        S.wait(sc, "v", b.t_sxb)
        for tt in range(NTT):
            j = tt % 2
            S.wait(sc, f"qld{j}", (tt // 2 + 1) * 16)
            if tt >= 2:
                S.wait(sc, "v", tqv(tt - 2))
            ii = sc.activation(out=qmg[j][:], in_=qmg[j][:], func=AF.Copy,
                               bias=MAGIC, scale=b.rsx_b[:, 0:1])
            S.inc("s", ii, 1)

    @b.block.vector
    def _(v):
        b.gate(v)
        for tt in range(NTT):
            j = tt % 2
            S.wait(v, "s", sq_base + tt + 1)
            if tt >= 2:
                S.wait(v, f"qtp{j}", (tt // 2) * 16 * KD)
            st = _chain(S, v)
            st(v.tensor_scalar(out=qint[j][:], in0=qmg[j][:], scalar1=-MAGIC,
                               scalar2=None, op0=ALU.add))
            st(v.tensor_scalar(out=qbf[j][:], in0=qint[j][:], scalar1=0.0,
                               scalar2=None, op0=ALU.add))
            assert S.bump(v, "v", 1) == tqv(tt)

    b.qtp_tot = [S.tok("qtp0"), S.tok("qtp1")]


def _proj(b, layer):
    S, c = b.S, b.cfg
    RV = b.RV
    D, DL, NT, TS = c["D"], c["DL"], c["NT"], c["TS"]
    KD, KL, NS = c["KD"], c["KL"], c["NS"]
    L = layer
    if L == 0:
        KO = KL
        ag_in, src_fm = b.ag0_in, b.xn_fm
        W1s, W2s = [b.wq, b.wk, b.wv], [b.wql, b.wkl, b.wvl]
        biasL, two_stage, OROW = b.biasL, True, DL
    else:
        KO = KD
        ag_in, src_fm = b.ag1_in, b.xn1_fm
        W1s, W2s = [b.Wq1, b.Wk1, b.Wv1], None
        biasL, two_stage, OROW = b.biasQ1, False, D
    pfx = f"L{L}p"
    SPP = 2 if NS >= 2 else 1
    NSP = max(NS // SPP, 1)

    if L == 0:
        b.a_fm = [RV("RB", BF16, k * 2 * NT, 128, NT) for k in range(KD)]
        b.wcol = [RV("RW", BF16, j * 2 * D, 128, D) for j in range(2)]
        b.pbounce = [RV("RW", BF16, 4 * D + jb * 2 * TS, 128, TS) for jb in range(4)]
    a_fm, wcol, pbounce = b.a_fm, b.wcol, b.pbounce
    psP = b.psP

    NBLK = KD + (KO if two_stage else 0)
    t0 = S.tok("t")
    e0 = S.tok(pfx + "e")

    @b.block.gpsimd
    def _(g):
        if L == 1:
            b.gate(g)
        n = 0
        for pj in range(3):
            for blk in range(NBLK):
                j = n % 2
                if n >= 2:
                    S.wait(g, "t", t0 + (n - 1) * NSP)
                for kq in range(KD):
                    if two_stage and blk >= KD:
                        W, dc = W2s[pj], blk - KD
                        ap = bass.AP(W.tensor, dc * 128 + kq * 128 * DL,
                                     [[DL, 128], [1, 128]])
                    else:
                        W = W1s[pj]
                        ap = bass.AP(W.tensor, blk * 128 + kq * 128 * D,
                                     [[D, 128], [1, 128]])
                    ii = g.dma_start(out=wcol[j][:, kq * 128:(kq + 1) * 128],
                                     in_=ap)
                    S.inc(f"{pfx}w{j}", ii, 16)
                n += 1

    @b.block.tensor
    def _(t):
        g = 0
        for pj in range(3):
            for blk in range(NBLK):
                n = pj * NBLK + blk
                j = n % 2
                S.wait(t, f"{pfx}w{j}", (n // 2 + 1) * 16 * KD)
                stage2 = two_stage and blk >= KD
                src = a_fm if stage2 else src_fm
                if L == 0 and n == 0:
                    S.wait(t, "p0sp0", b.p0sp_tot[0])
                    S.wait(t, "p0sp1", b.p0sp_tot[1])
                    S.wait(t, "v", b.t_sxb)
                if L == 1 and n == 0:
                    S.wait(t, "ln1tp0", b.ln1tp_tot[0])
                    S.wait(t, "ln1tp1", b.ln1tp_tot[1])
                bank0 = 4 if stage2 else 0
                for spp in range(NSP):
                    pp = bank0 + (g % 2) * SPP
                    if stage2 and blk == KD and spp == 0:
                        S.wait(t, pfx + "e", e0 + g)
                    elif g >= 2:
                        S.wait(t, pfx + "e", e0 + g - 1)
                    ii = None
                    for k in range(KD):
                        for s2 in range(SPP):
                            sp = spp * SPP + s2
                            ii = t.matmul(psP[pp + s2][:],
                                          wcol[j][:, k * 128:(k + 1) * 128],
                                          src[k][:, sp * TS:(sp + 1) * TS],
                                          start=(k == 0), stop=(k == KD - 1))
                    S.inc("t", ii, 1)
                    g += 1

    bnc_jobs = []

    @b.block.vector
    def _(v):
        S.wait(v, "cst", b.cst_tot)
        g = 0
        nb = 0
        for pj in range(3):
            for blk in range(NBLK):
                stage2 = two_stage and blk >= KD
                outp = stage2 or (not two_stage)
                bank0 = 4 if stage2 else 0
                for spp in range(NSP):
                    S.wait(v, "t", t0 + g + 1)
                    pp = bank0 + (g % 2) * SPP
                    lastii = None
                    for s2 in range(SPP):
                        sp = spp * SPP + s2
                        if not outp:
                            f = blk
                            lastii = v.tensor_scalar(
                                out=a_fm[f][:, sp * TS:(sp + 1) * TS],
                                in0=psP[pp + s2][:],
                                scalar1=b.biasK[:, pj * KD + f:pj * KD + f + 1],
                                scalar2=None, op0=ALU.add)
                        else:
                            dc = (blk - KD) if two_stage else blk
                            jb = nb % 4
                            if nb >= 4:
                                S.wait(v, f"{pfx}bn{jb}", (nb // 4) * 16)
                            lastii = v.tensor_scalar(
                                out=pbounce[jb][:], in0=psP[pp + s2][:],
                                scalar1=biasL[:, pj * KO + dc:pj * KO + dc + 1],
                                scalar2=None, op0=ALU.add)
                            tk = S.inc("v", lastii, 1)
                            bnc_jobs.append((pj, dc, sp, jb, tk))
                            nb += 1
                    if lastii is not None:
                        S.wait(v, "v" if outp else "t", None if outp else t0 + g + 1)
                        if not outp:
                            S.inc(pfx + "e", lastii, 1)
                        else:
                            S.bump(v, pfx + "e", 1)
                    g += 1

    @b.block.sync
    def _(y):
        for (pj, dc, sp, jb, tk) in bnc_jobs:
            S.wait(y, "v", tk)
            dst = ag_in[pj * OROW + dc * 128:pj * OROW + (dc + 1) * 128,
                        sp * TS:(sp + 1) * TS]
            ii = y.dma_start(out=dst, in_=pbounce[jb][:])
            S.inc(f"{pfx}bn{jb}", ii, 16)
    b.__dict__[f"{pfx}bn_tot"] = [S.tok(f"{pfx}bn{jj}") for jj in range(4)]
    b.__dict__[f"{pfx}tlast"] = S.tok("t")


def _ag_qkv(b, layer):
    S = b.S
    L = layer
    pfx = f"L{L}p"
    ag_in, ag_out = (b.ag0_in, b.ag0_out) if L == 0 else (b.ag1_in, b.ag1_out)

    @b.block.gpsimd
    def _(g):
        for jj in range(4):
            S.wait(g, f"{pfx}bn{jj}", b.__dict__[f"{pfx}bn_tot"][jj])
        ii = g.collective_compute("AllGather", ALU.bypass, replica_groups=b.PAIRS,
                                  ins=[ag_in.opt()], outs=[ag_out.opt()])
        b.__dict__[f"{pfx}cc"] = S.inc("cc", ii, 1)


def _m1(b):
    S, c = b.S, b.cfg
    D, I, NT, TS = c["D"], c["I"], c["NT"], c["TS"]
    KD, KI, NS = c["KD"], c["KI"], c["NS"]
    RV = b.RV

    w1f = [RV("RC", F32, j * 4 * D, 128, D) for j in range(2)]
    qw1r = [RV("RC", BF16, 8 * D + j * 2 * D, 128, D) for j in range(2)]
    qw1T = [RV("RC", BF16, 12 * D + j * 2 * D, 128, D) for j in range(2)]
    hpre = [RV("RC", F32, 16 * D + jb * 4 * TS, 128, TS) for jb in range(4)]
    hout = [RV("RC", F32, 16 * D + 16 * TS + jb * 4 * TS, 128, TS) for jb in range(4)]
    wst = b.sb("wst", [128, 8], F32)

    psP = b.psP
    t0 = S.tok("t")
    v0 = S.tok("m1e")
    s0 = S.tok("m1s")
    g0w = [S.tok("m1w0"), S.tok("m1w1")]

    @b.block.gpsimd
    def _(g):
        b.gate(g)
        for i in range(KI):
            j = i % 2
            if i >= 2:
                S.wait(g, "m1q", i - 1)
            ii = g.dma_start(out=w1f[j][:], in_=b.w1[i * 128:(i + 1) * 128, :])
            S.inc(f"m1w{j}", ii, 16)

    @b.block.vector
    def _(v):
        S.wait(v, "cst", b.cst_tot)
        def absmax_step(n):
            jb = n % 4
            S.wait(v, "m1s", s0 + n + 1)
            st = _chain(S, v)
            st(v.reduce_max(out=wst[:, 3:4], in_=hout[jb][:], axis=AX.X,
                            apply_absolute_value=True))
            st(v.tensor_tensor(out=b.habs[:], in0=b.habs[:], in1=wst[:, 3:4],
                               op=ALU.max))
            S.bump(v, "m1hv", 1)

        for i in range(KI + 3):
            if i < KI:
                j = i % 2
                wb = 4 * j
                S.wait(v, f"m1w{j}", (i // 2 + 1) * 16)
                if i >= 2:
                    S.wait(v, f"m1tp{j}", (i // 2) * 16 * KD)
                    S.wait(v, "m1g", i - 1)
                st = _chain(S, v)
                st(v.reduce_max(out=wst[:, wb:wb + 1], in_=w1f[j][:], axis=AX.X,
                                apply_absolute_value=True))
                st(v.reciprocal(out=wst[:, wb + 1:wb + 2], in_=wst[:, wb:wb + 1]))
                st(v.tensor_scalar(out=wst[:, wb + 2:wb + 3],
                                   in0=wst[:, wb + 1:wb + 2], scalar1=127.0,
                                   scalar2=None, op0=ALU.mult))
                st(v.tensor_scalar(out=b.s1c[:, i:i + 1], in0=wst[:, wb:wb + 1],
                                   scalar1=1.0 / 127.0, scalar2=b.sm[:, 3:4],
                                   op0=ALU.mult, op1=ALU.mult))
                S.bump(v, "m1v", 1)
                S.wait(v, "m1g", i + 1)
                st(v.tensor_scalar(out=qw1r[j][:], in0=w1f[j][:], scalar1=-MAGIC,
                                   scalar2=None, op0=ALU.add))
                st(v.tensor_scalar(out=qw1r[j][:], in0=qw1r[j][:], scalar1=0.0,
                                   scalar2=None, op0=ALU.add))
                S.bump(v, "m1q", 1)
            if 3 <= i:
                ic = i - 3
                if ic < KI:
                    for sp in range(NS):
                        absmax_step(ic * NS + sp)
            if 2 <= i <= KI + 1:
                ic = i - 2
                for sp in range(NS):
                    n = ic * NS + sp
                    jb = n % 4
                    S.wait(v, "t", t0 + ic + 1)
                    if n >= 4:
                        S.wait(v, "m1s", s0 + n - 3)
                    ii = v.tensor_scalar(out=hpre[jb][:],
                                         in0=psP[(ic % 2) * 4 + sp][:],
                                         scalar1=b.s1c[:, ic:ic + 1],
                                         scalar2=b.bias1[:, ic:ic + 1],
                                         op0=ALU.mult, op1=ALU.add)
                    S.inc("m1e", ii, 1)

    @b.block.scalar
    def _(sc):
        for i in range(KI + 2):
            if i < KI:
                j = i % 2
                S.wait(sc, "m1v", i + 1)
                ii = sc.activation(out=w1f[j][:], in_=w1f[j][:], func=AF.Copy,
                                   bias=MAGIC, scale=wst[:, 4 * j + 2:4 * j + 3])
                S.inc("m1g", ii, 1)
            if i >= 2:
                ic = i - 2
                for sp in range(NS):
                    n = ic * NS + sp
                    jb = n % 4
                    S.wait(sc, "m1e", v0 + n + 1)
                    if n >= 4:
                        S.wait(sc, f"m1hx{jb}", (n // 4) * 16)
                    ii = sc.activation(out=hout[jb][:], in_=hpre[jb][:],
                                       func=AF.Gelu)
                    S.inc("m1s", ii, 1)


    def m1_spill(y, i):
        for sp in range(NS):
            n = i * NS + sp
            jb = n % 4
            S.wait(y, "m1hv", n + 1)
            ii = y.dma_start(out=b.h_sp[i * 128:(i + 1) * 128, sp * TS:(sp + 1) * TS],
                             in_=hout[jb][:])
            S.inc(f"m1hx{jb}", ii, 16)

    @b.block.sync
    def _(y):
        for i in range(KI):
            j = i % 2
            S.wait(y, "m1q", i + 1)
            if i >= 2:
                S.wait(y, "t", t0 + i - 1)
            for k in range(KD):
                ii = y.dma_start_transpose(qw1T[j][:, k * 128:(k + 1) * 128],
                                           qw1r[j][:, k * 128:(k + 1) * 128])
                S.inc(f"m1tp{j}", ii, 16)
            if i >= 1:
                m1_spill(y, i - 1)
        m1_spill(y, KI - 1)

    @b.block.tensor
    def _(t):
        for i in range(KI):
            j = i % 2
            S.wait(t, f"m1tp{j}", (i // 2 + 1) * 16 * KD)
            if i == 0:
                S.wait(t, "qtp0", b.qtp_tot[0])
                S.wait(t, "qtp1", b.qtp_tot[1])
                S.wait(t, "L0pe", None)
            if i >= 2:
                S.wait(t, "m1e", v0 + (i - 1) * NS)
            ii = None
            for k in range(KD):
                for sp in range(NS):
                    ii = t.matmul(psP[(i % 2) * 4 + sp][:],
                                  qw1T[j][:, k * 128:(k + 1) * 128],
                                  b.qx_fm[k][:, sp * TS:(sp + 1) * TS],
                                  start=(k == 0), stop=(k == KD - 1))
                if k == KD - 1:
                    S.inc("t", ii, 1)

    b.m1_end_t = S.tok("t")
    if b.upto == "m1a":
        return

    @b.block.sync
    def _(y):
        S.wait(y, "m1hv", KI * NS)
        ii = y.dma_start(out=b.row_dr[0:128], in_=b.habs[:, 0])
        S.inc("ar2io", ii, 16)
        S.wait(y, "ar2io", None)
        ii = y.dma_start(out=b.arrow[:], in_=b.row_dr.rearrange("(a x) -> a x", a=64)[0])
        S.inc("ar2io", ii, 16)

    @b.block.vector
    def _(v):
        S.wait(v, "ar2io", None)
        st = _chain(S, v)
        st(v.reduce_max(out=b.sm[0:1, 5:6], in_=b.arrow[:], axis=AX.X))
        st(v.tensor_scalar(out=b.arrow[:], in0=b.arrow[:], scalar1=0.0,
                           scalar2=b.sm[0:1, 5:6], op0=ALU.mult, op1=ALU.add))
        b.t_ar2 = S.bump(v, "v", 1)

    if b.upto == "m1c":
        return

    @b.block.gpsimd
    def _(g):
        S.wait(g, "v", b.t_ar2)
        for jb in range(4):
            S.wait(g, f"m1hx{jb}", None)
        ii = g.dma_start(out=b.ar_in, in_=b.arrow[:])
        S.inc("ar2w", ii, 16)
        S.wait(g, "ar2w", None)
        if b.upto == "m1b":
            ii = g.dma_start(out=b.ar_out, in_=b.ar_in)
            S.inc("cc", ii, 16)
            S.wait(g, "cc", None)
        else:
            ii = g.collective_compute("AllReduce", ALU.max, replica_groups=b.ALLG,
                                      ins=[b.ar_in.opt()], outs=[b.ar_out.opt()])
            S.inc("cc", ii, 1)
            S.wait(g, "cc", None)
        ii = g.dma_start(out=b.arrow[:], in_=b.ar_out)
        S.inc("ar2q", ii, 16)

    if b.upto == "m1d":
        return

    @b.block.vector
    def _(v):
        S.wait(v, "ar2q", None)
        st = _chain(S, v)
        st(v.tensor_scalar(out=b.sm[0:1, 6:7], in0=b.arrow[:, 0:1], scalar1=1.0 / 127.0,
                           scalar2=1e-8, op0=ALU.mult, op1=ALU.max))
        st(v.reciprocal(out=b.sm[0:1, 7:8], in_=b.sm[0:1, 6:7]))
        b.t_s2 = S.bump(v, "v", 1)

    if b.upto == "m1e":
        return

    @b.block.tensor
    def _(t):
        S.wait(t, "v", b.t_s2)
        ii = t.matmul(b.psP[7][:, 4:6], b.onesr_f[:], b.sm[0:1, 6:8],
                      start=True, stop=True)
        b.t_s2mm = S.inc("t", ii, 1)

    @b.block.vector
    def _(v):
        S.wait(v, "t", b.t_s2mm)
        st = _chain(S, v)
        st(v.tensor_copy(out=b.sm[:, 8:10], in_=b.psP[7][:, 4:6]))
        st(v.tensor_copy(out=b.rs2_b[:], in_=b.sm[:, 9:10]))
        b.t_s2b = S.bump(v, "v", 1)




def _w2_quant(b):
    S, c = b.S, b.cfg
    D, I, KD, KI = c["D"], c["I"], c["KD"], c["KI"]

    RV = b.RV
    w2f = RV("RB", F32, 0, 128, I)
    sw2 = b.sb("sw2", [128, KD], F32)
    rwbc = RV("RB", F32, 4 * I, 128, D)
    SWB0 = max(16 * D, 16 * c["NT"])
    b.sw2bc = RV("RC", F32, SWB0, 128, D)
    b.bias0bc = RV("RC", F32, SWB0 + 4 * D, 128, D)
    w2t = [RV("RB", F32, 4 * I + 4 * D + j * 4 * D, 128, D) for j in range(2)]
    w2i = [RV("RB", I32, j * 4 * D, 128, D) for j in range(2)]
    w2b = [RV("RC", BF16, j * 2 * D, 128, D) for j in range(2)]
    rwrow = RV("RC", F32, 4 * D, 1, D)
    swrow = RV("RC", F32, 8 * D, 1, D)
    b0row = RV("RC", F32, 12 * D, 1, D)
    amrow = RV("RB", F32, 4 * I + 4 * D, 1, D)
    borow = RV("RB", F32, 4 * I, 1, D)

    @b.block.gpsimd
    def _(g):
        b.gate(g)
        for oc in range(KD):
            if oc >= 1:
                S.wait(g, "w2a", oc)
            ii = g.dma_start(out=w2f[:], in_=b.w2[oc * 128:(oc + 1) * 128, :])
            S.inc("w2l", ii, 16)

    @b.block.vector
    def _(v):
        b.gate(v)
        for oc in range(KD):
            S.wait(v, "w2l", (oc + 1) * 16)
            st = _chain(S, v)
            st(v.reduce_max(out=sw2[:, oc:oc + 1], in_=w2f[:], axis=AX.X,
                            apply_absolute_value=True))
            S.bump(v, "w2a", 1)
    t_w2a = S.tok("w2a")

    @b.block.sync
    def _(y):
        b.gate(y)
        S.wait(y, "w2a", t_w2a)
        for oc in range(KD):
            ii = y.dma_start(out=b.row_dr[oc * 128:(oc + 1) * 128], in_=sw2[:, oc])
            S.inc("w2r", ii, 16)
        S.wait(y, "w2r", None)
        ii = y.dma_start(out=amrow[:], in_=bass.AP(b.row_dr.tensor, 0, [[D, 1], [1, D]]))
        S.inc("w2r2", ii, 16)
        ii = y.dma_start(out=b0row[:], in_=bass.AP(b.b2.tensor, 0, [[D, 1], [1, D]]))
        S.inc("w2r2", ii, 16)
        ii = y.dma_start(out=borow[:], in_=bass.AP(b.bo.tensor, 0, [[D, 1], [1, D]]))
        S.inc("w2r2", ii, 16)

    @b.block.vector
    def _(v):
        S.wait(v, "w2r2", None)
        st = _chain(S, v)
        st(v.reciprocal(out=rwrow[:], in_=amrow[:]))
        st(v.tensor_scalar(out=rwrow[:], in0=rwrow[:], scalar1=127.0, scalar2=None,
                           op0=ALU.mult))
        S.wait(v, "v", b.t_s2b)
        st(v.tensor_scalar(out=swrow[:], in0=amrow[:], scalar1=1.0 / 127.0,
                           scalar2=b.sm[0:1, 8:9], op0=ALU.mult, op1=ALU.mult))
        st(v.tensor_tensor(out=b0row[:], in0=b0row[:], in1=borow[:], op=ALU.add))
        b.t_w2rows = S.bump(v, "v", 1)

    OC = min(512, D)
    NSTR = D // OC

    @b.block.tensor
    def _(t):
        S.wait(t, "v", b.t_w2rows)
        S.wait(t, "m1e", None)
        for sseg in range(NSTR):
            sl = slice(sseg * OC, (sseg + 1) * OC)
            if sseg >= 1:
                S.wait(t, "w2v", 3 * sseg)
            ii = t.matmul(b.psP[6][:, 0:OC], b.onesr_f[:], rwrow[:, sl],
                          start=True, stop=True)
            S.inc("w2t", ii, 1)
            ii = t.matmul(b.psP[7][:, 0:OC], b.onesr_f[:], swrow[:, sl],
                          start=True, stop=True)
            S.inc("w2t", ii, 1)
            S.wait(t, "w2v", 3 * sseg + 2)
            ii = t.matmul(b.psP[6][:, 0:OC], b.onesr_f[:], b0row[:, sl],
                          start=True, stop=True)
            S.inc("w2t", ii, 1)
    t_w2t0 = S.tok("w2t") - 3 * NSTR

    @b.block.vector
    def _(v):
        for sseg in range(NSTR):
            sl = slice(sseg * OC, (sseg + 1) * OC)
            S.wait(v, "w2t", t_w2t0 + 3 * sseg + 2)
            st = _chain(S, v)
            st(v.tensor_copy(out=rwbc[:, sl], in_=b.psP[6][:, 0:OC]))
            st(v.tensor_copy(out=b.sw2bc[:, sl], in_=b.psP[7][:, 0:OC]))
            S.bump(v, "w2v", 2)
            S.wait(v, "w2t", t_w2t0 + 3 * sseg + 3)
            st(v.tensor_copy(out=b.bias0bc[:, sl], in_=b.psP[6][:, 0:OC]))
            S.bump(v, "w2v", 1)
        S.bump(v, "w2bc", 1)
    b.t_w2bc = S.tok("w2bc")

    @b.block.gpsimd
    def _(g):
        S.wait(g, "v", b.t_w2rows)
        for ic in range(KI):
            j = ic % 2
            if ic >= 2:
                S.wait(g, "w2q", ic - 1)
            ii = g.dma_start(out=w2t[j][:], in_=b.w2T[ic * 128:(ic + 1) * 128, :])
            S.inc(f"w2tl{j}", ii, 16)

    sqw_base = S.tok("s")

    @b.block.vector
    def _(v):
        S.wait(v, "w2bc", b.t_w2bc)
        for ic in range(KI):
            j = ic % 2
            S.wait(v, f"w2tl{j}", (ic // 2 + 1) * 16)
            if ic >= 2:
                S.wait(v, f"w2st{j}", (ic // 2) * 16)
            st = _chain(S, v)
            st(v.tensor_tensor(out=w2t[j][:], in0=w2t[j][:], in1=rwbc[:], op=ALU.mult))
            S.bump(v, "w2m", 1)
            S.wait(v, "s", sqw_base + ic + 1)
            st(v.tensor_scalar(out=w2i[j][:], in0=w2t[j][:], scalar1=-MAGIC,
                               scalar2=None, op0=ALU.add))
            st(v.tensor_scalar(out=w2b[j][:], in0=w2i[j][:], scalar1=0.0,
                               scalar2=None, op0=ALU.add))
            S.bump(v, "w2q", 1)

    @b.block.scalar
    def _(sc):
        b.gate(sc)
        for ic in range(KI):
            j = ic % 2
            S.wait(sc, "w2m", ic + 1)
            ii = sc.activation(out=w2t[j][:], in_=w2t[j][:], func=AF.Copy, bias=MAGIC)
            S.inc("s", ii, 1)

    @b.block.sync
    def _(y):
        for ic in range(KI):
            j = ic % 2
            S.wait(y, "w2q", ic + 1)
            ii = y.dma_start(out=b.qw2T_dr[ic * 128:(ic + 1) * 128, :], in_=w2b[j][:])
            S.inc(f"w2st{j}", ii, 16)
    b.w2st_tot = [S.tok("w2st0"), S.tok("w2st1")]


def _attn0(b):
    S, c = b.S, b.cfg
    D, DL, NT, T, TS = c["D"], c["DL"], c["NT"], c["T"], c["TS"]
    NHm, HD, TQ, NKT = c["NHm"], c["HD"], c["TQ"], c["NKT"]
    SCL = 1.0 / math.sqrt(HD)

    cu = b.CUR("RA")
    qh = [cu(BF16, 64, T) for j in range(2)]
    kh = [cu(BF16, 64, T) for j in range(2)]
    vtm = [cu(BF16, 128, 65 * NKT) for j in range(2)]
    sqs = cu(F32, 64, 512)
    Pb = [cu(BF16, 128, TS) for j in range(3)]
    ctxl = [cu(F32, 65, TS) for j in range(2)]
    rr_ = cu(F32, 4, 2 * TS)
    rro = cu.last_off
    rrow = [b.RV("RA", F32, rro + jj * 4 * TS, 1, TS) for jj in range(2)]
    cbnc = [cu(BF16, 64, TS) for j in range(2)]
    nmax = b.sb("nmax", [1, 4], F32)
    vb = [b.sb(f"vb{sl}", [128, 64], BF16) for sl in range(4)]

    @b.block.vector
    def _(v):
        b.gate(v)
        for j in range(2):
            ap = bass.AP(vtm[j].tensor, vtm[j].offset + 64,
                         [[vtm[j].tensor.shape[1], 128], [65, NKT], [1, 1]])
            ii = v.memset(ap, 1.0)
            S.inc("avo", ii, 1)

    @b.block.scalar
    def _(sc):
        b.gate(sc)
        sc.nop()

    NSQ = T // 512
    TQl = TQ
    NE0 = 2 * TQl * (TQl + 1)
    ent = 0
    grp = 0

    @b.block.sync
    def _(y):
        b.gate(y)
        S.wait(y, "cc", b.L0pcc)
        for side in b.sswitch(y, "y"):
            ro = side * NHm * HD * NT
            for h in range(NHm):
                j = h % 2
                if h >= 2:
                    S.wait(y, "ac", (h - 1) * NE0)
                    S.wait(y, "ax", h * NE0)
                for tb in range(2):
                    src_q = bass.AP(b.ag0_out.tensor,
                                    (tb * 3 * DL + h * 64) * NT + ro,
                                    [[NT, 64], [1, NT]])
                    ii = y.dma_start(out=qh[j][:, tb * NT:(tb + 1) * NT], in_=src_q)
                    S.inc(f"aq{j}", ii, 16)
                    src_k = bass.AP(b.ag0_out.tensor,
                                    (tb * 3 * DL + DL + h * 64) * NT + ro,
                                    [[NT, 64], [1, NT]])
                    ii = y.dma_start(out=kh[j][:, tb * NT:(tb + 1) * NT], in_=src_k)
                    S.inc(f"aq{j}", ii, 16)
                for kt in range(NKT):
                    tb = kt * 128 // NT
                    col = kt * 128 - tb * NT
                    n = h * NKT + kt
                    sl = n % 4
                    src_v = bass.AP(b.ag0_out.tensor,
                                    (tb * 3 * DL + 2 * DL + h * 64) * NT + ro + col,
                                    [[NT, 64], [1, 128]])
                    if n >= 4:
                        S.wait(y, f"acp{sl}", (n // 4) * 16)
                    ii = y.dma_start_transpose(vb[sl][:], src_v)
                    S.inc(f"avb{sl}", ii, 16)
                for qs in range(TQl):
                    g = h * TQl + qs
                    S.wait(y, "aee", g + 1)
                    ii = y.dma_start(out=b.ag0c_in[h * 64:(h + 1) * 64,
                                                   qs * TS:(qs + 1) * TS],
                                     in_=cbnc[g % 2][:])
                    S.inc(f"acw{g % 2}", ii, 16)

    for h in range(NHm):
        j = h % 2
        nsq0 = S.tok("ant")

        @b.block.vector
        def _(v, h=h, j=j):
            st = _chain(S, v)
            for kt in range(NKT):
                n = h * NKT + kt
                sl = n % 4
                S.wait(v, f"avb{sl}", (n // 4 + 1) * 16)
                st(v.tensor_copy(out=vtm[j][:, kt * 65:kt * 65 + 64],
                                 in_=vb[sl][:]))
                S.bump(v, f"acp{sl}", 16)

        @b.block.scalar
        def _(sc, h=h, j=j, nsq0=nsq0):
            S.wait(sc, f"aq{j}", (h // 2 + 1) * 64)
            for w_ in range(2):
                for st_ in range(NSQ):
                    n = w_ * NSQ + st_
                    if n >= 1:
                        S.wait(sc, "ant", nsq0 + n)
                    src = (qh if w_ == 0 else kh)[j][:, st_ * 512:(st_ + 1) * 512]
                    ii = sc.activation(out=sqs[:], in_=src, func=AF.Square)
                    S.inc("asq", ii, 1)

        asq0 = S.tok("asq") - 2 * NSQ

        @b.block.tensor
        def _(t, h=h, nsq0=nsq0, asq0=asq0):
            if h == 0:
                S.wait(t, "m1e", None)
                S.wait(t, "w2t", None)
            for n in range(2 * NSQ):
                S.wait(t, "asq", asq0 + n + 1)
                if n >= 1:
                    S.wait(t, "anv", nsq0 + n)
                ii = t.matmul(b.psP[4][0:1, 0:512], b.ones_f[0:64, :], sqs[:],
                              start=True, stop=True)
                S.inc("ant", ii, 1)

        @b.block.vector
        def _(v, h=h, nsq0=nsq0):
            st = _chain(S, v)
            st(v.memset(nmax[:], 0.0))
            for w_ in range(2):
                for st_ in range(NSQ):
                    n = w_ * NSQ + st_
                    S.wait(v, "ant", nsq0 + n + 1)
                    st(v.reduce_max(out=nmax[0:1, 2:3], in_=b.psP[4][0:1, 0:512],
                                    axis=AX.X))
                    st(v.tensor_tensor(out=nmax[0:1, w_:w_ + 1],
                                       in0=nmax[0:1, w_:w_ + 1],
                                       in1=nmax[0:1, 2:3], op=ALU.max))
                    S.bump(v, "anv", 1)
            st(v.tensor_tensor(out=nmax[0:1, 3:4], in0=nmax[0:1, 0:1],
                               in1=nmax[0:1, 1:2], op=ALU.add))
            st(v.tensor_scalar(out=nmax[0:1, 3:4], in0=nmax[0:1, 3:4],
                               scalar1=-0.5 * SCL, scalar2=None, op0=ALU.mult))
            S.bump(v, "acr", 1)

        @b.block.tensor
        def _(t, h=h):
            S.wait(t, "acr", h + 1)
            ii = t.matmul(b.psP[6][:, 0:1], b.onesr_f[:], nmax[0:1, 3:4],
                          start=True, stop=True)
            S.inc("acb", ii, 1)

        @b.block.vector
        def _(v, h=h, j=j):
            S.wait(v, "acb", h + 1)
            if h >= 2:
                S.wait(v, "ax", (h - 1) * NE0)
            st = _chain(S, v)
            st(v.tensor_copy(out=b.sm[:, 10 + j:11 + j], in_=b.psP[6][:, 0:1]))
            S.bump(v, "acc2", 1)

        ent0 = S.tok("at")
        sched_h = [(qs, kt) for qs in range(TQ) for kt in range(4 * (qs + 1))]
        NE = len(sched_h)
        eidx_h = {e: i for i, e in enumerate(sched_h)}
        ax0 = S.tok("ax")
        ac0 = S.tok("ac")
        am_map = {}
        nm = S.tok("am")
        for (qs, kt) in sched_h:
            if kt >= 4 * qs:
                am_map[(qs, kt)] = nm
                nm += 1
        grp0 = grp

        @b.block.tensor
        def _(t, h=h, j=j, ent0=ent0, ax0=ax0, ac0=ac0, am_map=am_map,
              sched_h=sched_h, eidx_h=eidx_h, grp0=grp0):
            S.wait(t, f"aq{j}", (h // 2 + 1) * 64)
            for sl_ in range(4):
                S.wait(t, f"acp{sl_}", (h + 1) * (NKT // 4) * 16)
            S.wait(t, "avo", 2)
            S.wait(t, "acc2", h + 1)
            def ctx_mm(n, qs, kt):
                g = grp0 + qs
                if kt == 0 and g >= 2:
                    S.wait(t, "aev", g - 1)
                if kt >= 4 * qs:
                    S.wait(t, "am", am_map[(qs, kt)] + 1)
                else:
                    S.wait(t, "ax", ent0 + n + 1)
                ii = t.matmul(b.psP[2 + g % 2][0:65, 0:TS],
                              vtm[j][:, kt * 65:(kt + 1) * 65],
                              Pb[(ent0 + n) % 3][:],
                              start=(kt == 0), stop=(kt == 4 * (qs + 1) - 1))
                S.inc("ac", ii, 1)
                if kt == 4 * (qs + 1) - 1:
                    S.wait(t, "arx", g + 1)
                    ii = t.matmul(b.psP[5][0:64, 0:TS], b.onesr_f[:, 0:64],
                                  rrow[g % 2][:], start=True, stop=True)
                    S.inc("ab", ii, 1)

            for n, (qs, kt) in enumerate(sched_h):
                if ent0 + n >= 2:
                    S.wait(t, "ax", ent0 + n - 1)
                ii = t.matmul(b.psP[n % 2][:, 0:TS],
                              kh[j][:, kt * 128:(kt + 1) * 128],
                              qh[j][:, qs * TS:(qs + 1) * TS],
                              start=True, stop=True)
                S.inc("at", ii, 1)
                if n >= 1:
                    ctx_mm(n - 1, *sched_h[n - 1])
            ctx_mm(len(sched_h) - 1, *sched_h[-1])

        @b.block.scalar
        def _(sc, h=h, j=j, ent0=ent0, ac0=ac0, sched_h=sched_h):
            for n, (qs, kt) in enumerate(sched_h):
                S.wait(sc, "at", ent0 + n + 1)
                if ent0 + n >= 3:
                    S.wait(sc, "ac", ac0 - ent0 + ent0 + n - 2)
                ii = sc.activation(out=Pb[(ent0 + n) % 3][:],
                                   in_=b.psP[n % 2][:, 0:TS],
                                   func=AF.Exp, bias=b.sm[:, 10 + j:11 + j],
                                   scale=SCL)
                S.inc("ax", ii, 1)

        @b.block.vector
        def _(v, h=h, ent0=ent0, ac0=ac0, sched_h=sched_h, eidx_h=eidx_h,
              grp0=grp0):
            for n, (qs, kt) in enumerate(sched_h):
                g = grp0 + qs
                if kt >= 4 * qs:
                    S.wait(v, "ax", ent0 + n + 1)
                    vv = kt - 4 * qs
                    st = _chain(S, v)
                    st(v.tensor_tensor(out=Pb[(ent0 + n) % 3][:],
                                       in0=Pb[(ent0 + n) % 3][:],
                                       in1=b.tri[:, vv * TS:(vv + 1) * TS],
                                       op=ALU.mult))
                    S.bump(v, "am", 1)
                if kt == 4 * (qs + 1) - 1:
                    S.wait(v, "ac", ac0 + n + 1)
                    if g >= 2:
                        S.wait(v, f"acw{g % 2}", (g // 2) * 16)
                    st = _chain(S, v)
                    st(v.tensor_copy(out=ctxl[g % 2][:],
                                     in_=b.psP[2 + g % 2][0:65, 0:TS]))
                    S.bump(v, "aev", 1)
                    st(v.reciprocal(out=rrow[g % 2][:], in_=ctxl[g % 2][64:65, :]))
                    S.bump(v, "arx", 1)
                    S.wait(v, "ab", g + 1)
                    st(v.tensor_tensor(out=cbnc[g % 2][:], in0=ctxl[g % 2][0:64, :],
                                       in1=b.psP[5][0:64, 0:TS], op=ALU.mult))
                    S.bump(v, "aee", 1)

        ent += NE
        grp += TQ
    b.acw_tot = [S.tok("acw0"), S.tok("acw1")]


def _agc(b, layer):
    S = b.S
    L = layer
    ag_in, ag_out = (b.ag0c_in, b.ag0c_out) if L == 0 else (b.ag1c_in, b.ag1c_out)

    @b.block.gpsimd
    def _(g):
        if L == 0:
            for jj in range(2):
                S.wait(g, f"acw{jj}", b.acw_tot[jj])
        else:
            for jj in range(2):
                S.wait(g, f"a1cw{jj}", b.c1w_tot[jj])
        ii = g.collective_compute("AllGather", ALU.bypass, replica_groups=b.PAIRS,
                                  ins=[ag_in.opt()], outs=[ag_out.opt()])
        b.__dict__[f"agc{L}cc"] = S.inc("cc", ii, 1)


def _m2_x1(b):
    S, c = b.S, b.cfg
    D, DL, I, NT, T = c["D"], c["DL"], c["I"], c["NT"], c["T"]
    KD, KI, NTT = c["KD"], c["KI"], c["NTT"]
    KLc = DL // 128
    NTB = NT // 512
    NOS = 2 if D >= 1024 else 1
    OSW = D // NOS
    W2 = min(512, OSW)
    NOS2 = OSW // W2
    psP = b.psP

    RV = b.RV
    SWB0 = max(16 * D, 16 * NT)
    hq = [RV("RC", F32, j * 4 * NT, 128, NT) for j in range(2)]
    hqi = [RV("RC", I32, 8 * NT + j * 4 * NT, 128, NT) for j in range(2)]
    hqb = [RV("RC", BF16, SWB0 + 8 * D + j * 2 * NT, 128, NT)
           for j in range(2)]

    def q2_store(y, i):
        j = i % 2
        S.wait(y, "q2v", i + 1)
        ii = y.dma_start(out=b.q2_dr[i * 128:(i + 1) * 128, :], in_=hqb[j][:])
        S.inc(f"q2w{j}", ii, 16)

    @b.block.sync
    def _(y):
        b.gate(y)
        for jb in range(4):
            S.wait(y, f"m1hx{jb}", None)
        for i in range(KI):
            j = i % 2
            if i >= 2:
                S.wait(y, "q2v", i - 1)
            ii = y.dma_start(out=hq[j][:], in_=b.h_sp[i * 128:(i + 1) * 128, :])
            S.inc(f"q2l{j}", ii, 16)
            if i >= 1:
                q2_store(y, i - 1)
        q2_store(y, KI - 1)

    if b.upto == "q2A":
        return

    @b.block.scalar
    def _(sc):
        S.wait(sc, "v", b.t_s2b)
        for i in range(KI):
            j = i % 2
            S.wait(sc, f"q2l{j}", (i // 2 + 1) * 16)
            if b.upto == "q2v2":
                ii = sc.activation(out=hq[j][:], in_=hq[j][:], func=AF.Copy,
                                   bias=MAGIC, scale=26.0)
            else:
                ii = sc.activation(out=hq[j][:], in_=hq[j][:], func=AF.Copy,
                                   bias=MAGIC, scale=b.rs2_b[:, 0:1])
            S.inc("q2s", ii, 1)
    if b.upto == "q2B":
        return

    @b.block.vector
    def _(v):
        if b.upto != "q2ng":
            b.gate(v)
        for i in range(KI):
            j = i % 2
            S.wait(v, "q2s", i + 1)
            if i >= 2:
                S.wait(v, f"q2w{j}", (i // 2) * 16)
            st = _chain(S, v)
            st(v.tensor_scalar(out=hqi[j][:], in0=hq[j][:], scalar1=-MAGIC,
                               scalar2=None, op0=ALU.add))
            st(v.tensor_scalar(out=hqb[j][:], in0=hqi[j][:], scalar1=0.0,
                               scalar2=None, op0=ALU.add))
            S.bump(v, "q2v", 1)

    if b.upto == "q2C":
        return

    b.q2w_tot = [S.tok("q2w0"), S.tok("q2w1")]
    if b.upto == "q2":
        return

    cb_ = b.CUR("RB")
    wop = [cb_(BF16, 128, OSW) for kc in range(KLc)]
    q2s_ = [cb_(BF16, 128, 512) for j in range(2)]
    qwS = [cb_(BF16, 128, OSW) for j in range(2)]
    ctxf = [cb_(BF16, 128, 512) for kc in range(KLc)]
    rbm_top = cb_.pos[0]
    ca_ = b.CUR("RA")
    x1b = [ca_(F32, 128, D) for tt in range(4)]
    x0m = [ca_(F32, 128, D) for j in range(2)]
    sqm = ca_(F32, 128, D)
    xob = [ca_(BF16, 128, D) for j in range(2)]
    n1 = min(KD, SWB0 // (2 * NT))
    n2 = min(KD - n1, (b.reg["RC"]["size"] - (SWB0 + 8 * D)) // (2 * NT))
    b.xn1_fm = []
    for k in range(KD):
        if k < n1:
            b.xn1_fm.append(RV("RC", BF16, k * 2 * NT, 128, NT))
        elif k < n1 + n2:
            b.xn1_fm.append(RV("RC", BF16, SWB0 + 8 * D + (k - n1) * 2 * NT,
                               128, NT))
        else:
            off = max(32768, ((rbm_top + 63) // 64) * 64)
            b.xn1_fm.append(RV("RB", BF16, off + (k - n1 - n2) * 2 * NT, 128, NT))
    statm = b.sb("statm", [128, 8], F32)

    @b.block.gpsimd
    def _(g):
        b.gate(g)
        for tb in range(NTB):
            for osp in range(NOS):
                nwo = tb * NOS + osp
                if nwo >= 1:
                    S.wait(g, "m2p", nwo)
                for kc in range(KLc):
                    ap = bass.AP(b.wo.tensor, kc * 128 * D + osp * OSW,
                                 [[D, 128], [1, OSW]])
                    ii = g.dma_start(out=wop[kc][:], in_=ap)
                    S.inc("m2wo", ii, 16)
                for i in range(KI):
                    j = i % 2
                    ni = nwo * KI + i
                    if ni >= 2:
                        S.wait(g, "m2ti", ni - 1)
                    ii = g.dma_start(out=qwS[j][:],
                                     in_=b.qw2T_dr[i * 128:(i + 1) * 128,
                                                   osp * OSW:(osp + 1) * OSW])
                    S.inc(f"m2qw{j}", ii, 16)

    def _m2_sync_body(y, co):
        for tb in range(NTB):
            for osp in range(NOS):
                nwo = tb * NOS + osp
                for i in range(KI):
                    j = i % 2
                    ni = nwo * KI + i
                    if ni >= 2:
                        S.wait(y, "m2ti", ni - 1)
                    ii = y.dma_start(out=q2s_[j][:],
                                     in_=b.q2_dr[i * 128:(i + 1) * 128,
                                                 tb * 512:(tb + 1) * 512])
                    S.inc(f"m2q2{j}", ii, 16)
                if osp == 0:
                    if tb >= 1:
                        S.wait(y, "m2p", tb * NOS)
                    for kc in range(KLc):
                        ap = bass.AP(b.ag0c_out.tensor,
                                     co + (kc * 128 * T + tb * 512),
                                     [[T, 128], [1, 512]])
                        ii = y.dma_start(out=ctxf[kc][:], in_=ap)
                        S.inc("m2cx", ii, 16)
                if osp == NOS - 1:
                    for tt4 in range(4):
                        n = tb * 4 + tt4
                        jx = n % 2
                        if n >= 2:
                            S.wait(y, "m2x1", n - 1)
                        ii = y.dma_start(out=x0m[jx][:],
                                         in_=b.x0[n * 128:(n + 1) * 128, :])
                        S.inc(f"m2x0{jx}", ii, 16)

    @b.block.sync
    def _(y):
        for jj in range(2):
            S.wait(y, f"w2st{jj}", b.w2st_tot[jj])
            S.wait(y, f"q2w{jj}", b.q2w_tot[jj])
        S.wait(y, "cc", b.agc0cc)
        for side in b.sswitch(y, "y"):
            _m2_sync_body(y, side * NT)

    @b.block.tensor
    def _(t):
        S.wait(t, "w2bc", b.t_w2bc)
        for nm_ in ("aee", "ax", "anv", "acc2", "am"):
            S.wait(t, nm_, None)
        for tb in range(NTB):
            for osp in range(NOS):
                nwo = tb * NOS + osp
                for i in range(KI):
                    j = i % 2
                    ni = nwo * KI + i
                    S.wait(t, f"m2qw{j}", (ni // 2 + 1) * 16)
                    S.wait(t, f"m2q2{j}", (ni // 2 + 1) * 16)
                    if i == 0 and nwo >= 1:
                        S.wait(t, "m2eB", nwo)
                    ii = None
                    for tt4 in range(4):
                        for os2 in range(NOS2):
                            ii = t.matmul(psP[tt4 * 2 + os2][:, 0:W2],
                                          q2s_[j][:, tt4 * 128:(tt4 + 1) * 128],
                                          qwS[j][:, os2 * W2:(os2 + 1) * W2],
                                          start=(i == 0), stop=(i == KI - 1))
                    S.inc("m2ti", ii, 1)
                S.wait(t, "m2ev1", nwo + 1)
                S.wait(t, "m2wo", (nwo + 1) * 16 * KLc)
                S.wait(t, "m2cx", (tb + 1) * 16 * KLc)
                ii = None
                for kc in range(KLc):
                    for tt4 in range(4):
                        for os2 in range(NOS2):
                            ii = t.matmul(psP[tt4 * 2 + os2][:, 0:W2],
                                          ctxf[kc][:, tt4 * 128:(tt4 + 1) * 128],
                                          wop[kc][:, os2 * W2:(os2 + 1) * W2],
                                          start=(kc == 0), stop=(kc == KLc - 1))
                S.inc("m2p", ii, 1)

    m2ti0 = S.tok("m2ti") - NTB * NOS * KI
    m2p0 = S.tok("m2p") - NTB * NOS
    u_sp = 16 * (KD + 1)
    svm0 = S.tok("ln1s")
    vln0 = S.tok("ln1v")

    @b.block.vector
    def _(v):
        for tb in range(NTB):
            for osp in range(NOS):
                nwo = tb * NOS + osp
                S.wait(v, "m2ti", m2ti0 + (nwo + 1) * KI)
                if tb >= 1 and osp == 0:
                    S.wait(v, "ln1tp0", 2 * tb * u_sp)
                    S.wait(v, "ln1tp1", 2 * tb * u_sp)
                st = _chain(S, v)
                for tt4 in range(4):
                    for os2 in range(NOS2):
                        csl = slice(osp * OSW + os2 * W2,
                                    osp * OSW + (os2 + 1) * W2)
                        st(v.tensor_tensor(
                            out=x1b[tt4][:, csl],
                            in0=psP[tt4 * 2 + os2][:, 0:W2],
                            in1=b.sw2bc[:, csl], op=ALU.mult))
                S.bump(v, "m2ev1", 1)
                S.wait(v, "m2p", m2p0 + nwo + 1)
                st = _chain(S, v)
                for tt4 in range(4):
                    for os2 in range(NOS2):
                        csl = slice(osp * OSW + os2 * W2,
                                    osp * OSW + (os2 + 1) * W2)
                        st(v.tensor_tensor(
                            out=x1b[tt4][:, csl],
                            in0=x1b[tt4][:, csl],
                            in1=psP[tt4 * 2 + os2][:, 0:W2], op=ALU.add))
                        st(v.tensor_tensor(out=x1b[tt4][:, csl],
                                           in0=x1b[tt4][:, csl],
                                           in1=b.bias0bc[:, csl], op=ALU.add))
                S.bump(v, "m2eB", 1)
            for tt4 in range(4):
                n = tb * 4 + tt4
                jx = n % 2
                jo = n % 2
                S.wait(v, f"m2x0{jx}", ((n // 2) + 1) * 16)
                st = _chain(S, v)
                st(v.tensor_tensor(out=x1b[tt4][:], in0=x1b[tt4][:], in1=x0m[jx][:],
                                   op=ALU.add))
                S.bump(v, "m2x1", 1)
                S.wait(v, "ln1s", svm0 + 2 * n + 1)
                v.reduce_sum(out=statm[:, 0:1], in_=x1b[tt4][:], axis=AX.X)
                st(v.reduce_sum(out=statm[:, 1:2], in_=sqm[:], axis=AX.X))
                st(v.tensor_scalar(out=statm[:, 2:3], in0=statm[:, 0:1],
                                   scalar1=-1.0 / D, scalar2=None, op0=ALU.mult))
                st(v.tensor_tensor(out=statm[:, 4:5], in0=statm[:, 2:3],
                                   in1=statm[:, 2:3], op=ALU.mult))
                st(v.tensor_scalar(out=statm[:, 3:4], in0=statm[:, 1:2],
                                   scalar1=1.0 / D, scalar2=None, op0=ALU.mult))
                st(v.tensor_tensor(out=statm[:, 3:4], in0=statm[:, 3:4],
                                   in1=statm[:, 4:5], op=ALU.subtract))
                st(v.tensor_scalar(out=statm[:, 3:4], in0=statm[:, 3:4],
                                   scalar1=LNEPS, scalar2=None, op0=ALU.add))
                S.bump(v, "ln1x", 1)
                S.wait(v, "ln1s", svm0 + 2 * n + 2)
                st(v.reciprocal(out=statm[:, 6:7], in_=statm[:, 5:6]))
                st(v.tensor_tensor(out=statm[:, 7:8], in0=statm[:, 6:7],
                                   in1=statm[:, 6:7], op=ALU.mult))
                st(v.tensor_tensor(out=statm[:, 7:8], in0=statm[:, 7:8],
                                   in1=statm[:, 3:4], op=ALU.mult))
                st(v.tensor_scalar(out=statm[:, 7:8], in0=statm[:, 7:8],
                                   scalar1=-0.5, scalar2=1.5, op0=ALU.mult,
                                   op1=ALU.add))
                st(v.tensor_tensor(out=statm[:, 6:7], in0=statm[:, 6:7],
                                   in1=statm[:, 7:8], op=ALU.mult))
                if n >= 2:
                    S.wait(v, f"ln1tp{jo}", ((n - 2) // 2 + 1) * u_sp)
                st(v.tensor_scalar(out=xob[jo][:], in0=x1b[tt4][:],
                                   scalar1=statm[:, 2:3], scalar2=statm[:, 6:7],
                                   op0=ALU.add, op1=ALU.mult))
                S.bump(v, "ln1v", 1)

    def ln1_spill(sc, n):
        tt4 = n % 4
        jo = n % 2
        S.wait(sc, "ln1v", vln0 + n + 1)
        ii = sc.dma_start(out=b.x1_sp[n * 128:(n + 1) * 128, :], in_=x1b[tt4][:])
        S.inc(f"ln1tp{jo}", ii, 16)
        for k in range(KD):
            ii = sc.dma_start_transpose(b.xn1_fm[k][:, n * 128:(n + 1) * 128],
                                        xob[jo][:, k * 128:(k + 1) * 128])
            S.inc(f"ln1tp{jo}", ii, 16)

    @b.block.scalar
    def _(sc):
        for n in range(NTT):
            S.wait(sc, "m2x1", n + 1)
            if n >= 1:
                S.wait(sc, "ln1v", vln0 + n)
            ii = sc.activation(out=sqm[:], in_=x1b[n % 4][:], func=AF.Square)
            S.inc("ln1s", ii, 1)
            S.wait(sc, "ln1x", n + 1)
            ii = sc.activation(out=statm[:, 5:6], in_=statm[:, 3:4], func=AF.Sqrt)
            S.inc("ln1s", ii, 1)
            ln1_spill(sc, n)
    b.ln1tp_tot = [S.tok("ln1tp0"), S.tok("ln1tp1")]


def _ffn1(b):
    S, c = b.S, b.cfg
    D, NT, T = c["D"], c["NT"], c["T"]
    KD, NTT = c["KD"], c["NTT"]
    NS = NT // 512
    NTB = NT // 512
    OC1 = min(512, D)
    NOS1 = D // OC1
    psP = b.psP

    cu = b.CUR("RA")
    wrt = cu(BF16, 128, 8 * KD)
    ats = cu(BF16, 128, 64 * KD)
    bcats = cu(BF16, 64, D)
    wbp = [b.RV("RB", BF16, kc * 2 * OC1, 128, OC1) for kc in range(KD)]
    logits_sb = b.RV("RB", F32, 2 * OC1 * KD, 8, NT)
    h1sb = b.RV("RB", F32, 2 * OC1 * KD + 4 * NT, 64, NT)
    assert 2 * OC1 * KD + 8 * NT <= 32768
    lt = cu(F32, 128, 8)
    gstat = cu(F32, 128, 6)
    eqs = cu(F32, 128, 16)
    wt_ = cu(F32, 128, 8)
    wfm = cu(F32, 8, NT)
    wfb = cu(BF16, 8, 512)
    hpb = cu(BF16, 64, NT)
    br_ = cu(F32, 2, D)
    bro = cu.last_off
    bbrow = b.RV("RA", F32, bro, 1, D, p0=0)
    bbasebc = cu(F32, 128, D)
    ybnc = [cu(F32, 128, 512) for j in range(2)]
    assert cu.pos[0] <= b.reg["RA"]["size"], cu.pos[0]

    @b.block.gpsimd
    def _(g):
        b.gate(g)
        for kq in range(KD):
            ap = bass.AP(b.WrT.tensor, kq * 128 * 8, [[8, 128], [1, 8]])
            ii = g.dma_start(out=wrt[:, kq * 8:(kq + 1) * 8], in_=ap)
            S.inc("f1w", ii, 16)
            ap = bass.AP(b.AT.tensor, kq * 128 * 64, [[64, 128], [1, 64]])
            ii = g.dma_start(out=ats[:, kq * 64:(kq + 1) * 64], in_=ap)
            S.inc("f1w", ii, 16)
        ii = g.dma_start(out=bcats[:], in_=b.Bcat)
        S.inc("f1w", ii, 16)
    f1w_tot = S.tok("f1w")

    @b.block.sync
    def _(y):
        b.gate(y)
        ii = y.dma_start(out=bbrow[:], in_=bass.AP(b.bbase.tensor, 0, [[D, 1], [1, D]]))
        S.inc("f1r", ii, 16)

    NSTR = NOS1

    @b.block.tensor
    def _(t):
        S.wait(t, "L1pe", None)
        S.wait(t, "m2eB", None)
        S.wait(t, "m2ti", None)
        S.wait(t, "ln1v", None)
        S.wait(t, "f1r", None)
        for sseg in range(NSTR):
            if sseg >= 1:
                S.wait(t, "f1bv", sseg)
            ii = t.matmul(psP[6][:, 0:OC1], b.onesr_f[:],
                          bbrow[:, sseg * OC1:(sseg + 1) * OC1], start=True, stop=True)
            S.inc("f1bt", ii, 1)
    f1bt0 = S.tok("f1bt") - NSTR

    @b.block.vector
    def _(v):
        b.gate(v)
        for sseg in range(NSTR):
            S.wait(v, "f1bt", f1bt0 + sseg + 1)
            st = _chain(S, v)
            st(v.tensor_copy(out=bbasebc[:, sseg * OC1:(sseg + 1) * OC1],
                             in_=psP[6][:, 0:OC1]))
            S.bump(v, "f1bv", 1)
    t_f1bc = S.tok("f1bv")

    @b.block.tensor
    def _(t):
        S.wait(t, "f1w", f1w_tot)
        for ss in range(NS):
            if ss >= 1:
                S.wait(t, "f1le", 2 * ss)
            ii = None
            for k in range(KD):
                ii = t.matmul(psP[4][0:8, 0:512], wrt[:, k * 8:(k + 1) * 8],
                              b.xn1_fm[k][:, ss * 512:(ss + 1) * 512],
                              start=(k == 0), stop=(k == KD - 1))
            S.inc("f1l", ii, 1)
            for k in range(KD):
                ii = t.matmul(psP[5][0:64, 0:512], ats[:, k * 64:(k + 1) * 64],
                              b.xn1_fm[k][:, ss * 512:(ss + 1) * 512],
                              start=(k == 0), stop=(k == KD - 1))
            S.inc("f1l", ii, 1)
    f1l0 = S.tok("f1l") - 2 * NS

    @b.block.vector
    def _(v):
        for ss in range(NS):
            S.wait(v, "f1l", f1l0 + 2 * ss + 1)
            st = _chain(S, v)
            st(v.tensor_copy(out=logits_sb[:, ss * 512:(ss + 1) * 512],
                             in_=psP[4][0:8, 0:512]))
            S.bump(v, "f1le", 1)
            S.wait(v, "f1l", f1l0 + 2 * ss + 2)
            st(v.tensor_copy(out=h1sb[:, ss * 512:(ss + 1) * 512],
                             in_=psP[5][0:64, 0:512]))
            S.bump(v, "f1le", 1)
    f1le_tot = S.tok("f1le")

    sg0 = S.tok("f1gs")

    @b.block.tensor
    def _(t):
        for tt in range(NTT):
            S.wait(t, "f1le", f1le_tot)
            if tt == 0:
                S.wait(t, "f1bv", t_f1bc)
            if tt >= 1:
                S.wait(t, "f1gv", 2 * tt)
            ii = t.transpose(psP[6][0:128, 0:8],
                             logits_sb[:, tt * 128:(tt + 1) * 128],
                             b.idT[0:8, 0:8])
            S.inc("f1gt", ii, 1)
            S.wait(t, "f1gv", 2 * tt + 1)
            ii = t.transpose(psP[6][0:8, 8:136],
                             wt_[:], b.idT[:])
            S.inc("f1gt", ii, 1)
    f1gt0 = S.tok("f1gt") - 2 * NTT

    @b.block.vector
    def _(v):
        for tt in range(NTT):
            S.wait(v, "f1gt", f1gt0 + 2 * tt + 1)
            st = _chain(S, v)
            st(v.tensor_copy(out=lt[:], in_=psP[6][0:128, 0:8]))
            st(v.reduce_max(out=gstat[:, 0:1], in_=lt[:], axis=AX.X))
            st(v.tensor_scalar(out=eqs[:, 0:8], in0=lt[:], scalar1=gstat[:, 0:1],
                               scalar2=None, op0=ALU.is_equal))
            st(v.scalar_tensor_tensor(out=eqs[:, 8:16], in0=eqs[:, 0:8],
                                      scalar=-BIG, in1=lt[:],
                                      op0=ALU.mult, op1=ALU.add))
            st(v.reduce_max(out=gstat[:, 1:2], in_=eqs[:, 8:16], axis=AX.X))
            st(v.tensor_scalar(out=eqs[:, 8:16], in0=eqs[:, 8:16],
                               scalar1=gstat[:, 1:2], scalar2=None, op0=ALU.is_equal))
            st(v.tensor_tensor(out=gstat[:, 2:3], in0=gstat[:, 0:1],
                               in1=gstat[:, 1:2], op=ALU.subtract))
            S.bump(v, "f1gd", 1)
            S.wait(v, "f1gg", tt + 1)
            st(v.tensor_scalar(out=gstat[:, 4:5], in0=gstat[:, 3:4], scalar1=-1.0,
                               scalar2=1.0, op0=ALU.mult, op1=ALU.add))
            st(v.tensor_scalar(out=wt_[:], in0=eqs[:, 0:8], scalar1=gstat[:, 3:4],
                               scalar2=2.0, op0=ALU.mult, op1=ALU.mult))
            st(v.scalar_tensor_tensor(out=eqs[:, 8:16], in0=eqs[:, 8:16],
                                      scalar=gstat[:, 4:5], in1=wt_[:],
                                      op0=ALU.mult, op1=ALU.bypass))
            st(v.tensor_scalar(out=eqs[:, 8:16], in0=eqs[:, 8:16], scalar1=2.0,
                               scalar2=None, op0=ALU.mult))
            st(v.tensor_tensor(out=wt_[:], in0=wt_[:], in1=eqs[:, 8:16], op=ALU.add))
            S.bump(v, "f1gv", 1)
            S.wait(v, "f1gt", f1gt0 + 2 * tt + 2)
            st(v.tensor_copy(out=wfm[:, tt * 128:(tt + 1) * 128],
                             in_=psP[6][0:8, 8:136]))
            S.bump(v, "f1gv", 1)
    f1gv_tot = S.tok("f1gv")

    @b.block.scalar
    def _(sc):
        for tt in range(NTT):
            S.wait(sc, "f1gd", tt + 1)
            ii = sc.activation(out=gstat[:, 3:4], in_=gstat[:, 2:3], func=AF.Sigmoid)
            S.inc("f1gg", ii, 1)

    @b.block.tensor
    def _(t):
        S.wait(t, "f1gv", f1gv_tot)
        for ss in range(NS):
            S.wait(t, "f1hv", 2 * ss + 1)
            ii = t.matmul(psP[7][0:64, 0:512], b.r8[:],
                          wfb[:], start=True, stop=True)
            S.inc("f1ht", ii, 1)
    f1ht0 = S.tok("f1ht") - NS

    @b.block.vector
    def _(v):
        for ss in range(NS):
            st = _chain(S, v)
            if ss >= 1:
                S.wait(v, "f1ht", f1ht0 + ss)
            st(v.tensor_copy(out=wfb[:], in_=wfm[:, ss * 512:(ss + 1) * 512]))
            S.bump(v, "f1hv", 1)
            S.wait(v, "f1ht", f1ht0 + ss + 1)
            st(v.tensor_tensor(out=hpb[:, ss * 512:(ss + 1) * 512],
                               in0=h1sb[:, ss * 512:(ss + 1) * 512],
                               in1=psP[7][0:64, 0:512], op=ALU.mult))
            S.bump(v, "f1hv", 1)
    f1hp_tot = S.tok("f1hv")

    @b.block.gpsimd
    def _(g):
        for os in range(NOS1):
            if os >= 1:
                S.wait(g, "f1p", os * NTB * 4)
            for kc in range(KD):
                ap = bass.AP(b.WbT.tensor, kc * 128 * D + os * OC1,
                             [[D, 128], [1, OC1]])
                ii = g.dma_start(out=wbp[kc][:], in_=ap)
                S.inc("f1wb", ii, 16)

    @b.block.tensor
    def _(t):
        S.wait(t, "f1hv", f1hp_tot)
        np_ = 0
        for os in range(NOS1):
            S.wait(t, "f1wb", (os + 1) * 16 * KD)
            for tb in range(NTB):
                for tt4 in range(4):
                    n = tb * 4 + tt4
                    if np_ >= 4:
                        S.wait(t, "f1ye", np_ - 3)
                    ii = None
                    for k in range(KD):
                        ii = t.matmul(psP[n % 4][:, 0:OC1],
                                      b.xn1_fm[k][:, n * 128:(n + 1) * 128],
                                      wbp[k][:], start=(k == 0), stop=False)
                    ii = t.matmul(psP[n % 4][:, 0:OC1],
                                  hpb[:, n * 128:(n + 1) * 128],
                                  bcats[:, os * OC1:(os + 1) * OC1],
                                  start=False, stop=True)
                    S.inc("f1p", ii, 1)
                    np_ += 1
    f1p0 = S.tok("f1p") - NOS1 * NTB * 4

    @b.block.vector
    def _(v):
        S.wait(v, "f1bv", t_f1bc)
        np_ = 0
        for os in range(NOS1):
            for tb in range(NTB):
                for tt4 in range(4):
                    n = tb * 4 + tt4
                    jb = np_ % 2
                    S.wait(v, "f1p", f1p0 + np_ + 1)
                    if np_ >= 2:
                        S.wait(v, f"f1ys{jb}", (np_ // 2) * 16)
                    st = _chain(S, v)
                    st(v.tensor_tensor(out=ybnc[jb][:, 0:OC1],
                                       in0=psP[n % 4][:, 0:OC1],
                                       in1=bbasebc[:, os * OC1:(os + 1) * OC1],
                                       op=ALU.add))
                    S.bump(v, "f1ye", 1)
                    S.bump(v, "f1yv", 1)
                    np_ += 1

    @b.block.sync
    def _(y):
        np_ = 0
        for os in range(NOS1):
            for tb in range(NTB):
                for tt4 in range(4):
                    n = tb * 4 + tt4
                    jb = np_ % 2
                    S.wait(y, "f1yv", np_ + 1)
                    ii = y.dma_start(out=b.y1_sp[n * 128:(n + 1) * 128,
                                                 os * OC1:(os + 1) * OC1],
                                     in_=ybnc[jb][:, 0:OC1])
                    S.inc(f"f1ys{jb}", ii, 16)
                    np_ += 1
    b.f1ys_tot = [S.tok("f1ys0"), S.tok("f1ys1")]


def _l1_select(b):
    S, c = b.S, b.cfg
    D, NT, T = c["D"], c["NT"], c["T"]
    NH1m = c["NH1m"]
    SCL1 = 1.0 / math.sqrt(128.0)
    NSQ = T // 512

    cu = b.CUR("RA")
    q1h = cu(BF16, 128, T)
    sq1 = cu(F32, 128, 512)
    nrow = cu(F32, 1, 512)
    knm = cu(F32, 1, 4)
    kn8 = cu(F32, NH1m, 2)
    nq = cu(F32, NH1m, T)
    nqa = cu(F32, NH1m, T)
    nqb = cu(F32, NH1m, T)
    m8 = cu(F32, NH1m, 8)
    m8w = cu(F32, NH1m, 8)
    m8e = cu(F32, NH1m, 8)
    th = cu(F32, NH1m, 2)
    mask = nqa
    invm = nqb
    negidx = cu(F32, NH1m, 64)
    crow = cu(F32, 1, NH1m)
    assert cu.pos[0] <= b.reg["RA"]["size"], cu.pos[0]


    @b.block.sync
    def _(y):
        S.wait(y, "cc", b.L1pcc)
        for side in b.sswitch(y, "y"):
            ro = side * NH1m * 128 * NT
            for h in range(NH1m):
                for w_ in range(2):
                    S.wait(y, "sl_s", (2 * h + w_) * NSQ)
                    for tb in range(2):
                        src = bass.AP(b.ag1_out.tensor,
                                      (tb * 3 * D + w_ * D + h * 128) * NT + ro,
                                      [[NT, 128], [1, NT]])
                        ii = y.dma_start(out=q1h[:, tb * NT:(tb + 1) * NT],
                                         in_=src)
                        S.inc("sl_l", ii, 16)


    for h in range(NH1m):
        sqn0 = S.tok("sl_t")

        @b.block.scalar
        def _(sc, h=h):
            def nstore(st_):
                S.wait(sc, "sl_nr", h * NSQ + st_ + 1)
                ii = sc.dma_start(out=b.nrm_dr[h, st_ * 512:(st_ + 1) * 512],
                                  in_=nrow[0:1, :])
                S.inc("sl_nw", ii, 16)
            for w_ in range(2):
                S.wait(sc, "sl_l", (2 * h + w_ + 1) * 32)
                for st_ in range(NSQ):
                    n = w_ * NSQ + st_
                    if n >= 1:
                        S.wait(sc, "sl_t", sqn0 + n)
                    ii = sc.activation(out=sq1[:], in_=q1h[:, st_ * 512:(st_ + 1) * 512],
                                       func=AF.Square)
                    S.inc("sl_s", ii, 1)
                    if w_ == 0 and st_ >= 1:
                        nstore(st_ - 1)
                    if w_ == 1 and st_ == 0:
                        nstore(NSQ - 1)
            S.wait(sc, "sl_v", h + 1)
            ii = sc.dma_start(out=b.row_dr[4224 + h:4225 + h],
                              in_=knm[0:1, 0:1])
            S.inc("sl_kw", ii, 16)

        sls0 = S.tok("sl_s") - 2 * NSQ

        @b.block.tensor
        def _(t, h=h, sls0=sls0, sqn0=sqn0):
            if h == 0:
                S.wait(t, "f1p", None)
                S.wait(t, "f1gv", None)
            for n in range(2 * NSQ):
                S.wait(t, "sl_s", sls0 + n + 1)
                if n >= 1:
                    S.wait(t, "sl_v2", sqn0 + n)
                ii = t.matmul(b.psP[4][0:1, 0:512], b.ones_f[:], sq1[:],
                              start=True, stop=True)
                S.inc("sl_t", ii, 1)

        @b.block.vector
        def _(v, h=h, sqn0=sqn0):
            st = _chain(S, v)
            if h >= 1:
                S.wait(v, "sl_kw", h * 16)
            st(v.memset(knm[:], 0.0))
            for w_ in range(2):
                for st_ in range(NSQ):
                    n = w_ * NSQ + st_
                    S.wait(v, "sl_t", sqn0 + n + 1)
                    if w_ == 0:
                        S.wait(v, "sl_nw", (h * NSQ + st_) * 16)
                        st(v.tensor_copy(out=nrow[:], in_=b.psP[4][0:1, 0:512]))
                        S.bump(v, "sl_nr", 1)
                    else:
                        st(v.reduce_max(out=knm[0:1, 1:2], in_=b.psP[4][0:1, 0:512],
                                        axis=AX.X))
                        st(v.tensor_tensor(out=knm[0:1, 0:1], in0=knm[0:1, 0:1],
                                           in1=knm[0:1, 1:2], op=ALU.max))
                    S.bump(v, "sl_v2", 1)
            S.bump(v, "sl_v", 1)


    @b.block.sync
    def _(y):
        S.wait(y, "sl_nw", None)
        ii = y.dma_start(out=nq[:], in_=b.nrm_dr)
        S.inc("sl_q", ii, 16)
        S.wait(y, "sl_kw", None)
        ii = y.dma_start(out=kn8[:, 0:1],
                         in_=bass.AP(b.row_dr.tensor, 4224, [[1, NH1m], [1, 1]]))
        S.inc("sl_q", ii, 16)

    @b.block.vector
    def _(v):
        S.wait(v, "sl_q", None)
        st = _chain(S, v)
        st(v.tensor_copy(out=nqa[:], in_=nq[:]))
        cur, oth = nqa, nqb
        for r in range(7):
            st(v.max(out=m8[:], in_=cur[:]))
            st(v.match_replace(out=oth[:], in_to_replace=m8[:], in_values=cur[:],
                               imm_value=-BIG))
            cur, oth = oth, cur
        st(v.max(out=m8[:], in_=cur[:]))
        st(v.tensor_copy(out=m8w[:], in_=m8[:]))
        for r in range(6):
            st(v.reduce_max(out=th[:, 0:1], in_=m8w[:], axis=AX.X))
            if r < 5:
                st(v.tensor_scalar(out=m8e[:], in0=m8w[:], scalar1=th[:, 0:1],
                                   scalar2=None, op0=ALU.is_equal))
                st(v.scalar_tensor_tensor(out=m8w[:], in0=m8e[:], scalar=-BIG,
                                          in1=m8w[:], op0=ALU.mult, op1=ALU.add))
        st(v.tensor_scalar(out=mask[:], in0=nq[:], scalar1=th[:, 0:1],
                           scalar2=None, op0=ALU.is_ge))
        st(v.memset(mask[:, 0:1], 1.0))
        st(v.memset(mask[:, T - 1:T], 1.0))
        st(v.tensor_scalar(out=invm[:], in0=mask[:], scalar1=-1.0, scalar2=1.0,
                           op0=ALU.mult, op1=ALU.add))
        S.bump(v, "sl_inv", 1)
        S.wait(v, "sl_iw", 16)
        for ss in range(T // 512):
            sl = slice(ss * 512, (ss + 1) * 512)
            st(v.scalar_tensor_tensor(out=nqa[:, sl], in0=b.iota[0:NH1m, :],
                                      scalar=float(ss * 512), in1=mask[:, sl],
                                      op0=ALU.add, op1=ALU.mult))
            st(v.scalar_tensor_tensor(out=nqa[:, sl], in0=invm[:, sl], scalar=BIG,
                                      in1=nqa[:, sl], op0=ALU.mult, op1=ALU.add))
        st(v.tensor_scalar(out=nqa[:], in0=nqa[:], scalar1=-1.0, scalar2=None,
                           op0=ALU.mult))
        cur, oth = nqa, nqb
        for r in range(8):
            st(v.max(out=m8[:], in_=cur[:]))
            st(v.tensor_copy(out=negidx[:, r * 8:(r + 1) * 8], in_=m8[:]))
            if r < 7:
                st(v.match_replace(out=oth[:], in_to_replace=m8[:], in_values=cur[:],
                                   imm_value=-BIG))
                cur, oth = oth, cur
        st(v.tensor_scalar(out=negidx[:], in0=negidx[:], scalar1=-1.0, scalar2=None,
                           op0=ALU.mult))
        S.bump(v, "sl_idx", 1)
        st(v.reduce_max(out=kn8[:, 1:2], in_=nq[:], axis=AX.X))
        st(v.tensor_tensor(out=kn8[:, 0:1], in0=kn8[:, 0:1], in1=kn8[:, 1:2],
                           op=ALU.add))
        st(v.tensor_scalar(out=kn8[:, 0:1], in0=kn8[:, 0:1], scalar1=-0.5 * SCL1,
                           scalar2=None, op0=ALU.mult))
        S.bump(v, "sl_c", 1)

    @b.block.sync
    def _(y):
        S.wait(y, "sl_inv", None)
        ii = y.dma_start(out=b.inv_dr, in_=invm[:])
        S.inc("sl_iw", ii, 16)
        S.wait(y, "sl_idx", None)
        ii = y.dma_start(out=b.idx_dr, in_=negidx[:])
        S.inc("sl_iw", ii, 16)
        S.wait(y, "sl_c", None)
        ii = y.dma_start(out=b.row_dr[4096:4096 + NH1m], in_=kn8[:, 0:1])
        S.inc("sl_iw", ii, 16)
        S.wait(y, "sl_iw", None)
        ii = y.dma_start(out=crow[:], in_=bass.AP(b.row_dr.tensor, 4096,
                                                  [[NH1m, 1], [1, NH1m]]))
        S.inc("sl_cr", ii, 16)

    @b.block.tensor
    def _(t):
        S.wait(t, "sl_cr", None)
        ii = t.matmul(b.psP[6][:, 0:NH1m], b.onesr_f[:], crow[:],
                      start=True, stop=True)
        S.inc("sl_ct", ii, 1)
    t_slct = S.tok("sl_ct")

    @b.block.vector
    def _(v):
        S.wait(v, "sl_ct", t_slct)
        st = _chain(S, v)
        st(v.tensor_copy(out=b.biasC1[:, 0:NH1m], in_=b.psP[6][:, 0:NH1m]))
        S.bump(v, "sl_cb", 1)
    b.t_slcb = S.tok("sl_cb")
    b.sel_idx_done = S.tok("sl_iw")


def _attn1(b):
    S, c = b.S, b.cfg
    D, NT, T = c["D"], c["NT"], c["T"]
    NH1m, NKT = c["NH1m"], c["NKT"]
    SCL1 = 1.0 / math.sqrt(128.0)
    psP = b.psP

    cu = b.CUR("RA")
    q1h = [cu(BF16, 128, T) for j in range(2)]
    k1h = [cu(BF16, 128, T) for j in range(2)]
    vt1 = [cu(BF16, 128, 129 * NKT) for j in range(2)]
    idxc = [cu(F32, 64, 1) for j in range(2)]
    qtm = [cu(BF16, 128, 128) for j in range(2)]
    stile = [cu(BF16, 128, 64) for j in range(2)]
    qsel = cu(BF16, 128, 64)
    Pg = [cu(BF16, 128, 64) for j in range(2)]
    ctxN = cu(BF16, 64, 129)
    cgl = cu(F32, 64, 129)
    Pl = [cu(BF16, 128, 128) for j in range(3)]
    cll = [cu(F32, 128, 128) for j in range(2)]
    lr_ = cu(F32, 2, 256)
    lro = cu.last_off
    lrow = [b.RV("RA", F32, lro + jj * 512, 1, 128) for jj in range(2)]
    cb1 = [cu(BF16, 128, 128) for j in range(2)]
    assert cu.pos[0] <= b.reg["RA"]["size"], cu.pos[0]
    ssel = [b.RV("RB", BF16, j * 2 * T, 64, T) for j in range(2)]
    invr = [b.RV("RB", F32, b.reg["RB"]["size"] - (2 - jj) * 4 * T, 1, T)
            for jj in range(2)]
    vb2 = [b.RV("RW", BF16, sl * 256, 128, 128) for sl in range(4)]
    S.revive(*[f"avb{sl}" for sl in range(4)], *[f"acp{sl}" for sl in range(4)])

    @b.block.vector
    def _(v):
        b.gate(v)
        for j in range(2):
            ap = bass.AP(vt1[j].tensor, vt1[j].offset + 128,
                         [[vt1[j].tensor.shape[1], 128], [129, NKT], [1, 1]])
            ii = v.memset(ap, 1.0)
            S.inc("a1vo", ii, 1)
        for jj in range(3):
            ii = v.memset(Pl[jj][:], 0.0)
            S.inc("a1vo", ii, 1)

    head_end = []

    @b.block.sync
    def _(y):
        b.gate(y)
        for side in b.sswitch(y, "y"):
            ro = side * NH1m * 128 * NT
            for h in range(NH1m):
                j = h % 2
                if h == 0:
                    S.wait(y, "sl_iw", b.sel_idx_done)
                if h >= 2:
                    S.wait(y, "a1ce", h - 1)
                for tb in range(2):
                    src = bass.AP(b.ag1_out.tensor,
                                  (tb * 3 * D + h * 128) * NT + ro,
                                  [[NT, 128], [1, NT]])
                    ii = y.dma_start(out=q1h[j][:, tb * NT:(tb + 1) * NT],
                                     in_=src)
                    S.inc(f"a1l{j}", ii, 16)
                    src = bass.AP(b.ag1_out.tensor,
                                  (tb * 3 * D + D + h * 128) * NT + ro,
                                  [[NT, 128], [1, NT]])
                    ii = y.dma_start(out=k1h[j][:, tb * NT:(tb + 1) * NT],
                                     in_=src)
                    S.inc(f"a1l{j}", ii, 16)
                for kt in range(NKT):
                    tb = kt * 128 // NT
                    col = kt * 128 - tb * NT
                    n = h * NKT + kt
                    sl = n % 4
                    src = bass.AP(b.ag1_out.tensor,
                                  (tb * 3 * D + 2 * D + h * 128) * NT + ro + col,
                                  [[NT, 128], [1, 128]])
                    if n >= 4:
                        S.wait(y, f"acp{sl}", (n // 4) * 16)
                    ii = y.dma_start_transpose(vb2[sl][:], src)
                    S.inc(f"avb{sl}", ii, 16)
                ii = y.dma_start(out=idxc[j][:],
                                 in_=bass.AP(b.idx_dr.tensor, h * 64,
                                             [[1, 64], [1, 1]]))
                S.inc(f"a1l{j}", ii, 16)
                ii = y.dma_start(out=invr[j][:], in_=b.inv_dr[h:h + 1, :])
                S.inc(f"a1l{j}", ii, 16)
                S.wait(y, "a1sb", h + 1)
                for kt in range(NKT):
                    jj = kt % 2
                    if kt >= 2:
                        S.wait(y, "a1gt", h * NKT + kt - 1)
                    ii = y.dma_start_transpose(qtm[jj][:],
                                               q1h[j][:, kt * 128:(kt + 1) * 128])
                    S.inc(f"a1tq{jj}", ii, 16)
                    ii = y.dma_start_transpose(stile[jj][:],
                                               ssel[j][:, kt * 128:(kt + 1) * 128])
                    S.inc(f"a1tq{jj}", ii, 16)
                for qt in range(NKT):
                    ng = h * NKT + qt
                    S.wait(y, "a1ee", ng + 1)
                    ii = y.dma_start(out=b.ag1c_in[h * 128:(h + 1) * 128,
                                                   qt * 128:(qt + 1) * 128],
                                     in_=cb1[qt % 2][:])
                    S.inc(f"a1cw{qt % 2}", ii, 16)

    for h in range(NH1m):
        j = h % 2

        @b.block.vector
        def _(v, h=h, j=j):
            st = _chain(S, v)
            for kt in range(NKT):
                n = h * NKT + kt
                sl = n % 4
                S.wait(v, f"avb{sl}", (n // 4 + 1) * 16)
                st(v.tensor_copy(out=vt1[j][:, kt * 129:kt * 129 + 128],
                                 in_=vb2[sl][:]))
                S.bump(v, f"acp{sl}", 16)

        @b.block.vector
        def _(v, h=h, j=j):
            S.wait(v, f"a1l{j}", (h // 2 + 1) * (4 + 2) * 16)
            if h >= 2:
                S.wait(v, "a1ge", h - 1)
                S.wait(v, "a1sc", (h - 1) * NKT)
            st = _chain(S, v)
            for ss in range(T // 512):
                st(v.tensor_scalar(out=ssel[j][:, ss * 512:(ss + 1) * 512],
                                   in0=b.iota[0:64, :],
                                   scalar1=float(ss * 512),
                                   scalar2=idxc[j][:, 0:1],
                                   op0=ALU.add, op1=ALU.is_equal))
            S.bump(v, "a1sb", 1)

        a1g0 = S.tok("a1gt")

        @b.block.tensor
        def _(t, h=h, j=j, a1g0=a1g0):
            if h == 0:
                S.wait(t, "sl_v2", None)
                S.wait(t, "a1vo", 5)
                S.wait(t, "sl_cb", b.t_slcb)
            for kt in range(NKT):
                jj = kt % 2
                S.wait(t, f"a1tq{jj}", (h * NKT // 2 + kt // 2 + 1) * 32)
                ii = t.matmul(psP[4][:, 0:64], qtm[jj][:], stile[jj][:],
                              start=(kt == 0), stop=(kt == NKT - 1))
                S.inc("a1gt", ii, 1)

        @b.block.vector
        def _(v, h=h):
            S.wait(v, "a1gt", None)
            st = _chain(S, v)
            st(v.tensor_copy(out=qsel[:], in_=psP[4][:, 0:64]))
            S.bump(v, "a1ge", 1)

        a1x0 = S.tok("a1gx")
        a1c0 = S.tok("a1gc")

        @b.block.tensor
        def _(t, h=h, j=j, a1x0=a1x0):
            S.wait(t, "a1ge", h + 1)
            for sl_ in range(4):
                S.wait(t, f"acp{sl_}", (h + 1) * (NKT // 4) * 16)
            for kt in range(NKT):
                if kt >= 2:
                    S.wait(t, "a1gx", a1x0 + kt - 1)
                ii = t.matmul(psP[kt % 2][:, 0:64],
                              k1h[j][:, kt * 128:(kt + 1) * 128],
                              qsel[:], start=True, stop=True)
                S.inc("a1gs", ii, 1)
                S.wait(t, "a1gx", a1x0 + kt + 1)
                ii = t.matmul(psP[2][0:64, 0:129],
                              Pg[kt % 2][:], vt1[j][:, kt * 129:(kt + 1) * 129],
                              start=(kt == 0), stop=(kt == NKT - 1))
                S.inc("a1gc", ii, 1)

        a1s0 = S.tok("a1gs") - NKT

        @b.block.scalar
        def _(sc, h=h, a1s0=a1s0, a1c0=a1c0):
            for kt in range(NKT):
                S.wait(sc, "a1gs", a1s0 + kt + 1)
                if kt >= 2:
                    S.wait(sc, "a1gc", a1c0 + kt - 1)
                ii = sc.activation(out=Pg[kt % 2][:], in_=psP[kt % 2][:, 0:64],
                                   func=AF.Exp, bias=b.biasC1[:, h:h + 1],
                                   scale=SCL1)
                S.inc("a1gx", ii, 1)

        @b.block.vector
        def _(v, h=h):
            S.wait(v, "a1gc", None)
            st = _chain(S, v)
            st(v.tensor_copy(out=cgl[:], in_=psP[2][0:64, 0:129]))
            st(v.reciprocal(out=cgl[:, 128:129], in_=cgl[:, 128:129]))
            st(v.tensor_scalar(out=ctxN[:, 0:128], in0=cgl[:, 0:128],
                               scalar1=cgl[:, 128:129], scalar2=None, op0=ALU.mult))
            S.bump(v, "a1cn", 1)

        loc = []
        for qt in range(NKT):
            ktts = [k_ for k_ in (qt - 1, qt, qt + 1) if 0 <= k_ < NKT]
            loc.append((qt, ktts))
        a1lx0 = S.tok("a1lx")
        a1ls0 = S.tok("a1ls")
        a1lc0 = S.tok("a1lc")
        nen = 0
        entl = []
        for (qt, ktts) in loc:
            for ktt in ktts:
                entl.append((qt, ktt))
                nen += 1

        @b.block.tensor
        def _(t, h=h, j=j, loc=loc, a1lx0=a1lx0, entl=entl):
            S.wait(t, "a1cn", h + 1)
            ne = 0
            ng = 0
            for (qt, ktts) in loc:
                for ktt in ktts:
                    if ne >= 2:
                        S.wait(t, "a1lx", a1lx0 + ne - 1)
                    ii = t.matmul(psP[ne % 2][:, 0:128],
                                  k1h[j][:, ktt * 128:(ktt + 1) * 128],
                                  q1h[j][:, qt * 128:(qt + 1) * 128],
                                  start=True, stop=True)
                    S.inc("a1ls", ii, 1)
                    S.wait(t, "a1lx", a1lx0 + ne + 1)
                    ii = t.matmul(psP[2][:, 0:128],
                                  vt1[j][:, ktt * 129:ktt * 129 + 128],
                                  Pl[ne % 3][:],
                                  start=(ktt == ktts[0]), stop=(ktt == ktts[-1]))
                    S.inc("a1lc", ii, 1)
                    ii = t.matmul(psP[3][0:1, 0:128],
                                  vt1[j][:, ktt * 129 + 128:ktt * 129 + 129],
                                  Pl[ne % 3][:],
                                  start=(ktt == ktts[0]), stop=(ktt == ktts[-1]))
                    S.inc("a1lc", ii, 1)
                    ne += 1
                S.wait(t, "a1sb", h + 1)
                if h * NKT + ng >= 1:
                    S.wait(t, "a1ee", h * NKT + ng)
                ii = t.matmul(psP[5][:, 0:128], ctxN[:, 0:128],
                              ssel[j][:, qt * 128:(qt + 1) * 128],
                              start=True, stop=True)
                S.inc("a1sc", ii, 1)
                S.wait(t, "a1rx", h * NKT + ng + 1)
                ii = t.matmul(psP[6][:, 0:128], b.onesr_f[:], lrow[ng % 2][:],
                              start=True, stop=True)
                S.inc("a1rb", ii, 1)
                ng += 1

        @b.block.scalar
        def _(sc, h=h, entl=entl, a1ls0=a1ls0, a1lc0=a1lc0):
            for ne, (qt, ktt) in enumerate(entl):
                S.wait(sc, "a1ls", a1ls0 + ne + 1)
                if ne >= 3:
                    S.wait(sc, "a1lc", a1lc0 + 2 * (ne - 2))
                if ktt < qt:
                    dst = Pl[ne % 3][64:128, :]
                    srcp = psP[ne % 2][64:128, 0:128]
                    bia = b.biasC1[64:128, h:h + 1]
                elif ktt > qt:
                    dst = Pl[ne % 3][0:64, :]
                    srcp = psP[ne % 2][0:64, 0:128]
                    bia = b.biasC1[0:64, h:h + 1]
                else:
                    dst = Pl[ne % 3][:]
                    srcp = psP[ne % 2][:, 0:128]
                    bia = b.biasC1[:, h:h + 1]
                ii = sc.activation(out=dst, in_=srcp, func=AF.Exp,
                                   bias=bia, scale=SCL1)
                S.inc("a1lx", ii, 1)

        @b.block.vector
        def _(v, h=h, loc=loc, a1lc0=a1lc0):
            ng = 0
            for (qt, ktts) in loc:
                last = a1lc0 + 2 * sum(len(k_) for (q_, k_) in loc[:loc.index((qt, ktts)) + 1])
                S.wait(v, "a1lc", last)
                if h * NKT + ng >= 2:
                    S.wait(v, f"a1cw{ng % 2}", ((h * NKT + ng) // 2) * 16)
                st = _chain(S, v)
                st(v.tensor_copy(out=cll[ng % 2][:], in_=psP[2][:, 0:128]))
                st(v.tensor_copy(out=lrow[ng % 2][:], in_=psP[3][0:1, 0:128]))
                S.bump(v, "a1le", 1)
                st(v.reciprocal(out=lrow[ng % 2][:], in_=lrow[ng % 2][:]))
                st(v.tensor_tensor(out=lrow[ng % 2][:], in0=lrow[ng % 2][:],
                                   in1=invr[h % 2][:, qt * 128:(qt + 1) * 128],
                                   op=ALU.mult))
                S.bump(v, "a1rx", 1)
                S.wait(v, "a1rb", h * NKT + ng + 1)
                S.wait(v, "a1sc", h * NKT + ng + 1)
                st(v.tensor_tensor(out=cll[ng % 2][:], in0=cll[ng % 2][:],
                                   in1=psP[6][:, 0:128], op=ALU.mult))
                st(v.tensor_tensor(out=cb1[ng % 2][:], in0=cll[ng % 2][:],
                                   in1=psP[5][:, 0:128], op=ALU.add))
                S.bump(v, "a1ee", 1)
                ng += 1

        @b.block.vector
        def _(v, h=h):
            S.wait(v, "a1ee", (h + 1) * NKT)
            S.bump(v, "a1ce", 1)
        head_end.append(S.tok("a1ce"))
    b.c1w_tot = [S.tok("a1cw0"), S.tok("a1cw1")]


def _final(b):
    S, c = b.S, b.cfg
    D, NT, T = c["D"], c["NT"], c["T"]
    KD, NTT = c["KD"], c["NTT"]

    cu = b.CUR("RA")
    x1t = [cu(F32, 128, D) for j in range(2)]
    y1t = [cu(F32, 128, D) for j in range(2)]
    ctm = [cu(BF16, 128, D) for j in range(2)]
    ctf = cu(F32, 128, D)
    ot = [cu(F32, 128, D) for j in range(2)]
    assert cu.pos[0] <= b.reg["RA"]["size"], cu.pos[0]

    @b.block.gpsimd
    def _(g):
        S.wait(g, "cc", b.agc1cc)
        for side in b.sswitch(g, "g"):
            co = side * NT
            for kc in range(KD):
                ap = bass.AP(b.ag1c_out.tensor, co + kc * 128 * T,
                             [[T, 128], [1, NT]])
                ii = g.dma_start(out=b.ctx1m_dr[kc * 128:(kc + 1) * 128, :],
                                 in_=ap)
                S.inc("fcm", ii, 16)
    fcm_tot = S.tok("fcm")

    @b.block.sync
    def _(y):
        for jj in range(2):
            S.wait(y, f"f1ys{jj}", b.f1ys_tot[jj])
        S.wait(y, "fcm", fcm_tot)
        for tt in range(NTT):
            j = tt % 2
            if tt >= 2:
                S.wait(y, "fv", 2 * (tt - 1))
            ii = y.dma_start(out=x1t[j][:], in_=b.x1_sp[tt * 128:(tt + 1) * 128, :])
            S.inc(f"fl{j}", ii, 16)
            ii = y.dma_start(out=y1t[j][:], in_=b.y1_sp[tt * 128:(tt + 1) * 128, :])
            S.inc(f"fl{j}", ii, 16)
            for k in range(KD):
                ii = y.dma_start_transpose(ctm[j][:, k * 128:(k + 1) * 128],
                                           b.ctx1m_dr[k * 128:(k + 1) * 128,
                                                      tt * 128:(tt + 1) * 128])
                S.inc(f"fl{j}", ii, 16)

    fv0 = S.tok("fv")

    @b.block.vector
    def _(v):
        for tt in range(NTT):
            j = tt % 2
            S.wait(v, f"fl{j}", (tt // 2 + 1) * 16 * (2 + KD))
            if tt >= 2:
                S.wait(v, f"fo{j}", (tt // 2) * 16)
            st = _chain(S, v)
            st(v.tensor_copy(out=ctf[:], in_=ctm[j][:]))
            st(v.tensor_tensor(out=ot[j][:], in0=x1t[j][:], in1=y1t[j][:],
                               op=ALU.add))
            st(v.tensor_tensor(out=ot[j][:], in0=ot[j][:], in1=ctf[:], op=ALU.add))
            S.bump(v, "fv", 2)

    @b.block.scalar
    def _(sc):
        for tt in range(NTT):
            j = tt % 2
            S.wait(sc, "fv", fv0 + 2 * (tt + 1))
            ii = sc.dma_start(out=b.out[tt * 128:(tt + 1) * 128, :], in_=ot[j][:])
            S.inc(f"fo{j}", ii, 16)
        S.wait(sc, "fo0", None)
        S.wait(sc, "fo1", None)



def _finish(b):
    S, c = b.S, b.cfg

    outw = S.m.get("outw")

    @b.block.sync
    def _(y):
        for h, v in S.phys_state():
            if outw is not None and h is outw[0][0]:
                continue
            y.wait_ge(h, v)

    if b.upto != "all":
        z_ = b.sb("zot", [128, c["D"]], F32)

        @b.block.vector
        def _(v):
            ii = v.memset(z_[:], 0.0)
            b.t_z = S.inc("v", ii, 1)

        @b.block.sync
        def _(y):
            S.wait(y, "v", b.t_z)
            for tt in range(c["NTT"]):
                ii = y.dma_start(out=b.out[tt * 128:(tt + 1) * 128, :], in_=z_[:])
                S.inc("outw", ii, 16)
            S.wait(y, "outw", None)


def make_consts(cfg):
    T, TS = cfg["T"], cfg["TS"]
    TRIV = TS // 128
    IOTA = np.tile(np.arange(512, dtype=np.float32), (64, 1))
    ID128 = np.eye(128, dtype=np.float32)

    tri = np.zeros((TRIV * 128, TS), np.float32)
    for v in range(TRIV):
        for kj in range(128):
            tri[v * 128 + kj, :] = (np.arange(TS) >= (128 * v + kj)).astype(np.float32)
    TRI = tri.astype(ml_dtypes.bfloat16)
    R8 = np.zeros((8, 64), np.float32)
    for e in range(8):
        R8[e, e * 8:(e + 1) * 8] = 1.0
    return dict(IOTA=IOTA, ID128=ID128, TRI=TRI,
                R8=R8.astype(ml_dtypes.bfloat16))


def make_inmaps(cfg, A):
    c = cfg
    NT, D = c["NT"], c["D"]
    NHm, NH1m, HD = c["NHm"], c["NH1m"], c["HD"]
    consts = make_consts(c)
    base = dict(
        wk=A["L0_attn_wk"], bk=A["L0_attn_bk"], wq=A["L0_attn_wq"], bq=A["L0_attn_bq"],
        wv=A["L0_attn_wv"], bv=A["L0_attn_bv"], wkl=A["L0_attn_wkl"], bkl=A["L0_attn_bkl"],
        wql=A["L0_attn_wql"], bql=A["L0_attn_bql"], wvl=A["L0_attn_wvl"], bvl=A["L0_attn_bvl"],
        wo=A["L0_attn_wo"], bo=A["L0_attn_bo"],
        w1=A["L0_ffn_w1"], b1=A["L0_ffn_b1"], w2=A["L0_ffn_w2"], b2=A["L0_ffn_b2"],
        w2T=np.ascontiguousarray(np.transpose(A["L0_ffn_w2"])),
        Wq1=A["L1_attn_Wq"], bq1=A["L1_attn_bq"], Wk1=A["L1_attn_Wk"], bk1=A["L1_attn_bk"],
        Wv1=A["L1_attn_Wv"], bv1=A["L1_attn_bv"],
        WbT=np.ascontiguousarray(np.transpose(A["L1_ffn_W_base"])),
        bbase=A["L1_ffn_b_base"],
        WrT=np.ascontiguousarray(np.transpose(A["L1_ffn_W_router"])),
        AT=np.ascontiguousarray(np.asarray(A["L1_ffn_A_w"]).reshape(64, D).T),
        Bcat=np.ascontiguousarray(np.asarray(A["L1_ffn_B_w"]).transpose(0, 2, 1).reshape(64, D)),
        **consts,
    )
    base = {k: np.ascontiguousarray(v) for k, v in base.items()}
    x = A["L0_attn_x"]
    maps = []
    for core in range(c["NCORE"]):
        bidx, s = core // 2, core % 2
        m = dict(base)
        m["x0"] = np.ascontiguousarray(x[bidx, s * NT:(s + 1) * NT, :])
        m["soff"] = np.array([[s, s * NHm * HD * NT, s * NH1m * 128 * NT,
                               s * NT, 0, 0, 0, 0]], np.int32)
        maps.append(m)
    return maps


def kernel(**inputs):
    cfg = make_cfg(small=False)
    nc = build_core(cfg, debug=0, upto="all")
    in_maps = make_inmaps(cfg, inputs)
    res = run_bass_kernel_spmd(nc, in_maps, core_ids=list(range(cfg["NCORE"])))
    NT, D, B = cfg["NT"], cfg["D"], cfg["B"]
    out = np.empty((B, 2 * NT, D), np.float32)
    for core in range(cfg["NCORE"]):
        bidx, s = core // 2, core % 2
        out[bidx, s * NT:(s + 1) * NT, :] = res.results[core]["out"]
    return out


# version 09
# speedup vs baseline: 1.1442x; 1.0452x over previous
import math
import numpy as np
import ml_dtypes
from contextlib import ExitStack

import concourse.bass as bass
import concourse.mybir as mybir
from concourse.bass_utils import run_bass_kernel_spmd

F32 = mybir.dt.float32
BF16 = mybir.dt.bfloat16
I32 = mybir.dt.int32
AF = mybir.ActivationFunctionType
ALU = mybir.AluOpType
AX = mybir.AxisListType

MAGIC = 12582912.0
LNEPS = 1e-5
BIG = 1.0e9


def make_cfg(small=False):
    if small == "med":
        D, DL, I, NT, B = 1024, 512, 2048, 1024, 1
    elif small:
        D, DL, I, NT, B = 256, 128, 512, 512, 1
    else:
        D, DL, I, NT, B = 2048, 1024, 8192, 2048, 4
    c = dict(D=D, DL=DL, I=I, NT=NT, T=2 * NT, B=B,
             NH=DL // 64, HD=64, NH1=D // 128, TOPK=64, TS=512)
    c["KD"] = D // 128
    c["KI"] = I // 128
    c["KL"] = DL // 128
    c["NTT"] = NT // 128
    c["NS"] = NT // c["TS"]
    c["TQ"] = c["T"] // c["TS"]
    c["NKT"] = c["T"] // 128
    c["NHm"] = c["NH"] // 2
    c["NH1m"] = c["NH1"] // 2
    c["NCORE"] = 2 * B
    return c


def _is_sw_inc(instr):
    try:
        ins = instr.ins
    except Exception:
        return False
    cn = type(ins).__name__
    return ins.engine == mybir.EngineType.Pool and (
        "DMA" in cn or "Collective" in cn or "TensorCopy" in cn)


class Sems:

    SW_NAMES = {'L0pw0', 'L0pw1', 'L1pw0', 'L1pw1', 'SWN', 'f1w', 'f1wb', 'fcm', 'm1w0', 'm1w1', 'm2qw0', 'm2qw1', 'm2wo', 'w2l'}

    def __init__(self, nc, ctx):
        self.nc, self.ctx, self.m = nc, ctx, {}
        self.free, self.dead = [], set()
        self.serial = 0

    def h(self, name):
        if name in self.dead:
            raise KeyError(f"semaphore name {name!r} was retired")
        if name not in self.m:
            if name in self.SW_NAMES or not self.free:
                self.serial += 1
                ph = [self.ctx.enter_context(
                    self.nc.semaphore(f"{name}_{self.serial}")), 0, 0]
            else:
                ph = self.free.pop()
            self.m[name] = [ph, 0]
        return self.m[name]

    def inc(self, name, instr, n=1):
        e = self.h(name)
        ph = e[0]
        if _is_sw_inc(instr):
            if ph[2] == 1 or (ph[2] == 0 and ph[1] - e[1] != 0):
                raise RuntimeError(f"sem {name!r} needs a fresh SWDGE phys "
                                   f"sem: add to SW_NAMES")
            ph[2] = 2
        else:
            if ph[2] == 2:
                raise RuntimeError(f"sem {name!r} mixes SWDGE and other "
                                   f"updaters")
            ph[2] = 1
        instr.then_inc(ph[0], n)
        ph[1] += n
        e[1] += n
        return e[1]

    def tok(self, name):
        return self.h(name)[1]

    def wait(self, eng, name, val=None):
        e = self.h(name)
        lv = e[1] if val is None else val
        eng.wait_ge(e[0][0], e[0][1] - e[1] + lv)

    def bump(self, eng, name, n=1):
        e = self.h(name)
        if e[0][2] == 2:
            raise RuntimeError(f"sem {name!r} mixes SWDGE and engine bumps")
        e[0][2] = 1
        eng.sem_inc(e[0][0], n)
        e[0][1] += n
        e[1] += n
        return e[1]

    def revive(self, *names):
        for nm in names:
            self.dead.discard(nm)

    def retire(self, *names):
        for nm in names:
            if nm in self.m:
                ph = self.m.pop(nm)[0]
                if ph[2] != 2:
                    self.free.append(ph)
                self.dead.add(nm)

    def save(self):
        return {n: (e[0], e[1], e[0][1]) for n, e in self.m.items()}

    def delta(self, snap):
        d = {}
        for n, e in self.m.items():
            base = snap[n][1] if n in snap else 0
            if e[1] != base:
                d[n] = e[1] - base
        return d

    def restore(self, snap):
        for n, e in list(self.m.items()):
            if n in snap:
                _, loc, phv = snap[n]
                e[0][1] = phv
                e[1] = loc
            else:
                e[0][1] -= e[1]
                e[1] = 0

    def phys_state(self):
        seen, out = set(), []
        for e in self.m.values():
            if id(e[0]) not in seen:
                seen.add(id(e[0]))
                out.append(e[0])
        for ph in self.free:
            if id(ph) not in seen:
                seen.add(id(ph))
                out.append(ph)
        return [(ph[0], ph[1]) for ph in out]


def _chain(S, eng, name="vch"):
    def step(instr):
        t = S.inc(name, instr, 1)
        S.wait(eng, name, t)
        return t
    return step


class Bld:
    def mark_gate(self):
        cc = self.S.m.get("cc")
        gd = [(h, v) for (h, v) in self.S.phys_state()
              if cc is None or h is not cc[0][0]]
        self.pending_gates.append(gd)

    def gate(self, eng):
        for gd in self.pending_gates[-1:]:
            for h, v in gd:
                eng.wait_ge(h, v)

    def sswitch(self, eng, ename):
        snap = d0 = None
        for side in self.nc.Switch(engines=[eng], index=[self.sreg[ename]], n=2):
            if side == 0:
                snap = self.S.save()
            else:
                d0 = self.S.delta(snap)
                self.S.restore(snap)
            yield side
        d1 = self.S.delta(snap)
        assert d0 == d1, ("switch bodies differ in sem deltas", d0, d1)


def build_core(cfg, debug=0, upto="all"):
    nc = bass.Bass("TRN2", target_bir_lowering=False, num_devices=cfg["NCORE"])
    b = Bld()
    b.cfg, b.debug, b.upto = cfg, debug, upto
    c = cfg
    D, DL, I, NT, T, TS = c["D"], c["DL"], c["I"], c["NT"], c["T"], c["TS"]
    NHm, NH1m, HD = c["NHm"], c["NH1m"], c["HD"]
    MYH0, MYH1 = NHm * HD, NH1m * 128
    TRIV = TS // 128

    inp = lambda n, sh, dt=F32: nc.dram_tensor(n, sh, dt, kind="ExternalInput").ap()
    DBG = {"xn32_sp", "h_sp", "x1_sp", "y1_sp", "q2_dr", "ctx1m_dr", "qw2T_dr",
           "nrm_dr", "idx_dr", "inv_dr", "dbg_dr"} if debug else set()
    dram = lambda n, sh, dt: nc.dram_tensor(
        n, sh, dt, kind="ExternalOutput" if n in DBG else "Internal").ap()

    b.x0 = inp("x0", [NT, D])
    for n_, sh in [("wk", [D, D]), ("wq", [D, D]), ("wv", [D, D]),
                   ("wkl", [D, DL]), ("wql", [D, DL]), ("wvl", [D, DL]),
                   ("wo", [DL, D]), ("w1", [I, D]), ("w2", [D, I]),
                   ("Wq1", [D, D]), ("Wk1", [D, D]), ("Wv1", [D, D]),
                   ("w2T", [I, D]), ("WbT", [D, D]), ("WrT", [D, 8]),
                   ("AT", [D, 64]), ("Bcat", [64, D])]:
        setattr(b, n_, inp(n_, sh))
    for n_ in ["bk", "bq", "bv", "bo", "b2", "bq1", "bk1", "bv1", "bbase"]:
        setattr(b, n_, inp(n_, [D]))
    for n_ in ["bkl", "bql", "bvl"]:
        setattr(b, n_, inp(n_, [DL]))
    b.b1 = inp("b1", [I])
    b.soff = inp("soff", [1, 8], I32)
    b.IOTA = inp("IOTA", [64, 512])
    b.ID128 = inp("ID128", [128, 128])
    b.TRI = inp("TRI", [TRIV * 128, TS], BF16)
    b.R8 = inp("R8", [8, 64], BF16)

    b.out = nc.dram_tensor("out", [NT, D], F32, kind="ExternalOutput").ap()
    b.dbg = {}

    b.xn32_sp = dram("xn32_sp", [NT, D], F32)
    b.h_sp = dram("h_sp", [I, NT], F32)
    b.qw2T_dr = dram("qw2T_dr", [I, D], BF16)
    b.q2_dr = dram("q2_dr", [I, NT], BF16)
    b.ctx1m_dr = dram("ctx1m_dr", [D, NT], BF16)
    b.x1_sp = dram("x1_sp", [NT, D], F32)
    b.y1_sp = dram("y1_sp", [NT, D], F32)
    b.row_dr = dram("row_dr", [8192], F32)
    b.dbg_dr = dram("dbg_dr", [4 * 128, NT], F32)
    b.nrm_dr = dram("nrm_dr", [max(NH1m, 1), T], F32)
    b.inv_dr = dram("inv_dr", [max(NH1m, 1), T], F32)
    b.idx_dr = dram("idx_dr", [max(NH1m, 1), 64], F32)
    b.ag0_in = dram("ag0_in", [3 * DL, NT], BF16)
    b.ag0_out = dram("ag0_out", [2 * 3 * DL, NT], BF16)
    b.ag0c_in = dram("ag0c_in", [MYH0, T], BF16)
    b.ag0c_out = dram("ag0c_out", [2 * MYH0, T], BF16)
    b.ag1_in = dram("ag1_in", [3 * D, NT], BF16)
    b.ag1_out = dram("ag1_out", [2 * 3 * D, NT], BF16)
    b.ag1c_in = dram("ag1c_in", [MYH1, T], BF16)
    b.ag1c_out = dram("ag1c_out", [2 * MYH1, T], BF16)
    b.ar_in = dram("ar_in", [1, 128], F32)
    b.ar_out = dram("ar_out", [1, 128], F32)
    b.PAIRS = [[2 * i, 2 * i + 1] for i in range(c["NCORE"] // 2)]
    b.ALLG = [list(range(c["NCORE"]))]

    ctx = ExitStack()
    b.ctx, b.nc = ctx, nc
    S = Sems(nc, ctx)
    b.S = S
    b.sb = lambda n, sh, dt: ctx.enter_context(nc.sbuf_tensor(n, sh, dt))
    b.ps = lambda n, sh, dt=F32: ctx.enter_context(nc.psum_tensor(n, sh, dt))
    PS = nc.SBUF_PARTITION_SIZE_BYTES - 64
    RSZ = {"RA": 67584, "RB": 65536, "RC": 57344, "RW": 12288}
    MB = PS - sum(RSZ.values())
    base = MB
    b.reg = {}
    for nm in ["RW", "RC", "RB", "RA"]:
        sz = RSZ[nm]
        b.reg[nm] = {
            "f": nc.alloc_sbuf_tensor_at(f"{nm}_f", [128, sz // 4], F32, offset=base),
            "b": nc.alloc_sbuf_tensor_at(f"{nm}_b", [128, sz // 2], BF16, offset=base),
            "i": nc.alloc_sbuf_tensor_at(f"{nm}_i", [128, sz // 4], I32, offset=base),
            "size": sz,
        }
        base += sz
    b.auto_limit = MB

    def RV(region, dt, off, p, cols, p0=0):
        r = b.reg[region]
        esz = 2 if dt == BF16 else 4
        t = r["b"] if dt == BF16 else (r["i"] if dt == I32 else r["f"])
        assert off % esz == 0 and off + cols * esz <= r["size"], (region, off, cols)
        return t[p0:p0 + p, off // esz: off // esz + cols]
    b.RV = RV

    def CUR(region, start=0):
        pos = [start]
        def take(dt, p, cols, p0=0, align=64):
            esz = 2 if dt == BF16 else 4
            off = (pos[0] + align - 1) // align * align
            pos[0] = off + cols * esz
            take.last_off = off
            return RV(region, dt, off, p, cols, p0)
        take.pos = pos
        return take
    b.CUR = CUR
    b.sreg = {}
    b.pending_gates = []

    KEEP = {'L0pe', 'L1pe', 'acc2', 'aee', 'am', 'anv', 'ax', 'cc', 'cst', 'f1ys0', 'f1ys1', 'ln1v', 'm1e', 'm1hx0', 'm1hx1', 'm1hx2', 'm1hx3', 'm2eB', 'm2ti', 'p0sp0', 'p0sp1', 's', 'sl_cb', 'sl_iw', 'sl_v2', 't', 'v', 'vch', 'w2bc', 'w2st0', 'w2st1'}
    phase_names = []

    def ph(fn, *a, **k):
        before = set(S.m) | S.dead
        fn(*a, **k)
        phase_names.append((set(S.m) - before) - KEEP)
        if len(phase_names) >= 2:
            S.retire(*phase_names[-2])
            phase_names[-2] = set()

    with ctx:
        _persistent(b)
        b.block = ctx.enter_context(nc.Block())
        ph(_preload, b)
        if upto == "pre":
            _finish(b)
            return nc
        ph(_p0_ln, b)
        if upto == "p0":
            _finish(b)
            return nc
        ph(_ar1, b)
        if upto == "ar":
            _finish(b)
            return nc
        ph(_proj, b, layer=0)
        if upto == "proj":
            _finish(b)
            return nc
        ph(_ag_qkv, b, layer=0)
        b.mark_gate()
        ph(_quant_x, b)
        if upto == "p1":
            _finish(b)
            return nc
        b.mark_gate()
        ph(_m1, b)
        if upto in ("m1", "m1a", "m1b", "m1c", "m1d", "m1e", "m1f"):
            _finish(b)
            return nc
        b.mark_gate()
        ph(_w2_quant, b)
        if upto == "w2":
            _finish(b)
            return nc
        ph(_attn0, b)
        ph(_agc, b, layer=0)
        if upto == "at0":
            _finish(b)
            return nc
        b.mark_gate()
        ph(_m2_x1, b)
        if upto in ("q2", "q2A", "q2B", "q2C", "q2ng", "q2v2", "l0"):
            _finish(b)
            return nc
        b.mark_gate()
        ph(_proj, b, layer=1)
        ph(_ag_qkv, b, layer=1)
        ph(_l1_select, b)
        b.mark_gate()
        ph(_ffn1, b)
        b.mark_gate()
        ph(_attn1, b)
        ph(_agc, b, layer=1)
        b.mark_gate()
        ph(_final, b)
        _finish(b)
        assert nc.sbuf_base <= MB, (nc.sbuf_base, MB)
    return nc


def _persistent(b):
    c, sb = b.cfg, b.sb
    D, T, TS = c["D"], c["T"], c["TS"]
    KD, KI, KL, NHm, NH1m = c["KD"], c["KI"], c["KL"], c["NHm"], c["NH1m"]
    TRIV = TS // 128
    b.ones_f = sb("ones_f", [128, 1], F32)
    b.onesr_f = sb("onesr_f", [1, 128], F32)
    b.ones_b = sb("ones_b", [128, 1], BF16)
    b.biasK = sb("biasK", [128, 3 * KD], F32)
    b.biasL = sb("biasL", [128, 3 * KL], F32)
    b.bias1 = sb("bias1", [128, KI], F32)
    b.biasQ1 = sb("biasQ1", [128, 3 * KD], F32)
    b.biasC1 = sb("biasC1", [128, max(NH1m, 1)], F32)
    b.s1c = sb("s1c", [128, KI], F32)
    b.absacc = sb("absacc", [128, 1], F32)
    b.habs = sb("habs", [128, 1], F32)
    b.sm = sb("sm", [128, 24], F32)
    b.rsx_b = sb("rsx_b", [128, 1], F32)
    b.rs2_b = sb("rs2_b", [128, 1], F32)
    b.idT = sb("idT", [128, 128], F32)
    b.tri = sb("tri", [128, TRIV * TS], BF16)
    b.r8 = sb("r8", [8, 64], BF16)
    b.iota = sb("iota", [64, 512], F32)
    b.soffs = sb("soffs", [1, 8], I32)
    b.psP = [b.ps(f"psP{i}", [128, 512]) for i in range(8)]


def _preload(b):
    S, c = b.S, b.cfg
    KD, KL, KI, TS = c["KD"], c["KL"], c["KI"], c["TS"]
    TRIV = TS // 128

    @b.block.sync
    def _(y):
        for bt, K, srcs in [(b.biasK, KD, (b.bq, b.bk, b.bv)),
                            (b.biasL, KL, (b.bql, b.bkl, b.bvl)),
                            (b.biasQ1, KD, (b.bq1, b.bk1, b.bv1))]:
            for j, src in enumerate(srcs):
                r = src.rearrange("(c p) -> c p", p=128)
                for cc in range(K):
                    ii = y.dma_start(out=bt[:, j * K + cc:j * K + cc + 1], in_=r[cc])
                    S.inc("cst", ii, 16)
        r = b.b1.rearrange("(c p) -> c p", p=128)
        for cc in range(KI):
            ii = y.dma_start(out=b.bias1[:, cc:cc + 1], in_=r[cc])
            S.inc("cst", ii, 16)
        for dst, src in [(b.idT, b.ID128), (b.r8, b.R8), (b.iota, b.IOTA),
                         (b.soffs, b.soff)]:
            ii = y.dma_start(out=dst[:], in_=src)
            S.inc("cst", ii, 16)
        for v in range(TRIV):
            ii = y.dma_start(out=b.tri[:, v * TS:(v + 1) * TS],
                             in_=b.TRI[v * 128:(v + 1) * 128, :])
            S.inc("cst", ii, 16)
    b.cst_tot = S.tok("cst")

    def loadoffs(eng, ename):
        S.wait(eng, "cst", b.cst_tot)
        b.sreg[ename] = eng.value_load(b.soffs[0:1, 0:1])

    @b.block.sync
    def _(y):
        loadoffs(y, "y")

    @b.block.gpsimd
    def _(g):
        loadoffs(g, "g")

    @b.block.vector
    def _(v):
        v.memset(b.ones_f[:], 1.0)
        v.memset(b.onesr_f[:], 1.0)
        v.memset(b.ones_b[:], 1.0)
        v.memset(b.absacc[:], 0.0)
        ii = v.memset(b.habs[:], 0.0)
        b.t_const = S.inc("v", ii, 1)


def _p0_ln(b):
    S, c = b.S, b.cfg
    D, NT, NTT, KD = c["D"], c["NT"], c["NTT"], c["KD"]
    RV = b.RV

    b.xn_fm = [RV("RA", BF16, k * 2 * NT, 128, NT) for k in range(KD)]
    x0t = [RV("RC", F32, j * 4 * D, 128, D) for j in range(2)]
    sqt = [RV("RC", F32, 8 * D + j * 4 * D, 128, D) for j in range(2)]
    xnt = [RV("RC", F32, 16 * D + j * 4 * D, 128, D) for j in range(2)]
    xnb = [RV("RC", BF16, 24 * D + j * 2 * D, 128, D) for j in range(2)]
    stat = b.sb("stat", [128, 8], F32)

    s_base = S.tok("s")
    vx_base = S.tok("vx")
    v0p = S.tok("v")
    vt = lambda tt: v0p + tt + 1

    def p0_spill(y, tt):
        j = tt % 2
        S.wait(y, "v", vt(tt))
        ii = y.dma_start(out=b.xn32_sp[tt * 128:(tt + 1) * 128, :], in_=xnt[j][:])
        S.inc(f"p0sp{j}", ii, 16)
        for k in range(KD):
            ii = y.dma_start_transpose(b.xn_fm[k][:, tt * 128:(tt + 1) * 128],
                                       xnb[j][:, k * 128:(k + 1) * 128])
            S.inc(f"p0sp{j}", ii, 16)

    @b.block.sync
    def _(y):
        for tt in range(NTT):
            j = tt % 2
            if tt >= 2:
                S.wait(y, "v", vt(tt - 2))
            ii = y.dma_start(out=x0t[j][:], in_=b.x0[tt * 128:(tt + 1) * 128, :])
            S.inc(f"p0ld{j}", ii, 16)
            if tt >= 1:
                p0_spill(y, tt - 1)
        p0_spill(y, NTT - 1)

    @b.block.vector
    def _(v):
        S.wait(v, "v", b.t_const)
        for tt in range(NTT):
            j = tt % 2
            st = _chain(S, v)
            S.wait(v, f"p0ld{j}", (tt // 2 + 1) * 16)
            S.wait(v, "s", s_base + 2 * tt + 1)
            if tt >= 2:
                S.wait(v, f"p0sp{j}", (tt // 2) * 16 * (1 + KD))
            v.reduce_sum(out=stat[:, 0:1], in_=x0t[j][:], axis=AX.X)
            st(v.reduce_sum(out=stat[:, 1:2], in_=sqt[j][:], axis=AX.X))
            st(v.tensor_scalar(out=stat[:, 2:3], in0=stat[:, 0:1], scalar1=-1.0 / D,
                               scalar2=None, op0=ALU.mult))
            st(v.tensor_tensor(out=stat[:, 4:5], in0=stat[:, 2:3], in1=stat[:, 2:3],
                               op=ALU.mult))
            st(v.tensor_scalar(out=stat[:, 3:4], in0=stat[:, 1:2], scalar1=1.0 / D,
                               scalar2=None, op0=ALU.mult))
            st(v.tensor_tensor(out=stat[:, 3:4], in0=stat[:, 3:4], in1=stat[:, 4:5],
                               op=ALU.subtract))
            st(v.tensor_scalar(out=stat[:, 3:4], in0=stat[:, 3:4], scalar1=LNEPS,
                               scalar2=None, op0=ALU.add))
            S.bump(v, "vx", 1)
            S.wait(v, "s", s_base + 2 * tt + 2)
            st(v.reciprocal(out=stat[:, 6:7], in_=stat[:, 5:6]))
            st(v.tensor_tensor(out=stat[:, 7:8], in0=stat[:, 6:7], in1=stat[:, 6:7],
                               op=ALU.mult))
            st(v.tensor_tensor(out=stat[:, 7:8], in0=stat[:, 7:8], in1=stat[:, 3:4],
                               op=ALU.mult))
            st(v.tensor_scalar(out=stat[:, 7:8], in0=stat[:, 7:8], scalar1=-0.5,
                               scalar2=1.5, op0=ALU.mult, op1=ALU.add))
            st(v.tensor_tensor(out=stat[:, 6:7], in0=stat[:, 6:7], in1=stat[:, 7:8],
                               op=ALU.mult))
            st(v.tensor_scalar(out=xnt[j][:], in0=x0t[j][:], scalar1=stat[:, 2:3],
                               scalar2=stat[:, 6:7], op0=ALU.add, op1=ALU.mult))
            st(v.reduce_max(out=stat[:, 4:5], in_=xnt[j][:], axis=AX.X,
                            apply_absolute_value=True))
            st(v.tensor_tensor(out=b.absacc[:], in0=b.absacc[:], in1=stat[:, 4:5],
                               op=ALU.max))
            st(v.tensor_copy(out=xnb[j][:], in_=xnt[j][:]))
            assert S.bump(v, "v", 1) == vt(tt)

    @b.block.scalar
    def _(sc):
        for tt in range(NTT):
            j = tt % 2
            S.wait(sc, f"p0ld{j}", (tt // 2 + 1) * 16)
            if tt >= 1:
                S.wait(sc, "v", vt(tt - 1))
            ii = sc.activation(out=sqt[j][:], in_=x0t[j][:], func=AF.Square)
            S.inc("s", ii, 1)
            S.wait(sc, "vx", vx_base + tt + 1)
            ii = sc.activation(out=stat[:, 5:6], in_=stat[:, 3:4], func=AF.Sqrt)
            S.inc("s", ii, 1)

    b.p0sp_tot = [S.tok("p0sp0"), S.tok("p0sp1")]
    b.v_p0_last = vt(NTT - 1)


def _ar1(b):
    S, sb = b.S, b.sb
    arrow = sb("arrow", [1, 128], F32)
    b.arrow = arrow

    @b.block.sync
    def _(y):
        S.wait(y, "v", b.v_p0_last)
        ii = y.dma_start(out=b.row_dr[0:128], in_=b.absacc[:, 0])
        S.inc("ario", ii, 16)
        S.wait(y, "ario", None)
        ii = y.dma_start(out=arrow[:], in_=b.row_dr.rearrange("(a x) -> a x", a=64)[0])
        S.inc("ario", ii, 16)

    @b.block.vector
    def _(v):
        S.wait(v, "ario", None)
        st = _chain(S, v)
        st(v.reduce_max(out=b.sm[0:1, 0:1], in_=arrow[:], axis=AX.X))
        st(v.tensor_scalar(out=arrow[:], in0=arrow[:], scalar1=0.0,
                           scalar2=b.sm[0:1, 0:1], op0=ALU.mult, op1=ALU.add))
        b.t_armax = S.bump(v, "v", 1)

    @b.block.gpsimd
    def _(g):
        S.wait(g, "v", b.t_armax)
        S.wait(g, "p0sp0", None)
        S.wait(g, "p0sp1", None)
        ii = g.dma_start(out=b.ar_in, in_=arrow[:])
        S.inc("arw", ii, 16)
        S.wait(g, "arw", None)
        ii = g.collective_compute("AllReduce", ALU.max, replica_groups=b.ALLG,
                                  ins=[b.ar_in.opt()], outs=[b.ar_out.opt()])
        S.inc("cc", ii, 1)
        S.wait(g, "cc", None)
        ii = g.dma_start(out=arrow[:], in_=b.ar_out)
        S.inc("arq", ii, 16)

    @b.block.vector
    def _(v):
        S.wait(v, "arq", None)
        st = _chain(S, v)
        st(v.tensor_scalar(out=b.sm[0:1, 1:2], in0=arrow[:, 0:1], scalar1=1.0 / 127.0,
                           scalar2=1e-8, op0=ALU.mult, op1=ALU.max))
        st(v.reciprocal(out=b.sm[0:1, 2:3], in_=b.sm[0:1, 1:2]))
        b.t_sx = S.bump(v, "v", 1)

    @b.block.tensor
    def _(t):
        S.wait(t, "v", b.t_sx)
        ii = t.matmul(b.psP[7][:, 0:2], b.onesr_f[:], b.sm[0:1, 1:3],
                      start=True, stop=True)
        b.t_sxmm = S.inc("t", ii, 1)

    @b.block.vector
    def _(v):
        S.wait(v, "t", b.t_sxmm)
        st = _chain(S, v)
        st(v.tensor_copy(out=b.sm[:, 3:5], in_=b.psP[7][:, 0:2]))
        st(v.tensor_copy(out=b.rsx_b[:], in_=b.sm[:, 4:5]))
        b.t_sxb = S.bump(v, "v", 1)


def _quant_x(b):
    S, c, RV = b.S, b.cfg, b.RV
    D, NT, NTT, KD = c["D"], c["NT"], c["NTT"], c["KD"]

    b.qx_fm = [RV("RA", BF16, k * 2 * NT, 128, NT) for k in range(KD)]
    qmg = [RV("RB", F32, j * 4 * D, 128, D) for j in range(2)]
    qint = [RV("RB", I32, 8 * D + j * 4 * D, 128, D) for j in range(2)]
    qbf = [RV("RB", BF16, 16 * D + j * 2 * D, 128, D) for j in range(2)]

    sq_base = S.tok("s")
    vq0 = S.tok("v")
    tqv = lambda tt: vq0 + tt + 1

    def q_transp(y, tt):
        j = tt % 2
        S.wait(y, "v", tqv(tt))
        for k in range(KD):
            ii = y.dma_start_transpose(b.qx_fm[k][:, tt * 128:(tt + 1) * 128],
                                       qbf[j][:, k * 128:(k + 1) * 128])
            S.inc(f"qtp{j}", ii, 16)

    @b.block.sync
    def _(y):
        b.gate(y)
        for tt in range(NTT):
            j = tt % 2
            S.wait(y, f"p0sp{j}", (tt // 2 + 1) * 16 * (1 + KD))
            if tt >= 2:
                S.wait(y, "v", tqv(tt - 2))
            ii = y.dma_start(out=qmg[j][:], in_=b.xn32_sp[tt * 128:(tt + 1) * 128, :])
            S.inc(f"qld{j}", ii, 16)
            if tt >= 1:
                q_transp(y, tt - 1)
        q_transp(y, NTT - 1)

    @b.block.scalar
    def _(sc):
        b.gate(sc)
        S.wait(sc, "v", b.t_sxb)
        for tt in range(NTT):
            j = tt % 2
            S.wait(sc, f"qld{j}", (tt // 2 + 1) * 16)
            if tt >= 2:
                S.wait(sc, "v", tqv(tt - 2))
            ii = sc.activation(out=qmg[j][:], in_=qmg[j][:], func=AF.Copy,
                               bias=MAGIC, scale=b.rsx_b[:, 0:1])
            S.inc("s", ii, 1)

    @b.block.vector
    def _(v):
        b.gate(v)
        for tt in range(NTT):
            j = tt % 2
            S.wait(v, "s", sq_base + tt + 1)
            if tt >= 2:
                S.wait(v, f"qtp{j}", (tt // 2) * 16 * KD)
            st = _chain(S, v)
            st(v.tensor_scalar(out=qint[j][:], in0=qmg[j][:], scalar1=-MAGIC,
                               scalar2=None, op0=ALU.add))
            st(v.tensor_scalar(out=qbf[j][:], in0=qint[j][:], scalar1=0.0,
                               scalar2=None, op0=ALU.add))
            assert S.bump(v, "v", 1) == tqv(tt)

    b.qtp_tot = [S.tok("qtp0"), S.tok("qtp1")]


def _proj(b, layer):
    S, c = b.S, b.cfg
    RV = b.RV
    D, DL, NT, TS = c["D"], c["DL"], c["NT"], c["TS"]
    KD, KL, NS = c["KD"], c["KL"], c["NS"]
    L = layer
    if L == 0:
        KO = KL
        ag_in, src_fm = b.ag0_in, b.xn_fm
        W1s, W2s = [b.wq, b.wk, b.wv], [b.wql, b.wkl, b.wvl]
        biasL, two_stage, OROW = b.biasL, True, DL
    else:
        KO = KD
        ag_in, src_fm = b.ag1_in, b.xn1_fm
        W1s, W2s = [b.Wq1, b.Wk1, b.Wv1], None
        biasL, two_stage, OROW = b.biasQ1, False, D
    pfx = f"L{L}p"
    SPP = 2 if NS >= 2 else 1
    NSP = max(NS // SPP, 1)

    if L == 0:
        b.a_fm = [RV("RB", BF16, k * 2 * NT, 128, NT) for k in range(KD)]
        b.wcol = [RV("RW", BF16, j * 2 * D, 128, D) for j in range(2)]
        b.pbounce = [RV("RW", BF16, 4 * D + jb * 2 * TS, 128, TS) for jb in range(4)]
    a_fm, wcol, pbounce = b.a_fm, b.wcol, b.pbounce
    psP = b.psP

    NBLK = KD + (KO if two_stage else 0)
    t0 = S.tok("t")
    e0 = S.tok(pfx + "e")

    @b.block.gpsimd
    def _(g):
        if L == 1:
            b.gate(g)
        n = 0
        for pj in range(3):
            for blk in range(NBLK):
                j = n % 2
                if n >= 2:
                    S.wait(g, "t", t0 + (n - 1) * NSP)
                for kq in range(KD):
                    if two_stage and blk >= KD:
                        W, dc = W2s[pj], blk - KD
                        ap = bass.AP(W.tensor, dc * 128 + kq * 128 * DL,
                                     [[DL, 128], [1, 128]])
                    else:
                        W = W1s[pj]
                        ap = bass.AP(W.tensor, blk * 128 + kq * 128 * D,
                                     [[D, 128], [1, 128]])
                    ii = g.dma_start(out=wcol[j][:, kq * 128:(kq + 1) * 128],
                                     in_=ap)
                    S.inc(f"{pfx}w{j}", ii, 16)
                n += 1

    @b.block.tensor
    def _(t):
        g = 0
        for pj in range(3):
            for blk in range(NBLK):
                n = pj * NBLK + blk
                j = n % 2
                S.wait(t, f"{pfx}w{j}", (n // 2 + 1) * 16 * KD)
                stage2 = two_stage and blk >= KD
                src = a_fm if stage2 else src_fm
                if L == 0 and n == 0:
                    S.wait(t, "p0sp0", b.p0sp_tot[0])
                    S.wait(t, "p0sp1", b.p0sp_tot[1])
                    S.wait(t, "v", b.t_sxb)
                if L == 1 and n == 0:
                    S.wait(t, "ln1tp0", b.ln1tp_tot[0])
                    S.wait(t, "ln1tp1", b.ln1tp_tot[1])
                bank0 = 4 if stage2 else 0
                for spp in range(NSP):
                    pp = bank0 + (g % 2) * SPP
                    if stage2 and blk == KD and spp == 0:
                        S.wait(t, pfx + "e", e0 + g)
                    elif g >= 2:
                        S.wait(t, pfx + "e", e0 + g - 1)
                    ii = None
                    for k in range(KD):
                        for s2 in range(SPP):
                            sp = spp * SPP + s2
                            ii = t.matmul(psP[pp + s2][:],
                                          wcol[j][:, k * 128:(k + 1) * 128],
                                          src[k][:, sp * TS:(sp + 1) * TS],
                                          start=(k == 0), stop=(k == KD - 1))
                    S.inc("t", ii, 1)
                    g += 1

    bnc_jobs = []

    @b.block.vector
    def _(v):
        S.wait(v, "cst", b.cst_tot)
        g = 0
        nb = 0
        for pj in range(3):
            for blk in range(NBLK):
                stage2 = two_stage and blk >= KD
                outp = stage2 or (not two_stage)
                bank0 = 4 if stage2 else 0
                for spp in range(NSP):
                    S.wait(v, "t", t0 + g + 1)
                    pp = bank0 + (g % 2) * SPP
                    lastii = None
                    for s2 in range(SPP):
                        sp = spp * SPP + s2
                        if not outp:
                            f = blk
                            lastii = v.tensor_scalar(
                                out=a_fm[f][:, sp * TS:(sp + 1) * TS],
                                in0=psP[pp + s2][:],
                                scalar1=b.biasK[:, pj * KD + f:pj * KD + f + 1],
                                scalar2=None, op0=ALU.add)
                        else:
                            dc = (blk - KD) if two_stage else blk
                            jb = nb % 4
                            if nb >= 4:
                                S.wait(v, f"{pfx}bn{jb}", (nb // 4) * 16)
                            lastii = v.tensor_scalar(
                                out=pbounce[jb][:], in0=psP[pp + s2][:],
                                scalar1=biasL[:, pj * KO + dc:pj * KO + dc + 1],
                                scalar2=None, op0=ALU.add)
                            tk = S.inc("v", lastii, 1)
                            bnc_jobs.append((pj, dc, sp, jb, tk))
                            nb += 1
                    if lastii is not None:
                        S.wait(v, "v" if outp else "t", None if outp else t0 + g + 1)
                        if not outp:
                            S.inc(pfx + "e", lastii, 1)
                        else:
                            S.bump(v, pfx + "e", 1)
                    g += 1

    @b.block.sync
    def _(y):
        for (pj, dc, sp, jb, tk) in bnc_jobs:
            S.wait(y, "v", tk)
            dst = ag_in[pj * OROW + dc * 128:pj * OROW + (dc + 1) * 128,
                        sp * TS:(sp + 1) * TS]
            ii = y.dma_start(out=dst, in_=pbounce[jb][:])
            S.inc(f"{pfx}bn{jb}", ii, 16)
    b.__dict__[f"{pfx}bn_tot"] = [S.tok(f"{pfx}bn{jj}") for jj in range(4)]
    b.__dict__[f"{pfx}tlast"] = S.tok("t")


def _ag_qkv(b, layer):
    S = b.S
    L = layer
    pfx = f"L{L}p"
    ag_in, ag_out = (b.ag0_in, b.ag0_out) if L == 0 else (b.ag1_in, b.ag1_out)

    @b.block.gpsimd
    def _(g):
        for jj in range(4):
            S.wait(g, f"{pfx}bn{jj}", b.__dict__[f"{pfx}bn_tot"][jj])
        ii = g.collective_compute("AllGather", ALU.bypass, replica_groups=b.PAIRS,
                                  ins=[ag_in.opt()], outs=[ag_out.opt()])
        b.__dict__[f"{pfx}cc"] = S.inc("cc", ii, 1)


def _m1(b):
    S, c = b.S, b.cfg
    D, I, NT, TS = c["D"], c["I"], c["NT"], c["TS"]
    KD, KI, NS = c["KD"], c["KI"], c["NS"]
    RV = b.RV

    w1f = [RV("RC", F32, j * 4 * D, 128, D) for j in range(2)]
    qw1r = [RV("RC", BF16, 8 * D + j * 2 * D, 128, D) for j in range(2)]
    qw1T = [RV("RC", BF16, 12 * D + j * 2 * D, 128, D) for j in range(2)]
    hpre = [RV("RC", F32, 16 * D + jb * 4 * TS, 128, TS) for jb in range(4)]
    hout = [RV("RC", F32, 16 * D + 16 * TS + jb * 4 * TS, 128, TS) for jb in range(4)]
    wst = b.sb("wst", [128, 8], F32)

    psP = b.psP
    t0 = S.tok("t")
    v0 = S.tok("m1e")
    s0 = S.tok("m1s")
    g0w = [S.tok("m1w0"), S.tok("m1w1")]

    @b.block.gpsimd
    def _(g):
        b.gate(g)
        for i in range(KI):
            j = i % 2
            if i >= 2:
                S.wait(g, "m1q", i - 1)
            ii = g.dma_start(out=w1f[j][:], in_=b.w1[i * 128:(i + 1) * 128, :])
            S.inc(f"m1w{j}", ii, 16)

    @b.block.vector
    def _(v):
        S.wait(v, "cst", b.cst_tot)
        def absmax_step(n):
            jb = n % 4
            S.wait(v, "m1s", s0 + n + 1)
            st = _chain(S, v)
            st(v.reduce_max(out=wst[:, 3:4], in_=hout[jb][:], axis=AX.X,
                            apply_absolute_value=True))
            st(v.tensor_tensor(out=b.habs[:], in0=b.habs[:], in1=wst[:, 3:4],
                               op=ALU.max))
            S.bump(v, "m1hv", 1)

        for i in range(KI + 3):
            if i < KI:
                j = i % 2
                wb = 4 * j
                S.wait(v, f"m1w{j}", (i // 2 + 1) * 16)
                if i >= 2:
                    S.wait(v, f"m1tp{j}", (i // 2) * 16 * KD)
                    S.wait(v, "m1g", i - 1)
                st = _chain(S, v)
                st(v.reduce_max(out=wst[:, wb:wb + 1], in_=w1f[j][:], axis=AX.X,
                                apply_absolute_value=True))
                st(v.reciprocal(out=wst[:, wb + 1:wb + 2], in_=wst[:, wb:wb + 1]))
                st(v.tensor_scalar(out=wst[:, wb + 2:wb + 3],
                                   in0=wst[:, wb + 1:wb + 2], scalar1=127.0,
                                   scalar2=None, op0=ALU.mult))
                st(v.tensor_scalar(out=b.s1c[:, i:i + 1], in0=wst[:, wb:wb + 1],
                                   scalar1=1.0 / 127.0, scalar2=b.sm[:, 3:4],
                                   op0=ALU.mult, op1=ALU.mult))
                S.bump(v, "m1v", 1)
                S.wait(v, "m1g", i + 1)
                st(v.tensor_scalar(out=qw1r[j][:], in0=w1f[j][:], scalar1=-MAGIC,
                                   scalar2=None, op0=ALU.add))
                st(v.tensor_scalar(out=qw1r[j][:], in0=qw1r[j][:], scalar1=0.0,
                                   scalar2=None, op0=ALU.add))
                S.bump(v, "m1q", 1)
            if 3 <= i:
                ic = i - 3
                if ic < KI:
                    for sp in range(NS):
                        absmax_step(ic * NS + sp)
            if 2 <= i <= KI + 1:
                ic = i - 2
                for sp in range(NS):
                    n = ic * NS + sp
                    jb = n % 4
                    S.wait(v, "t", t0 + ic + 1)
                    if n >= 4:
                        S.wait(v, "m1s", s0 + n - 3)
                    ii = v.tensor_scalar(out=hpre[jb][:],
                                         in0=psP[(ic % 2) * 4 + sp][:],
                                         scalar1=b.s1c[:, ic:ic + 1],
                                         scalar2=b.bias1[:, ic:ic + 1],
                                         op0=ALU.mult, op1=ALU.add)
                    S.inc("m1e", ii, 1)

    @b.block.scalar
    def _(sc):
        for i in range(KI + 2):
            if i < KI:
                j = i % 2
                S.wait(sc, "m1v", i + 1)
                ii = sc.activation(out=w1f[j][:], in_=w1f[j][:], func=AF.Copy,
                                   bias=MAGIC, scale=wst[:, 4 * j + 2:4 * j + 3])
                S.inc("m1g", ii, 1)
            if i >= 2:
                ic = i - 2
                for sp in range(NS):
                    n = ic * NS + sp
                    jb = n % 4
                    S.wait(sc, "m1e", v0 + n + 1)
                    if n >= 4:
                        S.wait(sc, f"m1hx{jb}", (n // 4) * 16)
                    ii = sc.activation(out=hout[jb][:], in_=hpre[jb][:],
                                       func=AF.Gelu)
                    S.inc("m1s", ii, 1)


    def m1_spill(y, i):
        for sp in range(NS):
            n = i * NS + sp
            jb = n % 4
            S.wait(y, "m1hv", n + 1)
            ii = y.dma_start(out=b.h_sp[i * 128:(i + 1) * 128, sp * TS:(sp + 1) * TS],
                             in_=hout[jb][:])
            S.inc(f"m1hx{jb}", ii, 16)

    @b.block.sync
    def _(y):
        for i in range(KI):
            j = i % 2
            S.wait(y, "m1q", i + 1)
            if i >= 2:
                S.wait(y, "t", t0 + i - 1)
            for k in range(KD):
                ii = y.dma_start_transpose(qw1T[j][:, k * 128:(k + 1) * 128],
                                           qw1r[j][:, k * 128:(k + 1) * 128])
                S.inc(f"m1tp{j}", ii, 16)
            if i >= 1:
                m1_spill(y, i - 1)
        m1_spill(y, KI - 1)

    @b.block.tensor
    def _(t):
        for i in range(KI):
            j = i % 2
            S.wait(t, f"m1tp{j}", (i // 2 + 1) * 16 * KD)
            if i == 0:
                S.wait(t, "qtp0", b.qtp_tot[0])
                S.wait(t, "qtp1", b.qtp_tot[1])
                S.wait(t, "L0pe", None)
            if i >= 2:
                S.wait(t, "m1e", v0 + (i - 1) * NS)
            ii = None
            for k in range(KD):
                for sp in range(NS):
                    ii = t.matmul(psP[(i % 2) * 4 + sp][:],
                                  qw1T[j][:, k * 128:(k + 1) * 128],
                                  b.qx_fm[k][:, sp * TS:(sp + 1) * TS],
                                  start=(k == 0), stop=(k == KD - 1))
                if k == KD - 1:
                    S.inc("t", ii, 1)

    b.m1_end_t = S.tok("t")
    if b.upto == "m1a":
        return

    @b.block.sync
    def _(y):
        S.wait(y, "m1hv", KI * NS)
        ii = y.dma_start(out=b.row_dr[0:128], in_=b.habs[:, 0])
        S.inc("ar2io", ii, 16)
        S.wait(y, "ar2io", None)
        ii = y.dma_start(out=b.arrow[:], in_=b.row_dr.rearrange("(a x) -> a x", a=64)[0])
        S.inc("ar2io", ii, 16)

    @b.block.vector
    def _(v):
        S.wait(v, "ar2io", None)
        st = _chain(S, v)
        st(v.reduce_max(out=b.sm[0:1, 5:6], in_=b.arrow[:], axis=AX.X))
        st(v.tensor_scalar(out=b.arrow[:], in0=b.arrow[:], scalar1=0.0,
                           scalar2=b.sm[0:1, 5:6], op0=ALU.mult, op1=ALU.add))
        b.t_ar2 = S.bump(v, "v", 1)

    if b.upto == "m1c":
        return

    @b.block.gpsimd
    def _(g):
        S.wait(g, "v", b.t_ar2)
        for jb in range(4):
            S.wait(g, f"m1hx{jb}", None)
        ii = g.dma_start(out=b.ar_in, in_=b.arrow[:])
        S.inc("ar2w", ii, 16)
        S.wait(g, "ar2w", None)
        if b.upto == "m1b":
            ii = g.dma_start(out=b.ar_out, in_=b.ar_in)
            S.inc("cc", ii, 16)
            S.wait(g, "cc", None)
        else:
            ii = g.collective_compute("AllReduce", ALU.max, replica_groups=b.ALLG,
                                      ins=[b.ar_in.opt()], outs=[b.ar_out.opt()])
            S.inc("cc", ii, 1)
            S.wait(g, "cc", None)
        ii = g.dma_start(out=b.arrow[:], in_=b.ar_out)
        S.inc("ar2q", ii, 16)

    if b.upto == "m1d":
        return

    @b.block.vector
    def _(v):
        S.wait(v, "ar2q", None)
        st = _chain(S, v)
        st(v.tensor_scalar(out=b.sm[0:1, 6:7], in0=b.arrow[:, 0:1], scalar1=1.0 / 127.0,
                           scalar2=1e-8, op0=ALU.mult, op1=ALU.max))
        st(v.reciprocal(out=b.sm[0:1, 7:8], in_=b.sm[0:1, 6:7]))
        b.t_s2 = S.bump(v, "v", 1)

    if b.upto == "m1e":
        return

    @b.block.tensor
    def _(t):
        S.wait(t, "v", b.t_s2)
        ii = t.matmul(b.psP[7][:, 4:6], b.onesr_f[:], b.sm[0:1, 6:8],
                      start=True, stop=True)
        b.t_s2mm = S.inc("t", ii, 1)

    @b.block.vector
    def _(v):
        S.wait(v, "t", b.t_s2mm)
        st = _chain(S, v)
        st(v.tensor_copy(out=b.sm[:, 8:10], in_=b.psP[7][:, 4:6]))
        st(v.tensor_copy(out=b.rs2_b[:], in_=b.sm[:, 9:10]))
        b.t_s2b = S.bump(v, "v", 1)




def _w2_quant(b):
    S, c = b.S, b.cfg
    D, I, KD, KI = c["D"], c["I"], c["KD"], c["KI"]

    RV = b.RV
    w2f = RV("RB", F32, 0, 128, I)
    sw2 = b.sb("sw2", [128, KD], F32)
    rwbc = RV("RB", F32, 4 * I, 128, D)
    SWB0 = max(16 * D, 16 * c["NT"])
    b.sw2bc = RV("RC", F32, SWB0, 128, D)
    b.bias0bc = RV("RC", F32, SWB0 + 4 * D, 128, D)
    w2t = [RV("RB", F32, 4 * I + 4 * D + j * 4 * D, 128, D) for j in range(2)]
    w2i = [RV("RB", I32, j * 4 * D, 128, D) for j in range(2)]
    w2b = [RV("RC", BF16, j * 2 * D, 128, D) for j in range(2)]
    rwrow = RV("RC", F32, 4 * D, 1, D)
    swrow = RV("RC", F32, 8 * D, 1, D)
    b0row = RV("RC", F32, 12 * D, 1, D)
    amrow = RV("RB", F32, 4 * I + 4 * D, 1, D)
    borow = RV("RB", F32, 4 * I, 1, D)

    @b.block.gpsimd
    def _(g):
        b.gate(g)
        for oc in range(KD):
            if oc >= 1:
                S.wait(g, "w2a", oc)
            ii = g.dma_start(out=w2f[:], in_=b.w2[oc * 128:(oc + 1) * 128, :])
            S.inc("w2l", ii, 16)

    @b.block.vector
    def _(v):
        b.gate(v)
        for oc in range(KD):
            S.wait(v, "w2l", (oc + 1) * 16)
            st = _chain(S, v)
            st(v.reduce_max(out=sw2[:, oc:oc + 1], in_=w2f[:], axis=AX.X,
                            apply_absolute_value=True))
            S.bump(v, "w2a", 1)
    t_w2a = S.tok("w2a")

    @b.block.sync
    def _(y):
        b.gate(y)
        S.wait(y, "w2a", t_w2a)
        for oc in range(KD):
            ii = y.dma_start(out=b.row_dr[oc * 128:(oc + 1) * 128], in_=sw2[:, oc])
            S.inc("w2r", ii, 16)
        S.wait(y, "w2r", None)
        ii = y.dma_start(out=amrow[:], in_=bass.AP(b.row_dr.tensor, 0, [[D, 1], [1, D]]))
        S.inc("w2r2", ii, 16)
        ii = y.dma_start(out=b0row[:], in_=bass.AP(b.b2.tensor, 0, [[D, 1], [1, D]]))
        S.inc("w2r2", ii, 16)
        ii = y.dma_start(out=borow[:], in_=bass.AP(b.bo.tensor, 0, [[D, 1], [1, D]]))
        S.inc("w2r2", ii, 16)

    @b.block.vector
    def _(v):
        S.wait(v, "w2r2", None)
        st = _chain(S, v)
        st(v.reciprocal(out=rwrow[:], in_=amrow[:]))
        st(v.tensor_scalar(out=rwrow[:], in0=rwrow[:], scalar1=127.0, scalar2=None,
                           op0=ALU.mult))
        S.wait(v, "v", b.t_s2b)
        st(v.tensor_scalar(out=swrow[:], in0=amrow[:], scalar1=1.0 / 127.0,
                           scalar2=b.sm[0:1, 8:9], op0=ALU.mult, op1=ALU.mult))
        st(v.tensor_tensor(out=b0row[:], in0=b0row[:], in1=borow[:], op=ALU.add))
        b.t_w2rows = S.bump(v, "v", 1)

    OC = min(512, D)
    NSTR = D // OC

    @b.block.tensor
    def _(t):
        S.wait(t, "v", b.t_w2rows)
        S.wait(t, "m1e", None)
        for sseg in range(NSTR):
            sl = slice(sseg * OC, (sseg + 1) * OC)
            if sseg >= 1:
                S.wait(t, "w2v", 3 * sseg)
            ii = t.matmul(b.psP[6][:, 0:OC], b.onesr_f[:], rwrow[:, sl],
                          start=True, stop=True)
            S.inc("w2t", ii, 1)
            ii = t.matmul(b.psP[7][:, 0:OC], b.onesr_f[:], swrow[:, sl],
                          start=True, stop=True)
            S.inc("w2t", ii, 1)
            S.wait(t, "w2v", 3 * sseg + 2)
            ii = t.matmul(b.psP[6][:, 0:OC], b.onesr_f[:], b0row[:, sl],
                          start=True, stop=True)
            S.inc("w2t", ii, 1)
    t_w2t0 = S.tok("w2t") - 3 * NSTR

    @b.block.vector
    def _(v):
        for sseg in range(NSTR):
            sl = slice(sseg * OC, (sseg + 1) * OC)
            S.wait(v, "w2t", t_w2t0 + 3 * sseg + 2)
            st = _chain(S, v)
            st(v.tensor_copy(out=rwbc[:, sl], in_=b.psP[6][:, 0:OC]))
            st(v.tensor_copy(out=b.sw2bc[:, sl], in_=b.psP[7][:, 0:OC]))
            S.bump(v, "w2v", 2)
            S.wait(v, "w2t", t_w2t0 + 3 * sseg + 3)
            st(v.tensor_copy(out=b.bias0bc[:, sl], in_=b.psP[6][:, 0:OC]))
            S.bump(v, "w2v", 1)
        S.bump(v, "w2bc", 1)
    b.t_w2bc = S.tok("w2bc")

    @b.block.gpsimd
    def _(g):
        S.wait(g, "v", b.t_w2rows)
        for ic in range(KI):
            j = ic % 2
            if ic >= 2:
                S.wait(g, "w2q", ic - 1)
            ii = g.dma_start(out=w2t[j][:], in_=b.w2T[ic * 128:(ic + 1) * 128, :])
            S.inc(f"w2tl{j}", ii, 16)

    sqw_base = S.tok("s")

    @b.block.vector
    def _(v):
        S.wait(v, "w2bc", b.t_w2bc)
        for ic in range(KI):
            j = ic % 2
            S.wait(v, f"w2tl{j}", (ic // 2 + 1) * 16)
            if ic >= 2:
                S.wait(v, f"w2st{j}", (ic // 2) * 16)
            st = _chain(S, v)
            st(v.tensor_tensor(out=w2t[j][:], in0=w2t[j][:], in1=rwbc[:], op=ALU.mult))
            S.bump(v, "w2m", 1)
            S.wait(v, "s", sqw_base + ic + 1)
            st(v.tensor_scalar(out=w2i[j][:], in0=w2t[j][:], scalar1=-MAGIC,
                               scalar2=None, op0=ALU.add))
            st(v.tensor_scalar(out=w2b[j][:], in0=w2i[j][:], scalar1=0.0,
                               scalar2=None, op0=ALU.add))
            S.bump(v, "w2q", 1)

    @b.block.scalar
    def _(sc):
        b.gate(sc)
        for ic in range(KI):
            j = ic % 2
            S.wait(sc, "w2m", ic + 1)
            ii = sc.activation(out=w2t[j][:], in_=w2t[j][:], func=AF.Copy, bias=MAGIC)
            S.inc("s", ii, 1)

    @b.block.sync
    def _(y):
        for ic in range(KI):
            j = ic % 2
            S.wait(y, "w2q", ic + 1)
            ii = y.dma_start(out=b.qw2T_dr[ic * 128:(ic + 1) * 128, :], in_=w2b[j][:])
            S.inc(f"w2st{j}", ii, 16)
    b.w2st_tot = [S.tok("w2st0"), S.tok("w2st1")]


def _attn0(b):
    S, c = b.S, b.cfg
    D, DL, NT, T, TS = c["D"], c["DL"], c["NT"], c["T"], c["TS"]
    NHm, HD, TQ, NKT = c["NHm"], c["HD"], c["TQ"], c["NKT"]
    SCL = 1.0 / math.sqrt(HD)

    cu = b.CUR("RA")
    qh = [cu(BF16, 64, T) for j in range(2)]
    kh = [cu(BF16, 64, T) for j in range(2)]
    vtm = [cu(BF16, 128, 65 * NKT) for j in range(2)]
    sqs = cu(F32, 64, 512)
    Pb = [cu(BF16, 128, TS) for j in range(3)]
    ctxl = [cu(F32, 65, TS) for j in range(2)]
    rr_ = cu(F32, 4, 2 * TS)
    rro = cu.last_off
    rrow = [b.RV("RA", F32, rro + jj * 4 * TS, 1, TS) for jj in range(2)]
    cbnc = [cu(BF16, 64, TS) for j in range(2)]
    nmax = b.sb("nmax", [1, 4], F32)
    vb = [b.sb(f"vb{sl}", [128, 64], BF16) for sl in range(4)]

    @b.block.vector
    def _(v):
        b.gate(v)
        for j in range(2):
            ap = bass.AP(vtm[j].tensor, vtm[j].offset + 64,
                         [[vtm[j].tensor.shape[1], 128], [65, NKT], [1, 1]])
            ii = v.memset(ap, 1.0)
            S.inc("avo", ii, 1)

    @b.block.scalar
    def _(sc):
        b.gate(sc)
        sc.nop()

    NSQ = T // 512
    TQl = TQ
    NE0 = 2 * TQl * (TQl + 1)
    ent = 0
    grp = 0

    @b.block.sync
    def _(y):
        b.gate(y)
        S.wait(y, "cc", b.L0pcc)
        for side in b.sswitch(y, "y"):
            ro = side * NHm * HD * NT
            for h in range(NHm):
                j = h % 2
                if h >= 2:
                    S.wait(y, "ac", (h - 1) * NE0)
                    S.wait(y, "ax", h * NE0)
                for tb in range(2):
                    src_q = bass.AP(b.ag0_out.tensor,
                                    (tb * 3 * DL + h * 64) * NT + ro,
                                    [[NT, 64], [1, NT]])
                    ii = y.dma_start(out=qh[j][:, tb * NT:(tb + 1) * NT], in_=src_q)
                    S.inc(f"aq{j}", ii, 16)
                    src_k = bass.AP(b.ag0_out.tensor,
                                    (tb * 3 * DL + DL + h * 64) * NT + ro,
                                    [[NT, 64], [1, NT]])
                    ii = y.dma_start(out=kh[j][:, tb * NT:(tb + 1) * NT], in_=src_k)
                    S.inc(f"aq{j}", ii, 16)
                for kt in range(NKT):
                    tb = kt * 128 // NT
                    col = kt * 128 - tb * NT
                    n = h * NKT + kt
                    sl = n % 4
                    src_v = bass.AP(b.ag0_out.tensor,
                                    (tb * 3 * DL + 2 * DL + h * 64) * NT + ro + col,
                                    [[NT, 64], [1, 128]])
                    if n >= 4:
                        S.wait(y, f"acp{sl}", (n // 4) * 16)
                    ii = y.dma_start_transpose(vb[sl][:], src_v)
                    S.inc(f"avb{sl}", ii, 16)
                for qs in range(TQl):
                    g = h * TQl + qs
                    S.wait(y, "aee", g + 1)
                    ii = y.dma_start(out=b.ag0c_in[h * 64:(h + 1) * 64,
                                                   qs * TS:(qs + 1) * TS],
                                     in_=cbnc[g % 2][:])
                    S.inc(f"acw{g % 2}", ii, 16)

    for h in range(NHm):
        j = h % 2
        nsq0 = S.tok("ant")

        @b.block.vector
        def _(v, h=h, j=j):
            st = _chain(S, v)
            for kt in range(NKT):
                n = h * NKT + kt
                sl = n % 4
                S.wait(v, f"avb{sl}", (n // 4 + 1) * 16)
                st(v.tensor_copy(out=vtm[j][:, kt * 65:kt * 65 + 64],
                                 in_=vb[sl][:]))
                S.bump(v, f"acp{sl}", 16)

        @b.block.scalar
        def _(sc, h=h, j=j, nsq0=nsq0):
            S.wait(sc, f"aq{j}", (h // 2 + 1) * 64)
            for w_ in range(2):
                for st_ in range(NSQ):
                    n = w_ * NSQ + st_
                    if n >= 1:
                        S.wait(sc, "ant", nsq0 + n)
                    src = (qh if w_ == 0 else kh)[j][:, st_ * 512:(st_ + 1) * 512]
                    ii = sc.activation(out=sqs[:], in_=src, func=AF.Square)
                    S.inc("asq", ii, 1)

        asq0 = S.tok("asq") - 2 * NSQ

        @b.block.tensor
        def _(t, h=h, nsq0=nsq0, asq0=asq0):
            if h == 0:
                S.wait(t, "m1e", None)
                S.wait(t, "w2t", None)
            for n in range(2 * NSQ):
                S.wait(t, "asq", asq0 + n + 1)
                if n >= 1:
                    S.wait(t, "anv", nsq0 + n)
                ii = t.matmul(b.psP[4][0:1, 0:512], b.ones_f[0:64, :], sqs[:],
                              start=True, stop=True)
                S.inc("ant", ii, 1)

        @b.block.vector
        def _(v, h=h, nsq0=nsq0):
            st = _chain(S, v)
            st(v.memset(nmax[:], 0.0))
            for w_ in range(2):
                for st_ in range(NSQ):
                    n = w_ * NSQ + st_
                    S.wait(v, "ant", nsq0 + n + 1)
                    st(v.reduce_max(out=nmax[0:1, 2:3], in_=b.psP[4][0:1, 0:512],
                                    axis=AX.X))
                    st(v.tensor_tensor(out=nmax[0:1, w_:w_ + 1],
                                       in0=nmax[0:1, w_:w_ + 1],
                                       in1=nmax[0:1, 2:3], op=ALU.max))
                    S.bump(v, "anv", 1)
            st(v.tensor_tensor(out=nmax[0:1, 3:4], in0=nmax[0:1, 0:1],
                               in1=nmax[0:1, 1:2], op=ALU.add))
            st(v.tensor_scalar(out=nmax[0:1, 3:4], in0=nmax[0:1, 3:4],
                               scalar1=-0.5 * SCL, scalar2=None, op0=ALU.mult))
            S.bump(v, "acr", 1)

        @b.block.tensor
        def _(t, h=h):
            S.wait(t, "acr", h + 1)
            ii = t.matmul(b.psP[6][:, 0:1], b.onesr_f[:], nmax[0:1, 3:4],
                          start=True, stop=True)
            S.inc("acb", ii, 1)

        @b.block.vector
        def _(v, h=h, j=j):
            S.wait(v, "acb", h + 1)
            if h >= 2:
                S.wait(v, "ax", (h - 1) * NE0)
            st = _chain(S, v)
            st(v.tensor_copy(out=b.sm[:, 10 + j:11 + j], in_=b.psP[6][:, 0:1]))
            S.bump(v, "acc2", 1)

        ent0 = S.tok("at")
        sched_h = [(qs, kt) for qs in range(TQ) for kt in range(4 * (qs + 1))]
        NE = len(sched_h)
        eidx_h = {e: i for i, e in enumerate(sched_h)}
        ax0 = S.tok("ax")
        ac0 = S.tok("ac")
        am_map = {}
        nm = S.tok("am")
        for (qs, kt) in sched_h:
            if kt >= 4 * qs:
                am_map[(qs, kt)] = nm
                nm += 1
        grp0 = grp

        @b.block.tensor
        def _(t, h=h, j=j, ent0=ent0, ax0=ax0, ac0=ac0, am_map=am_map,
              sched_h=sched_h, eidx_h=eidx_h, grp0=grp0):
            S.wait(t, f"aq{j}", (h // 2 + 1) * 64)
            for sl_ in range(4):
                S.wait(t, f"acp{sl_}", (h + 1) * (NKT // 4) * 16)
            S.wait(t, "avo", 2)
            S.wait(t, "acc2", h + 1)
            def ctx_mm(n, qs, kt):
                g = grp0 + qs
                if kt == 0 and g >= 2:
                    S.wait(t, "aev", g - 1)
                if kt >= 4 * qs:
                    S.wait(t, "am", am_map[(qs, kt)] + 1)
                else:
                    S.wait(t, "ax", ent0 + n + 1)
                ii = t.matmul(b.psP[2 + g % 2][0:65, 0:TS],
                              vtm[j][:, kt * 65:(kt + 1) * 65],
                              Pb[(ent0 + n) % 3][:],
                              start=(kt == 0), stop=(kt == 4 * (qs + 1) - 1))
                S.inc("ac", ii, 1)
                if kt == 4 * (qs + 1) - 1:
                    S.wait(t, "arx", g + 1)
                    ii = t.matmul(b.psP[5][0:64, 0:TS], b.onesr_f[:, 0:64],
                                  rrow[g % 2][:], start=True, stop=True)
                    S.inc("ab", ii, 1)

            for n, (qs, kt) in enumerate(sched_h):
                if ent0 + n >= 2:
                    S.wait(t, "ax", ent0 + n - 1)
                ii = t.matmul(b.psP[n % 2][:, 0:TS],
                              kh[j][:, kt * 128:(kt + 1) * 128],
                              qh[j][:, qs * TS:(qs + 1) * TS],
                              start=True, stop=True)
                S.inc("at", ii, 1)
                if n >= 1:
                    ctx_mm(n - 1, *sched_h[n - 1])
            ctx_mm(len(sched_h) - 1, *sched_h[-1])

        @b.block.scalar
        def _(sc, h=h, j=j, ent0=ent0, ac0=ac0, sched_h=sched_h):
            for n, (qs, kt) in enumerate(sched_h):
                S.wait(sc, "at", ent0 + n + 1)
                if ent0 + n >= 3:
                    S.wait(sc, "ac", ac0 - ent0 + ent0 + n - 2)
                ii = sc.activation(out=Pb[(ent0 + n) % 3][:],
                                   in_=b.psP[n % 2][:, 0:TS],
                                   func=AF.Exp, bias=b.sm[:, 10 + j:11 + j],
                                   scale=SCL)
                S.inc("ax", ii, 1)

        @b.block.vector
        def _(v, h=h, ent0=ent0, ac0=ac0, sched_h=sched_h, eidx_h=eidx_h,
              grp0=grp0):
            for n, (qs, kt) in enumerate(sched_h):
                g = grp0 + qs
                if kt >= 4 * qs:
                    S.wait(v, "ax", ent0 + n + 1)
                    vv = kt - 4 * qs
                    st = _chain(S, v)
                    st(v.tensor_tensor(out=Pb[(ent0 + n) % 3][:],
                                       in0=Pb[(ent0 + n) % 3][:],
                                       in1=b.tri[:, vv * TS:(vv + 1) * TS],
                                       op=ALU.mult))
                    S.bump(v, "am", 1)
                if kt == 4 * (qs + 1) - 1:
                    S.wait(v, "ac", ac0 + n + 1)
                    if g >= 2:
                        S.wait(v, f"acw{g % 2}", (g // 2) * 16)
                    st = _chain(S, v)
                    st(v.tensor_copy(out=ctxl[g % 2][:],
                                     in_=b.psP[2 + g % 2][0:65, 0:TS]))
                    S.bump(v, "aev", 1)
                    st(v.reciprocal(out=rrow[g % 2][:], in_=ctxl[g % 2][64:65, :]))
                    S.bump(v, "arx", 1)
                    S.wait(v, "ab", g + 1)
                    st(v.tensor_tensor(out=cbnc[g % 2][:], in0=ctxl[g % 2][0:64, :],
                                       in1=b.psP[5][0:64, 0:TS], op=ALU.mult))
                    S.bump(v, "aee", 1)

        ent += NE
        grp += TQ
    b.acw_tot = [S.tok("acw0"), S.tok("acw1")]


def _agc(b, layer):
    S = b.S
    L = layer
    ag_in, ag_out = (b.ag0c_in, b.ag0c_out) if L == 0 else (b.ag1c_in, b.ag1c_out)

    @b.block.gpsimd
    def _(g):
        if L == 0:
            for jj in range(2):
                S.wait(g, f"acw{jj}", b.acw_tot[jj])
        else:
            for jj in range(2):
                S.wait(g, f"a1cw{jj}", b.c1w_tot[jj])
        ii = g.collective_compute("AllGather", ALU.bypass, replica_groups=b.PAIRS,
                                  ins=[ag_in.opt()], outs=[ag_out.opt()])
        b.__dict__[f"agc{L}cc"] = S.inc("cc", ii, 1)


def _m2_x1(b):
    S, c = b.S, b.cfg
    D, DL, I, NT, T = c["D"], c["DL"], c["I"], c["NT"], c["T"]
    KD, KI, NTT = c["KD"], c["KI"], c["NTT"]
    KLc = DL // 128
    NTB = NT // 512
    NOS = 2 if D >= 1024 else 1
    OSW = D // NOS
    W2 = min(512, OSW)
    NOS2 = OSW // W2
    psP = b.psP

    RV = b.RV
    SWB0 = max(16 * D, 16 * NT)
    hq = [RV("RC", F32, j * 4 * NT, 128, NT) for j in range(2)]
    hqi = [RV("RC", I32, 8 * NT + j * 4 * NT, 128, NT) for j in range(2)]
    hqb = [RV("RC", BF16, SWB0 + 8 * D + j * 2 * NT, 128, NT)
           for j in range(2)]

    def q2_store(y, i):
        j = i % 2
        S.wait(y, "q2v", i + 1)
        ii = y.dma_start(out=b.q2_dr[i * 128:(i + 1) * 128, :], in_=hqb[j][:])
        S.inc(f"q2w{j}", ii, 16)

    @b.block.sync
    def _(y):
        b.gate(y)
        for jb in range(4):
            S.wait(y, f"m1hx{jb}", None)
        for i in range(KI):
            j = i % 2
            if i >= 2:
                S.wait(y, "q2v", i - 1)
            ii = y.dma_start(out=hq[j][:], in_=b.h_sp[i * 128:(i + 1) * 128, :])
            S.inc(f"q2l{j}", ii, 16)
            if i >= 1:
                q2_store(y, i - 1)
        q2_store(y, KI - 1)

    if b.upto == "q2A":
        return

    @b.block.scalar
    def _(sc):
        S.wait(sc, "v", b.t_s2b)
        for i in range(KI):
            j = i % 2
            S.wait(sc, f"q2l{j}", (i // 2 + 1) * 16)
            if b.upto == "q2v2":
                ii = sc.activation(out=hq[j][:], in_=hq[j][:], func=AF.Copy,
                                   bias=MAGIC, scale=26.0)
            else:
                ii = sc.activation(out=hq[j][:], in_=hq[j][:], func=AF.Copy,
                                   bias=MAGIC, scale=b.rs2_b[:, 0:1])
            S.inc("q2s", ii, 1)
    if b.upto == "q2B":
        return

    @b.block.vector
    def _(v):
        if b.upto != "q2ng":
            b.gate(v)
        for i in range(KI):
            j = i % 2
            S.wait(v, "q2s", i + 1)
            if i >= 2:
                S.wait(v, f"q2w{j}", (i // 2) * 16)
            st = _chain(S, v)
            st(v.tensor_scalar(out=hqi[j][:], in0=hq[j][:], scalar1=-MAGIC,
                               scalar2=None, op0=ALU.add))
            st(v.tensor_scalar(out=hqb[j][:], in0=hqi[j][:], scalar1=0.0,
                               scalar2=None, op0=ALU.add))
            S.bump(v, "q2v", 1)

    if b.upto == "q2C":
        return

    b.q2w_tot = [S.tok("q2w0"), S.tok("q2w1")]
    if b.upto == "q2":
        return

    cb_ = b.CUR("RB")
    wop = [cb_(BF16, 128, OSW) for kc in range(KLc)]
    q2s_ = [cb_(BF16, 128, 512) for j in range(2)]
    qwS = [cb_(BF16, 128, OSW) for j in range(2)]
    ctxf = [cb_(BF16, 128, 512) for kc in range(KLc)]
    rbm_top = cb_.pos[0]
    ca_ = b.CUR("RA")
    x1b = [ca_(F32, 128, D) for tt in range(4)]
    x0m = [ca_(F32, 128, D) for j in range(2)]
    sqm = ca_(F32, 128, D)
    xob = [ca_(BF16, 128, D) for j in range(2)]
    n1 = min(KD, SWB0 // (2 * NT))
    n2 = min(KD - n1, (b.reg["RC"]["size"] - (SWB0 + 8 * D)) // (2 * NT))
    b.xn1_fm = []
    for k in range(KD):
        if k < n1:
            b.xn1_fm.append(RV("RC", BF16, k * 2 * NT, 128, NT))
        elif k < n1 + n2:
            b.xn1_fm.append(RV("RC", BF16, SWB0 + 8 * D + (k - n1) * 2 * NT,
                               128, NT))
        else:
            off = max(32768, ((rbm_top + 63) // 64) * 64)
            b.xn1_fm.append(RV("RB", BF16, off + (k - n1 - n2) * 2 * NT, 128, NT))
    statm = b.sb("statm", [128, 8], F32)

    @b.block.gpsimd
    def _(g):
        b.gate(g)
        for tb in range(NTB):
            for osp in range(NOS):
                nwo = tb * NOS + osp
                if nwo >= 1:
                    S.wait(g, "m2p", nwo)
                for kc in range(KLc):
                    ap = bass.AP(b.wo.tensor, kc * 128 * D + osp * OSW,
                                 [[D, 128], [1, OSW]])
                    ii = g.dma_start(out=wop[kc][:], in_=ap)
                    S.inc("m2wo", ii, 16)
                for i in range(KI):
                    j = i % 2
                    ni = nwo * KI + i
                    if ni >= 2:
                        S.wait(g, "m2ti", ni - 1)
                    ii = g.dma_start(out=qwS[j][:],
                                     in_=b.qw2T_dr[i * 128:(i + 1) * 128,
                                                   osp * OSW:(osp + 1) * OSW])
                    S.inc(f"m2qw{j}", ii, 16)

    def _m2_sync_body(y, co):
        for tb in range(NTB):
            for osp in range(NOS):
                nwo = tb * NOS + osp
                for i in range(KI):
                    j = i % 2
                    ni = nwo * KI + i
                    if ni >= 2:
                        S.wait(y, "m2ti", ni - 1)
                    ii = y.dma_start(out=q2s_[j][:],
                                     in_=b.q2_dr[i * 128:(i + 1) * 128,
                                                 tb * 512:(tb + 1) * 512])
                    S.inc(f"m2q2{j}", ii, 16)
                if osp == 0:
                    if tb >= 1:
                        S.wait(y, "m2p", tb * NOS)
                    for kc in range(KLc):
                        ap = bass.AP(b.ag0c_out.tensor,
                                     co + (kc * 128 * T + tb * 512),
                                     [[T, 128], [1, 512]])
                        ii = y.dma_start(out=ctxf[kc][:], in_=ap)
                        S.inc("m2cx", ii, 16)
                if osp == NOS - 1:
                    for tt4 in range(4):
                        n = tb * 4 + tt4
                        jx = n % 2
                        if n >= 2:
                            S.wait(y, "m2x1", n - 1)
                        ii = y.dma_start(out=x0m[jx][:],
                                         in_=b.x0[n * 128:(n + 1) * 128, :])
                        S.inc(f"m2x0{jx}", ii, 16)

    @b.block.sync
    def _(y):
        for jj in range(2):
            S.wait(y, f"w2st{jj}", b.w2st_tot[jj])
            S.wait(y, f"q2w{jj}", b.q2w_tot[jj])
        S.wait(y, "cc", b.agc0cc)
        for side in b.sswitch(y, "y"):
            _m2_sync_body(y, side * NT)

    @b.block.tensor
    def _(t):
        S.wait(t, "w2bc", b.t_w2bc)
        for nm_ in ("aee", "ax", "anv", "acc2", "am"):
            S.wait(t, nm_, None)
        for tb in range(NTB):
            for osp in range(NOS):
                nwo = tb * NOS + osp
                for i in range(KI):
                    j = i % 2
                    ni = nwo * KI + i
                    S.wait(t, f"m2qw{j}", (ni // 2 + 1) * 16)
                    S.wait(t, f"m2q2{j}", (ni // 2 + 1) * 16)
                    if i == 0 and nwo >= 1:
                        S.wait(t, "m2eB", nwo)
                    ii = None
                    for tt4 in range(4):
                        for os2 in range(NOS2):
                            ii = t.matmul(psP[tt4 * 2 + os2][:, 0:W2],
                                          q2s_[j][:, tt4 * 128:(tt4 + 1) * 128],
                                          qwS[j][:, os2 * W2:(os2 + 1) * W2],
                                          start=(i == 0), stop=(i == KI - 1))
                    S.inc("m2ti", ii, 1)
                S.wait(t, "m2ev1", nwo + 1)
                S.wait(t, "m2wo", (nwo + 1) * 16 * KLc)
                S.wait(t, "m2cx", (tb + 1) * 16 * KLc)
                ii = None
                for kc in range(KLc):
                    for tt4 in range(4):
                        for os2 in range(NOS2):
                            ii = t.matmul(psP[tt4 * 2 + os2][:, 0:W2],
                                          ctxf[kc][:, tt4 * 128:(tt4 + 1) * 128],
                                          wop[kc][:, os2 * W2:(os2 + 1) * W2],
                                          start=(kc == 0), stop=(kc == KLc - 1))
                S.inc("m2p", ii, 1)

    m2ti0 = S.tok("m2ti") - NTB * NOS * KI
    m2p0 = S.tok("m2p") - NTB * NOS
    u_sp = 16 * (KD + 1)
    svm0 = S.tok("ln1s")
    vln0 = S.tok("ln1v")

    @b.block.vector
    def _(v):
        for tb in range(NTB):
            for osp in range(NOS):
                nwo = tb * NOS + osp
                S.wait(v, "m2ti", m2ti0 + (nwo + 1) * KI)
                if tb >= 1 and osp == 0:
                    S.wait(v, "ln1tp0", 2 * tb * u_sp)
                    S.wait(v, "ln1tp1", 2 * tb * u_sp)
                st = _chain(S, v)
                for tt4 in range(4):
                    for os2 in range(NOS2):
                        csl = slice(osp * OSW + os2 * W2,
                                    osp * OSW + (os2 + 1) * W2)
                        st(v.tensor_tensor(
                            out=x1b[tt4][:, csl],
                            in0=psP[tt4 * 2 + os2][:, 0:W2],
                            in1=b.sw2bc[:, csl], op=ALU.mult))
                S.bump(v, "m2ev1", 1)
                S.wait(v, "m2p", m2p0 + nwo + 1)
                st = _chain(S, v)
                for tt4 in range(4):
                    for os2 in range(NOS2):
                        csl = slice(osp * OSW + os2 * W2,
                                    osp * OSW + (os2 + 1) * W2)
                        st(v.tensor_tensor(
                            out=x1b[tt4][:, csl],
                            in0=x1b[tt4][:, csl],
                            in1=psP[tt4 * 2 + os2][:, 0:W2], op=ALU.add))
                        st(v.tensor_tensor(out=x1b[tt4][:, csl],
                                           in0=x1b[tt4][:, csl],
                                           in1=b.bias0bc[:, csl], op=ALU.add))
                S.bump(v, "m2eB", 1)
            for tt4 in range(4):
                n = tb * 4 + tt4
                jx = n % 2
                jo = n % 2
                S.wait(v, f"m2x0{jx}", ((n // 2) + 1) * 16)
                st = _chain(S, v)
                st(v.tensor_tensor(out=x1b[tt4][:], in0=x1b[tt4][:], in1=x0m[jx][:],
                                   op=ALU.add))
                S.bump(v, "m2x1", 1)
                S.wait(v, "ln1s", svm0 + 2 * n + 1)
                v.reduce_sum(out=statm[:, 0:1], in_=x1b[tt4][:], axis=AX.X)
                st(v.reduce_sum(out=statm[:, 1:2], in_=sqm[:], axis=AX.X))
                st(v.tensor_scalar(out=statm[:, 2:3], in0=statm[:, 0:1],
                                   scalar1=-1.0 / D, scalar2=None, op0=ALU.mult))
                st(v.tensor_tensor(out=statm[:, 4:5], in0=statm[:, 2:3],
                                   in1=statm[:, 2:3], op=ALU.mult))
                st(v.tensor_scalar(out=statm[:, 3:4], in0=statm[:, 1:2],
                                   scalar1=1.0 / D, scalar2=None, op0=ALU.mult))
                st(v.tensor_tensor(out=statm[:, 3:4], in0=statm[:, 3:4],
                                   in1=statm[:, 4:5], op=ALU.subtract))
                st(v.tensor_scalar(out=statm[:, 3:4], in0=statm[:, 3:4],
                                   scalar1=LNEPS, scalar2=None, op0=ALU.add))
                S.bump(v, "ln1x", 1)
                S.wait(v, "ln1s", svm0 + 2 * n + 2)
                st(v.reciprocal(out=statm[:, 6:7], in_=statm[:, 5:6]))
                st(v.tensor_tensor(out=statm[:, 7:8], in0=statm[:, 6:7],
                                   in1=statm[:, 6:7], op=ALU.mult))
                st(v.tensor_tensor(out=statm[:, 7:8], in0=statm[:, 7:8],
                                   in1=statm[:, 3:4], op=ALU.mult))
                st(v.tensor_scalar(out=statm[:, 7:8], in0=statm[:, 7:8],
                                   scalar1=-0.5, scalar2=1.5, op0=ALU.mult,
                                   op1=ALU.add))
                st(v.tensor_tensor(out=statm[:, 6:7], in0=statm[:, 6:7],
                                   in1=statm[:, 7:8], op=ALU.mult))
                if n >= 2:
                    S.wait(v, f"ln1tp{jo}", ((n - 2) // 2 + 1) * u_sp)
                st(v.tensor_scalar(out=xob[jo][:], in0=x1b[tt4][:],
                                   scalar1=statm[:, 2:3], scalar2=statm[:, 6:7],
                                   op0=ALU.add, op1=ALU.mult))
                S.bump(v, "ln1v", 1)

    def ln1_spill(sc, n):
        tt4 = n % 4
        jo = n % 2
        S.wait(sc, "ln1v", vln0 + n + 1)
        ii = sc.dma_start(out=b.x1_sp[n * 128:(n + 1) * 128, :], in_=x1b[tt4][:])
        S.inc(f"ln1tp{jo}", ii, 16)
        for k in range(KD):
            ii = sc.dma_start_transpose(b.xn1_fm[k][:, n * 128:(n + 1) * 128],
                                        xob[jo][:, k * 128:(k + 1) * 128])
            S.inc(f"ln1tp{jo}", ii, 16)

    @b.block.scalar
    def _(sc):
        for n in range(NTT):
            S.wait(sc, "m2x1", n + 1)
            if n >= 1:
                S.wait(sc, "ln1v", vln0 + n)
            ii = sc.activation(out=sqm[:], in_=x1b[n % 4][:], func=AF.Square)
            S.inc("ln1s", ii, 1)
            S.wait(sc, "ln1x", n + 1)
            ii = sc.activation(out=statm[:, 5:6], in_=statm[:, 3:4], func=AF.Sqrt)
            S.inc("ln1s", ii, 1)
            ln1_spill(sc, n)
    b.ln1tp_tot = [S.tok("ln1tp0"), S.tok("ln1tp1")]


def _ffn1(b):
    S, c = b.S, b.cfg
    D, NT, T = c["D"], c["NT"], c["T"]
    KD, NTT = c["KD"], c["NTT"]
    NS = NT // 512
    NTB = NT // 512
    OC1 = min(512, D)
    NOS1 = D // OC1
    psP = b.psP

    cu = b.CUR("RA")
    wrt = cu(BF16, 128, 8 * KD)
    ats = cu(BF16, 128, 64 * KD)
    bcats = cu(BF16, 64, D)
    wbp = [b.RV("RB", BF16, kc * 2 * OC1, 128, OC1) for kc in range(KD)]
    logits_sb = b.RV("RB", F32, 2 * OC1 * KD, 8, NT)
    h1sb = b.RV("RB", F32, 2 * OC1 * KD + 4 * NT, 64, NT)
    assert 2 * OC1 * KD + 8 * NT <= 32768
    lt = cu(F32, 128, 8)
    gstat = cu(F32, 128, 6)
    eqs = cu(F32, 128, 16)
    wt_ = cu(F32, 128, 8)
    wfm = cu(F32, 8, NT)
    wfb = cu(BF16, 8, 512)
    hpb = cu(BF16, 64, NT)
    br_ = cu(F32, 2, D)
    bro = cu.last_off
    bbrow = b.RV("RA", F32, bro, 1, D, p0=0)
    bbasebc = cu(F32, 128, D)
    ybnc = [cu(F32, 128, 512) for j in range(2)]
    assert cu.pos[0] <= b.reg["RA"]["size"], cu.pos[0]

    @b.block.gpsimd
    def _(g):
        b.gate(g)
        for kq in range(KD):
            ap = bass.AP(b.WrT.tensor, kq * 128 * 8, [[8, 128], [1, 8]])
            ii = g.dma_start(out=wrt[:, kq * 8:(kq + 1) * 8], in_=ap)
            S.inc("f1w", ii, 16)
            ap = bass.AP(b.AT.tensor, kq * 128 * 64, [[64, 128], [1, 64]])
            ii = g.dma_start(out=ats[:, kq * 64:(kq + 1) * 64], in_=ap)
            S.inc("f1w", ii, 16)
        ii = g.dma_start(out=bcats[:], in_=b.Bcat)
        S.inc("f1w", ii, 16)
    f1w_tot = S.tok("f1w")

    @b.block.sync
    def _(y):
        b.gate(y)
        ii = y.dma_start(out=bbrow[:], in_=bass.AP(b.bbase.tensor, 0, [[D, 1], [1, D]]))
        S.inc("f1r", ii, 16)

    NSTR = NOS1

    @b.block.tensor
    def _(t):
        S.wait(t, "L1pe", None)
        S.wait(t, "m2eB", None)
        S.wait(t, "m2ti", None)
        S.wait(t, "ln1v", None)
        S.wait(t, "f1r", None)
        for sseg in range(NSTR):
            if sseg >= 1:
                S.wait(t, "f1bv", sseg)
            ii = t.matmul(psP[6][:, 0:OC1], b.onesr_f[:],
                          bbrow[:, sseg * OC1:(sseg + 1) * OC1], start=True, stop=True)
            S.inc("f1bt", ii, 1)
    f1bt0 = S.tok("f1bt") - NSTR

    @b.block.vector
    def _(v):
        b.gate(v)
        for sseg in range(NSTR):
            S.wait(v, "f1bt", f1bt0 + sseg + 1)
            st = _chain(S, v)
            st(v.tensor_copy(out=bbasebc[:, sseg * OC1:(sseg + 1) * OC1],
                             in_=psP[6][:, 0:OC1]))
            S.bump(v, "f1bv", 1)
    t_f1bc = S.tok("f1bv")

    @b.block.tensor
    def _(t):
        S.wait(t, "f1w", f1w_tot)
        for ss in range(NS):
            if ss >= 1:
                S.wait(t, "f1le", 2 * ss)
            ii = None
            for k in range(KD):
                ii = t.matmul(psP[4][0:8, 0:512], wrt[:, k * 8:(k + 1) * 8],
                              b.xn1_fm[k][:, ss * 512:(ss + 1) * 512],
                              start=(k == 0), stop=(k == KD - 1))
            S.inc("f1l", ii, 1)
            for k in range(KD):
                ii = t.matmul(psP[5][0:64, 0:512], ats[:, k * 64:(k + 1) * 64],
                              b.xn1_fm[k][:, ss * 512:(ss + 1) * 512],
                              start=(k == 0), stop=(k == KD - 1))
            S.inc("f1l", ii, 1)
    f1l0 = S.tok("f1l") - 2 * NS

    @b.block.vector
    def _(v):
        for ss in range(NS):
            S.wait(v, "f1l", f1l0 + 2 * ss + 1)
            st = _chain(S, v)
            st(v.tensor_copy(out=logits_sb[:, ss * 512:(ss + 1) * 512],
                             in_=psP[4][0:8, 0:512]))
            S.bump(v, "f1le", 1)
            S.wait(v, "f1l", f1l0 + 2 * ss + 2)
            st(v.tensor_copy(out=h1sb[:, ss * 512:(ss + 1) * 512],
                             in_=psP[5][0:64, 0:512]))
            S.bump(v, "f1le", 1)
    f1le_tot = S.tok("f1le")

    sg0 = S.tok("f1gs")

    @b.block.tensor
    def _(t):
        for tt in range(NTT):
            S.wait(t, "f1le", f1le_tot)
            if tt == 0:
                S.wait(t, "f1bv", t_f1bc)
            if tt >= 1:
                S.wait(t, "f1gv", 2 * tt)
            ii = t.transpose(psP[6][0:128, 0:8],
                             logits_sb[:, tt * 128:(tt + 1) * 128],
                             b.idT[0:8, 0:8])
            S.inc("f1gt", ii, 1)
            S.wait(t, "f1gv", 2 * tt + 1)
            ii = t.transpose(psP[6][0:8, 8:136],
                             wt_[:], b.idT[:])
            S.inc("f1gt", ii, 1)
    f1gt0 = S.tok("f1gt") - 2 * NTT

    @b.block.vector
    def _(v):
        for tt in range(NTT):
            S.wait(v, "f1gt", f1gt0 + 2 * tt + 1)
            st = _chain(S, v)
            st(v.tensor_copy(out=lt[:], in_=psP[6][0:128, 0:8]))
            st(v.reduce_max(out=gstat[:, 0:1], in_=lt[:], axis=AX.X))
            st(v.tensor_scalar(out=eqs[:, 0:8], in0=lt[:], scalar1=gstat[:, 0:1],
                               scalar2=None, op0=ALU.is_equal))
            st(v.scalar_tensor_tensor(out=eqs[:, 8:16], in0=eqs[:, 0:8],
                                      scalar=-BIG, in1=lt[:],
                                      op0=ALU.mult, op1=ALU.add))
            st(v.reduce_max(out=gstat[:, 1:2], in_=eqs[:, 8:16], axis=AX.X))
            st(v.tensor_scalar(out=eqs[:, 8:16], in0=eqs[:, 8:16],
                               scalar1=gstat[:, 1:2], scalar2=None, op0=ALU.is_equal))
            st(v.tensor_tensor(out=gstat[:, 2:3], in0=gstat[:, 0:1],
                               in1=gstat[:, 1:2], op=ALU.subtract))
            S.bump(v, "f1gd", 1)
            S.wait(v, "f1gg", tt + 1)
            st(v.tensor_scalar(out=gstat[:, 4:5], in0=gstat[:, 3:4], scalar1=-1.0,
                               scalar2=1.0, op0=ALU.mult, op1=ALU.add))
            st(v.tensor_scalar(out=wt_[:], in0=eqs[:, 0:8], scalar1=gstat[:, 3:4],
                               scalar2=2.0, op0=ALU.mult, op1=ALU.mult))
            st(v.scalar_tensor_tensor(out=eqs[:, 8:16], in0=eqs[:, 8:16],
                                      scalar=gstat[:, 4:5], in1=wt_[:],
                                      op0=ALU.mult, op1=ALU.bypass))
            st(v.tensor_scalar(out=eqs[:, 8:16], in0=eqs[:, 8:16], scalar1=2.0,
                               scalar2=None, op0=ALU.mult))
            st(v.tensor_tensor(out=wt_[:], in0=wt_[:], in1=eqs[:, 8:16], op=ALU.add))
            S.bump(v, "f1gv", 1)
            S.wait(v, "f1gt", f1gt0 + 2 * tt + 2)
            st(v.tensor_copy(out=wfm[:, tt * 128:(tt + 1) * 128],
                             in_=psP[6][0:8, 8:136]))
            S.bump(v, "f1gv", 1)
    f1gv_tot = S.tok("f1gv")

    @b.block.scalar
    def _(sc):
        for tt in range(NTT):
            S.wait(sc, "f1gd", tt + 1)
            ii = sc.activation(out=gstat[:, 3:4], in_=gstat[:, 2:3], func=AF.Sigmoid)
            S.inc("f1gg", ii, 1)

    @b.block.tensor
    def _(t):
        S.wait(t, "f1gv", f1gv_tot)
        for ss in range(NS):
            S.wait(t, "f1hv", 2 * ss + 1)
            ii = t.matmul(psP[7][0:64, 0:512], b.r8[:],
                          wfb[:], start=True, stop=True)
            S.inc("f1ht", ii, 1)
    f1ht0 = S.tok("f1ht") - NS

    @b.block.vector
    def _(v):
        for ss in range(NS):
            st = _chain(S, v)
            if ss >= 1:
                S.wait(v, "f1ht", f1ht0 + ss)
            st(v.tensor_copy(out=wfb[:], in_=wfm[:, ss * 512:(ss + 1) * 512]))
            S.bump(v, "f1hv", 1)
            S.wait(v, "f1ht", f1ht0 + ss + 1)
            st(v.tensor_tensor(out=hpb[:, ss * 512:(ss + 1) * 512],
                               in0=h1sb[:, ss * 512:(ss + 1) * 512],
                               in1=psP[7][0:64, 0:512], op=ALU.mult))
            S.bump(v, "f1hv", 1)
    f1hp_tot = S.tok("f1hv")

    @b.block.gpsimd
    def _(g):
        for os in range(NOS1):
            if os >= 1:
                S.wait(g, "f1p", os * NTB * 4)
            for kc in range(KD):
                ap = bass.AP(b.WbT.tensor, kc * 128 * D + os * OC1,
                             [[D, 128], [1, OC1]])
                ii = g.dma_start(out=wbp[kc][:], in_=ap)
                S.inc("f1wb", ii, 16)

    @b.block.tensor
    def _(t):
        S.wait(t, "f1hv", f1hp_tot)
        np_ = 0
        for os in range(NOS1):
            S.wait(t, "f1wb", (os + 1) * 16 * KD)
            for tb in range(NTB):
                for tt4 in range(4):
                    n = tb * 4 + tt4
                    if np_ >= 4:
                        S.wait(t, "f1ye", np_ - 3)
                    ii = None
                    for k in range(KD):
                        ii = t.matmul(psP[n % 4][:, 0:OC1],
                                      b.xn1_fm[k][:, n * 128:(n + 1) * 128],
                                      wbp[k][:], start=(k == 0), stop=False)
                    ii = t.matmul(psP[n % 4][:, 0:OC1],
                                  hpb[:, n * 128:(n + 1) * 128],
                                  bcats[:, os * OC1:(os + 1) * OC1],
                                  start=False, stop=True)
                    S.inc("f1p", ii, 1)
                    np_ += 1
    f1p0 = S.tok("f1p") - NOS1 * NTB * 4

    @b.block.vector
    def _(v):
        S.wait(v, "f1bv", t_f1bc)
        np_ = 0
        for os in range(NOS1):
            for tb in range(NTB):
                for tt4 in range(4):
                    n = tb * 4 + tt4
                    jb = np_ % 2
                    S.wait(v, "f1p", f1p0 + np_ + 1)
                    if np_ >= 2:
                        S.wait(v, f"f1ys{jb}", (np_ // 2) * 16)
                    st = _chain(S, v)
                    st(v.tensor_tensor(out=ybnc[jb][:, 0:OC1],
                                       in0=psP[n % 4][:, 0:OC1],
                                       in1=bbasebc[:, os * OC1:(os + 1) * OC1],
                                       op=ALU.add))
                    S.bump(v, "f1ye", 1)
                    S.bump(v, "f1yv", 1)
                    np_ += 1

    @b.block.sync
    def _(y):
        np_ = 0
        for os in range(NOS1):
            for tb in range(NTB):
                for tt4 in range(4):
                    n = tb * 4 + tt4
                    jb = np_ % 2
                    S.wait(y, "f1yv", np_ + 1)
                    ii = y.dma_start(out=b.y1_sp[n * 128:(n + 1) * 128,
                                                 os * OC1:(os + 1) * OC1],
                                     in_=ybnc[jb][:, 0:OC1])
                    S.inc(f"f1ys{jb}", ii, 16)
                    np_ += 1
    b.f1ys_tot = [S.tok("f1ys0"), S.tok("f1ys1")]


def _l1_select(b):
    S, c = b.S, b.cfg
    D, NT, T = c["D"], c["NT"], c["T"]
    NH1m = c["NH1m"]
    SCL1 = 1.0 / math.sqrt(128.0)
    NSQ = T // 512

    cu = b.CUR("RA")
    q1h = cu(BF16, 128, T)
    sq1 = cu(F32, 128, 512)
    nrow = cu(F32, 1, 512)
    knm = cu(F32, 1, 4)
    kn8 = cu(F32, NH1m, 2)
    nq = cu(F32, NH1m, T)
    nqa = cu(F32, NH1m, T)
    nqb = cu(F32, NH1m, T)
    m8 = cu(F32, NH1m, 8)
    m8w = cu(F32, NH1m, 8)
    m8e = cu(F32, NH1m, 8)
    th = cu(F32, NH1m, 2)
    mask = nqa
    invm = nqb
    negidx = cu(F32, NH1m, 64)
    crow = cu(F32, 1, NH1m)
    assert cu.pos[0] <= b.reg["RA"]["size"], cu.pos[0]


    @b.block.sync
    def _(y):
        S.wait(y, "cc", b.L1pcc)
        for side in b.sswitch(y, "y"):
            ro = side * NH1m * 128 * NT
            for h in range(NH1m):
                for w_ in range(2):
                    S.wait(y, "sl_s", (2 * h + w_) * NSQ)
                    for tb in range(2):
                        src = bass.AP(b.ag1_out.tensor,
                                      (tb * 3 * D + w_ * D + h * 128) * NT + ro,
                                      [[NT, 128], [1, NT]])
                        ii = y.dma_start(out=q1h[:, tb * NT:(tb + 1) * NT],
                                         in_=src)
                        S.inc("sl_l", ii, 16)


    for h in range(NH1m):
        sqn0 = S.tok("sl_t")

        @b.block.scalar
        def _(sc, h=h):
            def nstore(st_):
                S.wait(sc, "sl_nr", h * NSQ + st_ + 1)
                ii = sc.dma_start(out=b.nrm_dr[h, st_ * 512:(st_ + 1) * 512],
                                  in_=nrow[0:1, :])
                S.inc("sl_nw", ii, 16)
            for w_ in range(2):
                S.wait(sc, "sl_l", (2 * h + w_ + 1) * 32)
                for st_ in range(NSQ):
                    n = w_ * NSQ + st_
                    if n >= 1:
                        S.wait(sc, "sl_t", sqn0 + n)
                    ii = sc.activation(out=sq1[:], in_=q1h[:, st_ * 512:(st_ + 1) * 512],
                                       func=AF.Square)
                    S.inc("sl_s", ii, 1)
                    if w_ == 0 and st_ >= 1:
                        nstore(st_ - 1)
                    if w_ == 1 and st_ == 0:
                        nstore(NSQ - 1)
            S.wait(sc, "sl_v", h + 1)
            ii = sc.dma_start(out=b.row_dr[4224 + h:4225 + h],
                              in_=knm[0:1, 0:1])
            S.inc("sl_kw", ii, 16)

        sls0 = S.tok("sl_s") - 2 * NSQ

        @b.block.tensor
        def _(t, h=h, sls0=sls0, sqn0=sqn0):
            if h == 0:
                S.wait(t, "f1p", None)
                S.wait(t, "f1gv", None)
            for n in range(2 * NSQ):
                S.wait(t, "sl_s", sls0 + n + 1)
                if n >= 1:
                    S.wait(t, "sl_v2", sqn0 + n)
                ii = t.matmul(b.psP[4][0:1, 0:512], b.ones_f[:], sq1[:],
                              start=True, stop=True)
                S.inc("sl_t", ii, 1)

        @b.block.vector
        def _(v, h=h, sqn0=sqn0):
            st = _chain(S, v)
            if h >= 1:
                S.wait(v, "sl_kw", h * 16)
            st(v.memset(knm[:], 0.0))
            for w_ in range(2):
                for st_ in range(NSQ):
                    n = w_ * NSQ + st_
                    S.wait(v, "sl_t", sqn0 + n + 1)
                    if w_ == 0:
                        S.wait(v, "sl_nw", (h * NSQ + st_) * 16)
                        st(v.tensor_copy(out=nrow[:], in_=b.psP[4][0:1, 0:512]))
                        S.bump(v, "sl_nr", 1)
                    else:
                        st(v.reduce_max(out=knm[0:1, 1:2], in_=b.psP[4][0:1, 0:512],
                                        axis=AX.X))
                        st(v.tensor_tensor(out=knm[0:1, 0:1], in0=knm[0:1, 0:1],
                                           in1=knm[0:1, 1:2], op=ALU.max))
                    S.bump(v, "sl_v2", 1)
            S.bump(v, "sl_v", 1)


    @b.block.sync
    def _(y):
        S.wait(y, "sl_nw", None)
        ii = y.dma_start(out=nq[:], in_=b.nrm_dr)
        S.inc("sl_q", ii, 16)
        S.wait(y, "sl_kw", None)
        ii = y.dma_start(out=kn8[:, 0:1],
                         in_=bass.AP(b.row_dr.tensor, 4224, [[1, NH1m], [1, 1]]))
        S.inc("sl_q", ii, 16)

    @b.block.vector
    def _(v):
        S.wait(v, "sl_q", None)
        st = _chain(S, v)
        st(v.tensor_copy(out=nqa[:], in_=nq[:]))
        cur, oth = nqa, nqb
        for r in range(7):
            st(v.max(out=m8[:], in_=cur[:]))
            st(v.match_replace(out=oth[:], in_to_replace=m8[:], in_values=cur[:],
                               imm_value=-BIG))
            cur, oth = oth, cur
        st(v.max(out=m8[:], in_=cur[:]))
        st(v.tensor_copy(out=m8w[:], in_=m8[:]))
        for r in range(6):
            st(v.reduce_max(out=th[:, 0:1], in_=m8w[:], axis=AX.X))
            if r < 5:
                st(v.tensor_scalar(out=m8e[:], in0=m8w[:], scalar1=th[:, 0:1],
                                   scalar2=None, op0=ALU.is_equal))
                st(v.scalar_tensor_tensor(out=m8w[:], in0=m8e[:], scalar=-BIG,
                                          in1=m8w[:], op0=ALU.mult, op1=ALU.add))
        st(v.tensor_scalar(out=mask[:], in0=nq[:], scalar1=th[:, 0:1],
                           scalar2=None, op0=ALU.is_ge))
        st(v.memset(mask[:, 0:1], 1.0))
        st(v.memset(mask[:, T - 1:T], 1.0))
        st(v.tensor_scalar(out=invm[:], in0=mask[:], scalar1=-1.0, scalar2=1.0,
                           op0=ALU.mult, op1=ALU.add))
        S.bump(v, "sl_inv", 1)
        S.wait(v, "sl_iw", 16)
        for ss in range(T // 512):
            sl = slice(ss * 512, (ss + 1) * 512)
            st(v.scalar_tensor_tensor(out=nqa[:, sl], in0=b.iota[0:NH1m, :],
                                      scalar=float(ss * 512), in1=mask[:, sl],
                                      op0=ALU.add, op1=ALU.mult))
            st(v.scalar_tensor_tensor(out=nqa[:, sl], in0=invm[:, sl], scalar=BIG,
                                      in1=nqa[:, sl], op0=ALU.mult, op1=ALU.add))
        st(v.tensor_scalar(out=nqa[:], in0=nqa[:], scalar1=-1.0, scalar2=None,
                           op0=ALU.mult))
        cur, oth = nqa, nqb
        for r in range(8):
            st(v.max(out=m8[:], in_=cur[:]))
            st(v.tensor_copy(out=negidx[:, r * 8:(r + 1) * 8], in_=m8[:]))
            if r < 7:
                st(v.match_replace(out=oth[:], in_to_replace=m8[:], in_values=cur[:],
                                   imm_value=-BIG))
                cur, oth = oth, cur
        st(v.tensor_scalar(out=negidx[:], in0=negidx[:], scalar1=-1.0, scalar2=None,
                           op0=ALU.mult))
        S.bump(v, "sl_idx", 1)
        st(v.reduce_max(out=kn8[:, 1:2], in_=nq[:], axis=AX.X))
        st(v.tensor_tensor(out=kn8[:, 0:1], in0=kn8[:, 0:1], in1=kn8[:, 1:2],
                           op=ALU.add))
        st(v.tensor_scalar(out=kn8[:, 0:1], in0=kn8[:, 0:1], scalar1=-0.5 * SCL1,
                           scalar2=None, op0=ALU.mult))
        S.bump(v, "sl_c", 1)

    @b.block.sync
    def _(y):
        S.wait(y, "sl_inv", None)
        ii = y.dma_start(out=b.inv_dr, in_=invm[:])
        S.inc("sl_iw", ii, 16)
        S.wait(y, "sl_idx", None)
        ii = y.dma_start(out=b.idx_dr, in_=negidx[:])
        S.inc("sl_iw", ii, 16)
        S.wait(y, "sl_c", None)
        ii = y.dma_start(out=b.row_dr[4096:4096 + NH1m], in_=kn8[:, 0:1])
        S.inc("sl_iw", ii, 16)
        S.wait(y, "sl_iw", None)
        ii = y.dma_start(out=crow[:], in_=bass.AP(b.row_dr.tensor, 4096,
                                                  [[NH1m, 1], [1, NH1m]]))
        S.inc("sl_cr", ii, 16)

    @b.block.tensor
    def _(t):
        S.wait(t, "sl_cr", None)
        ii = t.matmul(b.psP[6][:, 0:NH1m], b.onesr_f[:], crow[:],
                      start=True, stop=True)
        S.inc("sl_ct", ii, 1)
    t_slct = S.tok("sl_ct")

    @b.block.vector
    def _(v):
        S.wait(v, "sl_ct", t_slct)
        st = _chain(S, v)
        st(v.tensor_copy(out=b.biasC1[:, 0:NH1m], in_=b.psP[6][:, 0:NH1m]))
        S.bump(v, "sl_cb", 1)
    b.t_slcb = S.tok("sl_cb")
    b.sel_idx_done = S.tok("sl_iw")


def _attn1(b):
    S, c = b.S, b.cfg
    D, NT, T = c["D"], c["NT"], c["T"]
    NH1m, NKT = c["NH1m"], c["NKT"]
    SCL1 = 1.0 / math.sqrt(128.0)
    psP = b.psP

    cu = b.CUR("RA")
    q1h = [cu(BF16, 128, T) for j in range(2)]
    k1h = [cu(BF16, 128, T) for j in range(2)]
    vt1 = [cu(BF16, 128, 129 * NKT) for j in range(2)]
    idxc = [cu(F32, 64, 1) for j in range(2)]
    qtm = [cu(BF16, 128, 128) for j in range(2)]
    stile = [cu(BF16, 128, 64) for j in range(2)]
    qsel = cu(BF16, 128, 64)
    Pg = [cu(BF16, 128, 64) for j in range(2)]
    ctxN = cu(BF16, 64, 129)
    cgl = cu(F32, 64, 129)
    Pl = [cu(BF16, 128, 128) for j in range(3)]
    cll = [cu(F32, 128, 128) for j in range(2)]
    lr_ = cu(F32, 2, 256)
    lro = cu.last_off
    lrow = [b.RV("RA", F32, lro + jj * 512, 1, 128) for jj in range(2)]
    cb1 = [cu(BF16, 128, 128) for j in range(2)]
    assert cu.pos[0] <= b.reg["RA"]["size"], cu.pos[0]
    ssel = [b.RV("RB", BF16, j * 2 * T, 64, T) for j in range(2)]
    invr = [b.RV("RB", F32, b.reg["RB"]["size"] - (2 - jj) * 4 * T, 1, T)
            for jj in range(2)]
    vb2 = [b.RV("RW", BF16, sl * 256, 128, 128) for sl in range(4)]
    S.revive(*[f"avb{sl}" for sl in range(4)], *[f"acp{sl}" for sl in range(4)])

    @b.block.vector
    def _(v):
        b.gate(v)
        for j in range(2):
            ap = bass.AP(vt1[j].tensor, vt1[j].offset + 128,
                         [[vt1[j].tensor.shape[1], 128], [129, NKT], [1, 1]])
            ii = v.memset(ap, 1.0)
            S.inc("a1vo", ii, 1)
        for jj in range(3):
            ii = v.memset(Pl[jj][:], 0.0)
            S.inc("a1vo", ii, 1)

    head_end = []

    @b.block.sync
    def _(y):
        b.gate(y)
        for side in b.sswitch(y, "y"):
            ro = side * NH1m * 128 * NT
            for h in range(NH1m):
                j = h % 2
                if h == 0:
                    S.wait(y, "sl_iw", b.sel_idx_done)
                if h >= 2:
                    S.wait(y, "a1ce", h - 1)
                for tb in range(2):
                    src = bass.AP(b.ag1_out.tensor,
                                  (tb * 3 * D + h * 128) * NT + ro,
                                  [[NT, 128], [1, NT]])
                    ii = y.dma_start(out=q1h[j][:, tb * NT:(tb + 1) * NT],
                                     in_=src)
                    S.inc(f"a1l{j}", ii, 16)
                    src = bass.AP(b.ag1_out.tensor,
                                  (tb * 3 * D + D + h * 128) * NT + ro,
                                  [[NT, 128], [1, NT]])
                    ii = y.dma_start(out=k1h[j][:, tb * NT:(tb + 1) * NT],
                                     in_=src)
                    S.inc(f"a1l{j}", ii, 16)
                for kt in range(NKT):
                    tb = kt * 128 // NT
                    col = kt * 128 - tb * NT
                    n = h * NKT + kt
                    sl = n % 4
                    src = bass.AP(b.ag1_out.tensor,
                                  (tb * 3 * D + 2 * D + h * 128) * NT + ro + col,
                                  [[NT, 128], [1, 128]])
                    if n >= 4:
                        S.wait(y, f"acp{sl}", (n // 4) * 16)
                    ii = y.dma_start_transpose(vb2[sl][:], src)
                    S.inc(f"avb{sl}", ii, 16)
                ii = y.dma_start(out=idxc[j][:],
                                 in_=bass.AP(b.idx_dr.tensor, h * 64,
                                             [[1, 64], [1, 1]]))
                S.inc(f"a1l{j}", ii, 16)
                ii = y.dma_start(out=invr[j][:], in_=b.inv_dr[h:h + 1, :])
                S.inc(f"a1l{j}", ii, 16)
                S.wait(y, "a1sb", h + 1)
                for kt in range(NKT):
                    jj = kt % 2
                    if kt >= 2:
                        S.wait(y, "a1gt", h * NKT + kt - 1)
                    ii = y.dma_start_transpose(qtm[jj][:],
                                               q1h[j][:, kt * 128:(kt + 1) * 128])
                    S.inc(f"a1tq{jj}", ii, 16)
                    ii = y.dma_start_transpose(stile[jj][:],
                                               ssel[j][:, kt * 128:(kt + 1) * 128])
                    S.inc(f"a1tq{jj}", ii, 16)
                for qt in range(NKT):
                    ng = h * NKT + qt
                    S.wait(y, "a1ee", ng + 1)
                    ii = y.dma_start(out=b.ag1c_in[h * 128:(h + 1) * 128,
                                                   qt * 128:(qt + 1) * 128],
                                     in_=cb1[qt % 2][:])
                    S.inc(f"a1cw{qt % 2}", ii, 16)

    for h in range(NH1m):
        j = h % 2

        @b.block.vector
        def _(v, h=h, j=j):
            st = _chain(S, v)
            for kt in range(NKT):
                n = h * NKT + kt
                sl = n % 4
                S.wait(v, f"avb{sl}", (n // 4 + 1) * 16)
                st(v.tensor_copy(out=vt1[j][:, kt * 129:kt * 129 + 128],
                                 in_=vb2[sl][:]))
                S.bump(v, f"acp{sl}", 16)

        @b.block.vector
        def _(v, h=h, j=j):
            S.wait(v, f"a1l{j}", (h // 2 + 1) * (4 + 2) * 16)
            if h >= 2:
                S.wait(v, "a1ge", h - 1)
                S.wait(v, "a1sc", (h - 1) * NKT)
            st = _chain(S, v)
            for ss in range(T // 512):
                st(v.tensor_scalar(out=ssel[j][:, ss * 512:(ss + 1) * 512],
                                   in0=b.iota[0:64, :],
                                   scalar1=float(ss * 512),
                                   scalar2=idxc[j][:, 0:1],
                                   op0=ALU.add, op1=ALU.is_equal))
            S.bump(v, "a1sb", 1)

        a1g0 = S.tok("a1gt")

        @b.block.tensor
        def _(t, h=h, j=j, a1g0=a1g0):
            if h == 0:
                S.wait(t, "sl_v2", None)
                S.wait(t, "a1vo", 5)
                S.wait(t, "sl_cb", b.t_slcb)
            for kt in range(NKT):
                jj = kt % 2
                S.wait(t, f"a1tq{jj}", (h * NKT // 2 + kt // 2 + 1) * 32)
                ii = t.matmul(psP[4][:, 0:64], qtm[jj][:], stile[jj][:],
                              start=(kt == 0), stop=(kt == NKT - 1))
                S.inc("a1gt", ii, 1)

        @b.block.vector
        def _(v, h=h):
            S.wait(v, "a1gt", None)
            st = _chain(S, v)
            st(v.tensor_copy(out=qsel[:], in_=psP[4][:, 0:64]))
            S.bump(v, "a1ge", 1)

        a1x0 = S.tok("a1gx")
        a1c0 = S.tok("a1gc")

        @b.block.tensor
        def _(t, h=h, j=j, a1x0=a1x0):
            S.wait(t, "a1ge", h + 1)
            for sl_ in range(4):
                S.wait(t, f"acp{sl_}", (h + 1) * (NKT // 4) * 16)
            def gctx(kt):
                S.wait(t, "a1gx", a1x0 + kt + 1)
                ii = t.matmul(psP[2][0:64, 0:129],
                              Pg[kt % 2][:], vt1[j][:, kt * 129:(kt + 1) * 129],
                              start=(kt == 0), stop=(kt == NKT - 1))
                S.inc("a1gc", ii, 1)
            for kt in range(NKT):
                if kt >= 2:
                    S.wait(t, "a1gx", a1x0 + kt - 1)
                ii = t.matmul(psP[kt % 2][:, 0:64],
                              k1h[j][:, kt * 128:(kt + 1) * 128],
                              qsel[:], start=True, stop=True)
                S.inc("a1gs", ii, 1)
                if kt >= 1:
                    gctx(kt - 1)
            gctx(NKT - 1)

        a1s0 = S.tok("a1gs") - NKT

        @b.block.scalar
        def _(sc, h=h, a1s0=a1s0, a1c0=a1c0):
            for kt in range(NKT):
                S.wait(sc, "a1gs", a1s0 + kt + 1)
                if kt >= 2:
                    S.wait(sc, "a1gc", a1c0 + kt - 1)
                ii = sc.activation(out=Pg[kt % 2][:], in_=psP[kt % 2][:, 0:64],
                                   func=AF.Exp, bias=b.biasC1[:, h:h + 1],
                                   scale=SCL1)
                S.inc("a1gx", ii, 1)

        @b.block.vector
        def _(v, h=h):
            S.wait(v, "a1gc", None)
            st = _chain(S, v)
            st(v.tensor_copy(out=cgl[:], in_=psP[2][0:64, 0:129]))
            st(v.reciprocal(out=cgl[:, 128:129], in_=cgl[:, 128:129]))
            st(v.tensor_scalar(out=ctxN[:, 0:128], in0=cgl[:, 0:128],
                               scalar1=cgl[:, 128:129], scalar2=None, op0=ALU.mult))
            S.bump(v, "a1cn", 1)

        loc = []
        for qt in range(NKT):
            ktts = [k_ for k_ in (qt - 1, qt, qt + 1) if 0 <= k_ < NKT]
            loc.append((qt, ktts))
        a1lx0 = S.tok("a1lx")
        a1ls0 = S.tok("a1ls")
        a1lc0 = S.tok("a1lc")
        nen = 0
        entl = []
        for (qt, ktts) in loc:
            for ktt in ktts:
                entl.append((qt, ktt))
                nen += 1

        @b.block.tensor
        def _(t, h=h, j=j, loc=loc, a1lx0=a1lx0, entl=entl):
            S.wait(t, "a1cn", h + 1)

            def lctx(ne, qt, ktt, ktts):
                S.wait(t, "a1lx", a1lx0 + ne + 1)
                ii = t.matmul(psP[2][:, 0:128],
                              vt1[j][:, ktt * 129:ktt * 129 + 128],
                              Pl[ne % 3][:],
                              start=(ktt == ktts[0]), stop=(ktt == ktts[-1]))
                S.inc("a1lc", ii, 1)
                ii = t.matmul(psP[3][0:1, 0:128],
                              vt1[j][:, ktt * 129 + 128:ktt * 129 + 129],
                              Pl[ne % 3][:],
                              start=(ktt == ktts[0]), stop=(ktt == ktts[-1]))
                S.inc("a1lc", ii, 1)
                if ktt == ktts[-1]:
                    ng = loc.index((qt, ktts))
                    S.wait(t, "a1sb", h + 1)
                    if h * NKT + ng >= 1:
                        S.wait(t, "a1ee", h * NKT + ng)
                    ii = t.matmul(psP[5][:, 0:128], ctxN[:, 0:128],
                                  ssel[j][:, qt * 128:(qt + 1) * 128],
                                  start=True, stop=True)
                    S.inc("a1sc", ii, 1)
                    S.wait(t, "a1rx", h * NKT + ng + 1)
                    ii = t.matmul(psP[6][:, 0:128], b.onesr_f[:], lrow[ng % 2][:],
                                  start=True, stop=True)
                    S.inc("a1rb", ii, 1)

            prev = None
            ne = 0
            for (qt, ktts) in loc:
                for ktt in ktts:
                    if ne >= 2:
                        S.wait(t, "a1lx", a1lx0 + ne - 1)
                    ii = t.matmul(psP[ne % 2][:, 0:128],
                                  k1h[j][:, ktt * 128:(ktt + 1) * 128],
                                  q1h[j][:, qt * 128:(qt + 1) * 128],
                                  start=True, stop=True)
                    S.inc("a1ls", ii, 1)
                    if prev is not None:
                        lctx(*prev)
                    prev = (ne, qt, ktt, ktts)
                    ne += 1
            lctx(*prev)

        @b.block.scalar
        def _(sc, h=h, entl=entl, a1ls0=a1ls0, a1lc0=a1lc0):
            for ne, (qt, ktt) in enumerate(entl):
                S.wait(sc, "a1ls", a1ls0 + ne + 1)
                if ne >= 3:
                    S.wait(sc, "a1lc", a1lc0 + 2 * (ne - 2))
                if ktt < qt:
                    dst = Pl[ne % 3][64:128, :]
                    srcp = psP[ne % 2][64:128, 0:128]
                    bia = b.biasC1[64:128, h:h + 1]
                elif ktt > qt:
                    dst = Pl[ne % 3][0:64, :]
                    srcp = psP[ne % 2][0:64, 0:128]
                    bia = b.biasC1[0:64, h:h + 1]
                else:
                    dst = Pl[ne % 3][:]
                    srcp = psP[ne % 2][:, 0:128]
                    bia = b.biasC1[:, h:h + 1]
                ii = sc.activation(out=dst, in_=srcp, func=AF.Exp,
                                   bias=bia, scale=SCL1)
                S.inc("a1lx", ii, 1)

        @b.block.vector
        def _(v, h=h, loc=loc, a1lc0=a1lc0):
            ng = 0
            for (qt, ktts) in loc:
                last = a1lc0 + 2 * sum(len(k_) for (q_, k_) in loc[:loc.index((qt, ktts)) + 1])
                S.wait(v, "a1lc", last)
                if h * NKT + ng >= 2:
                    S.wait(v, f"a1cw{ng % 2}", ((h * NKT + ng) // 2) * 16)
                st = _chain(S, v)
                st(v.tensor_copy(out=cll[ng % 2][:], in_=psP[2][:, 0:128]))
                st(v.tensor_copy(out=lrow[ng % 2][:], in_=psP[3][0:1, 0:128]))
                S.bump(v, "a1le", 1)
                st(v.reciprocal(out=lrow[ng % 2][:], in_=lrow[ng % 2][:]))
                st(v.tensor_tensor(out=lrow[ng % 2][:], in0=lrow[ng % 2][:],
                                   in1=invr[h % 2][:, qt * 128:(qt + 1) * 128],
                                   op=ALU.mult))
                S.bump(v, "a1rx", 1)
                S.wait(v, "a1rb", h * NKT + ng + 1)
                S.wait(v, "a1sc", h * NKT + ng + 1)
                st(v.tensor_tensor(out=cll[ng % 2][:], in0=cll[ng % 2][:],
                                   in1=psP[6][:, 0:128], op=ALU.mult))
                st(v.tensor_tensor(out=cb1[ng % 2][:], in0=cll[ng % 2][:],
                                   in1=psP[5][:, 0:128], op=ALU.add))
                S.bump(v, "a1ee", 1)
                ng += 1

        @b.block.vector
        def _(v, h=h):
            S.wait(v, "a1ee", (h + 1) * NKT)
            S.bump(v, "a1ce", 1)
        head_end.append(S.tok("a1ce"))
    b.c1w_tot = [S.tok("a1cw0"), S.tok("a1cw1")]


def _final(b):
    S, c = b.S, b.cfg
    D, NT, T = c["D"], c["NT"], c["T"]
    KD, NTT = c["KD"], c["NTT"]

    cu = b.CUR("RA")
    x1t = [cu(F32, 128, D) for j in range(2)]
    y1t = [cu(F32, 128, D) for j in range(2)]
    ctm = [cu(BF16, 128, D) for j in range(2)]
    ctf = cu(F32, 128, D)
    ot = [cu(F32, 128, D) for j in range(2)]
    assert cu.pos[0] <= b.reg["RA"]["size"], cu.pos[0]

    @b.block.gpsimd
    def _(g):
        S.wait(g, "cc", b.agc1cc)
        for side in b.sswitch(g, "g"):
            co = side * NT
            for kc in range(KD):
                ap = bass.AP(b.ag1c_out.tensor, co + kc * 128 * T,
                             [[T, 128], [1, NT]])
                ii = g.dma_start(out=b.ctx1m_dr[kc * 128:(kc + 1) * 128, :],
                                 in_=ap)
                S.inc("fcm", ii, 16)
    fcm_tot = S.tok("fcm")

    @b.block.sync
    def _(y):
        for jj in range(2):
            S.wait(y, f"f1ys{jj}", b.f1ys_tot[jj])
        S.wait(y, "fcm", fcm_tot)
        for tt in range(NTT):
            j = tt % 2
            if tt >= 2:
                S.wait(y, "fv", 2 * (tt - 1))
            ii = y.dma_start(out=x1t[j][:], in_=b.x1_sp[tt * 128:(tt + 1) * 128, :])
            S.inc(f"fl{j}", ii, 16)
            ii = y.dma_start(out=y1t[j][:], in_=b.y1_sp[tt * 128:(tt + 1) * 128, :])
            S.inc(f"fl{j}", ii, 16)
            for k in range(KD):
                ii = y.dma_start_transpose(ctm[j][:, k * 128:(k + 1) * 128],
                                           b.ctx1m_dr[k * 128:(k + 1) * 128,
                                                      tt * 128:(tt + 1) * 128])
                S.inc(f"fl{j}", ii, 16)

    fv0 = S.tok("fv")

    @b.block.vector
    def _(v):
        for tt in range(NTT):
            j = tt % 2
            S.wait(v, f"fl{j}", (tt // 2 + 1) * 16 * (2 + KD))
            if tt >= 2:
                S.wait(v, f"fo{j}", (tt // 2) * 16)
            st = _chain(S, v)
            st(v.tensor_copy(out=ctf[:], in_=ctm[j][:]))
            st(v.tensor_tensor(out=ot[j][:], in0=x1t[j][:], in1=y1t[j][:],
                               op=ALU.add))
            st(v.tensor_tensor(out=ot[j][:], in0=ot[j][:], in1=ctf[:], op=ALU.add))
            S.bump(v, "fv", 2)

    @b.block.scalar
    def _(sc):
        for tt in range(NTT):
            j = tt % 2
            S.wait(sc, "fv", fv0 + 2 * (tt + 1))
            ii = sc.dma_start(out=b.out[tt * 128:(tt + 1) * 128, :], in_=ot[j][:])
            S.inc(f"fo{j}", ii, 16)
        S.wait(sc, "fo0", None)
        S.wait(sc, "fo1", None)



def _finish(b):
    S, c = b.S, b.cfg

    outw = S.m.get("outw")

    @b.block.sync
    def _(y):
        for h, v in S.phys_state():
            if outw is not None and h is outw[0][0]:
                continue
            y.wait_ge(h, v)

    if b.upto != "all":
        z_ = b.sb("zot", [128, c["D"]], F32)

        @b.block.vector
        def _(v):
            ii = v.memset(z_[:], 0.0)
            b.t_z = S.inc("v", ii, 1)

        @b.block.sync
        def _(y):
            S.wait(y, "v", b.t_z)
            for tt in range(c["NTT"]):
                ii = y.dma_start(out=b.out[tt * 128:(tt + 1) * 128, :], in_=z_[:])
                S.inc("outw", ii, 16)
            S.wait(y, "outw", None)


def make_consts(cfg):
    T, TS = cfg["T"], cfg["TS"]
    TRIV = TS // 128
    IOTA = np.tile(np.arange(512, dtype=np.float32), (64, 1))
    ID128 = np.eye(128, dtype=np.float32)

    tri = np.zeros((TRIV * 128, TS), np.float32)
    for v in range(TRIV):
        for kj in range(128):
            tri[v * 128 + kj, :] = (np.arange(TS) >= (128 * v + kj)).astype(np.float32)
    TRI = tri.astype(ml_dtypes.bfloat16)
    R8 = np.zeros((8, 64), np.float32)
    for e in range(8):
        R8[e, e * 8:(e + 1) * 8] = 1.0
    return dict(IOTA=IOTA, ID128=ID128, TRI=TRI,
                R8=R8.astype(ml_dtypes.bfloat16))


def make_inmaps(cfg, A):
    c = cfg
    NT, D = c["NT"], c["D"]
    NHm, NH1m, HD = c["NHm"], c["NH1m"], c["HD"]
    consts = make_consts(c)
    base = dict(
        wk=A["L0_attn_wk"], bk=A["L0_attn_bk"], wq=A["L0_attn_wq"], bq=A["L0_attn_bq"],
        wv=A["L0_attn_wv"], bv=A["L0_attn_bv"], wkl=A["L0_attn_wkl"], bkl=A["L0_attn_bkl"],
        wql=A["L0_attn_wql"], bql=A["L0_attn_bql"], wvl=A["L0_attn_wvl"], bvl=A["L0_attn_bvl"],
        wo=A["L0_attn_wo"], bo=A["L0_attn_bo"],
        w1=A["L0_ffn_w1"], b1=A["L0_ffn_b1"], w2=A["L0_ffn_w2"], b2=A["L0_ffn_b2"],
        w2T=np.ascontiguousarray(np.transpose(A["L0_ffn_w2"])),
        Wq1=A["L1_attn_Wq"], bq1=A["L1_attn_bq"], Wk1=A["L1_attn_Wk"], bk1=A["L1_attn_bk"],
        Wv1=A["L1_attn_Wv"], bv1=A["L1_attn_bv"],
        WbT=np.ascontiguousarray(np.transpose(A["L1_ffn_W_base"])),
        bbase=A["L1_ffn_b_base"],
        WrT=np.ascontiguousarray(np.transpose(A["L1_ffn_W_router"])),
        AT=np.ascontiguousarray(np.asarray(A["L1_ffn_A_w"]).reshape(64, D).T),
        Bcat=np.ascontiguousarray(np.asarray(A["L1_ffn_B_w"]).transpose(0, 2, 1).reshape(64, D)),
        **consts,
    )
    base = {k: np.ascontiguousarray(v) for k, v in base.items()}
    x = A["L0_attn_x"]
    maps = []
    for core in range(c["NCORE"]):
        bidx, s = core // 2, core % 2
        m = dict(base)
        m["x0"] = np.ascontiguousarray(x[bidx, s * NT:(s + 1) * NT, :])
        m["soff"] = np.array([[s, s * NHm * HD * NT, s * NH1m * 128 * NT,
                               s * NT, 0, 0, 0, 0]], np.int32)
        maps.append(m)
    return maps


def kernel(**inputs):
    cfg = make_cfg(small=False)
    nc = build_core(cfg, debug=0, upto="all")
    in_maps = make_inmaps(cfg, inputs)
    res = run_bass_kernel_spmd(nc, in_maps, core_ids=list(range(cfg["NCORE"])))
    NT, D, B = cfg["NT"], cfg["D"], cfg["B"]
    out = np.empty((B, 2 * NT, D), np.float32)
    for core in range(cfg["NCORE"]):
        bidx, s = core // 2, core % 2
        out[bidx, s * NT:(s + 1) * NT, :] = res.results[core]["out"]
    return out
